# version 11
# baseline (speedup 1.0000x reference)
import contextlib

import ml_dtypes
import numpy as np

import concourse.bass as bass
import concourse.mybir as mybir
from concourse.bass_utils import run_bass_kernel_spmd

F32 = mybir.dt.float32
BF16 = mybir.dt.bfloat16
FP8 = mybir.dt.float8e4
AF = mybir.ActivationFunctionType
ALU = mybir.AluOpType
AXX = mybir.AxisListType.X

N = 4096
C = 1024
HID = 2048
T = 512
NT = N // T
KC = C // 128
HC = HID // 128
NB = T // 128
SQC = 32.000001


class Marks(dict):

    def __init__(self, prev=None):
        super().__init__()
        self.prev = prev

    def __getitem__(self, k):
        if dict.__contains__(self, k):
            return dict.__getitem__(self, k)
        if self.prev is not None:
            return self.prev.get(k)
        return None


class Coll:

    ORDER = ("sync", "gpsimd", "tensor", "scalar", "vector")

    def __init__(self, real):
        self.real = real
        self.q = {k: [] for k in self.ORDER}

    def _add(self, name, f):
        self.q[name].append(f)

    def sync(self, f):
        self._add("sync", f)

    def gpsimd(self, f):
        self._add("gpsimd", f)

    def tensor(self, f):
        self._add("tensor", f)

    def scalar(self, f):
        self._add("scalar", f)

    def vector(self, f):
        self._add("vector", f)

    def flush(self):
        for k in self.ORDER:
            fs, self.q[k] = self.q[k], []
            if fs:
                getattr(self.real, k)(lambda e, fs=fs: [f(e) for f in fs])


class Ctr:
    def __init__(self, sems):
        self.sems = sems
        self.cnt = {k: 0 for k in sems}
        self.water = {}

    def inc(self, ins, sem, k=1):
        self.cnt[sem] += k
        ins.then_inc(self.sems[sem], k)
        return self.cnt[sem]

    def w(self, eng, engname, sem, val):
        if val is None or val <= 0:
            return
        key = (engname, sem)
        if self.water.get(key, 0) >= val:
            return
        eng.wait_ge(self.sems[sem], val)
        self.water[key] = val


def build_nc(stage=3, taps=False, prev=None):
    nc = bass.Bass("TRN2", target_bir_lowering=False, detect_race_conditions=False)

    xT = nc.dram_tensor("xT", [C, N], F32, kind="ExternalInput").ap()
    qkvwT = nc.dram_tensor("qkvwT", [C, 3 * C], F32, kind="ExternalInput").ap()
    projwT = nc.dram_tensor("projwT", [C, C], F32, kind="ExternalInput").ap()
    conv1wT = nc.dram_tensor("conv1wT", [C, HID], F32, kind="ExternalInput").ap()
    conv2wT = nc.dram_tensor("conv2wT", [HID, C], F32, kind="ExternalInput").ap()
    ffnwT = nc.dram_tensor("ffnwT", [C, C], F32, kind="ExternalInput").ap()
    scale1_d = nc.dram_tensor("scale1", [128, KC], F32, kind="ExternalInput").ap()
    scale2_d = nc.dram_tensor("scale2", [128, KC], F32, kind="ExternalInput").ap()
    gamma_d = nc.dram_tensor("gamma", [128, KC], F32, kind="ExternalInput").ap()
    projb_d = nc.dram_tensor("projb", [C], F32, kind="ExternalInput").ap()
    conv1b_d = nc.dram_tensor("conv1b", [128, HC], F32, kind="ExternalInput").ap()
    conv2b_d = nc.dram_tensor("conv2b", [128, KC], F32, kind="ExternalInput").ap()
    ffnb_d = nc.dram_tensor("ffnb", [128, KC], F32, kind="ExternalInput").ap()

    yT = nc.dram_tensor("yT", [C, N], F32, kind="ExternalOutput").ap()

    spk = "ExternalOutput" if taps else "Internal"
    phisp = nc.dram_tensor("phisp", [C, N], BF16, kind=spk).ap()
    x1sp = nc.dram_tensor("x1sp", [C, N], F32, kind=spk).ap()
    ln2sp = nc.dram_tensor("ln2sp", [C, N], F32, kind=spk).ap()
    ksum_d = nc.dram_tensor("ksum_d", [C], F32, kind=spk).ap()
    rowsD = nc.dram_tensor("rowsD", [2, N], BF16, kind="Internal").ap()
    wam_d = nc.dram_tensor("wam_d", [128], F32, kind="Internal").ap()
    ccin = nc.dram_tensor("ccin", [1, 128], F32, kind="Internal")
    ccout = nc.dram_tensor("ccout", [1, 128], F32, kind="Internal")
    if taps:
        tap_sc = nc.dram_tensor("tap_sc", [1, 8], F32, kind="ExternalOutput").ap()
        tap_kv = nc.dram_tensor("tap_kv", [128, 512], F32, kind="ExternalOutput").ap()

    ones_col = nc.inline_tensor(np.ones((128, 1), dtype=ml_dtypes.bfloat16), "c_onescol").ap()
    onesrow_f = nc.inline_tensor(np.ones((1, 128), dtype=np.float32), "c_onesrow").ap()
    onesrow_b = nc.inline_tensor(np.ones((1, 128), dtype=ml_dtypes.bfloat16), "c_onesrowb").ap()
    ones_colf = nc.inline_tensor(np.ones((128, 1), dtype=np.float32), "c_onescolf").ap()
    ones512 = nc.inline_tensor(np.ones((1, T), dtype=ml_dtypes.bfloat16), "c_ones512").ap()
    ind = np.zeros((16, 8 * 128), dtype=np.float32)
    for p in range(8):
        ind[2 * p, 128 * p:128 * p + 64] = 1.0
        ind[2 * p + 1, 128 * p + 64:128 * p + 128] = 1.0
    ind16_d = nc.inline_tensor(ind.astype(ml_dtypes.bfloat16), "c_ind16").ap()

    with contextlib.ExitStack() as st:
        sb = lambda n_, s_, d_: st.enter_context(nc.sbuf_tensor(n_, s_, d_))
        sems = {k: st.enter_context(nc.semaphore(k)) for k in
                ["S", "G", "R", "GN", "PE", "V", "A", "CC", "W", "SP"]}
        ct = Ctr(sems)
        blk_real = st.enter_context(nc.Block())
        block = Coll(blk_real)

        def ndma(eng, engname, sem, out, in_, **kw):
            ct.w(eng, engname, sem, ct.cnt[sem])
            ins = eng.dma_start(out=out, in_=in_, **kw)
            return ct.inc(ins, sem, 16)

        xbuf = sb("xbuf", [128, 2 * 4096], F32)
        phq = sb("phq", [128, 4096], BF16)
        xn_bf = sb("xn_bf", [128, 4096], BF16)
        stAB = sb("stAB", [128, 2 * T], BF16)
        projw = sb("projw", [128, KC * C], BF16)
        wq8 = sb("wq8", [128, KC * C], FP8)
        kv_bf = sb("kv_bf", [128, 512], BF16)
        ksblk = sb("ksblk", [128, 128], BF16)
        mtile = sb("mtile", [1, T], BF16)
        gptile = sb("gptile", [1, T], BF16)
        rowb = sb("rowb", [1, T], BF16)
        rowt0 = sb("rowt0", [1, T], F32)
        rowt1 = sb("rowt1", [1, T], F32)
        bc0 = sb("bc0", [128, T], F32)
        bc1 = sb("bc1", [128, T], F32)
        tsub = sb("tsub", [128, T], F32)
        onc = sb("onc", [128, 1], BF16)
        oncf = sb("oncf", [128, 1], F32)
        onr = sb("onr", [1, 128], F32)
        onrb = sb("onrb", [1, 128], BF16)
        onr5 = sb("onr5", [1, T], BF16)
        ind16 = sb("ind16", [16, 8 * 128], BF16)
        s1sb = sb("s1sb", [128, KC], F32)
        s2sb = sb("s2sb", [128, KC], F32)
        gmsb = sb("gmsb", [128, KC], F32)
        c1bsb = sb("c1bsb", [128, HC], F32)
        fbsb = sb("fbsb", [128, KC], F32)
        prow = sb("prow", [1, C], BF16)
        c2bsb = sb("c2bsb", [128, KC], F32)
        gc2b = sb("gc2b", [128, KC], F32)
        scr = sb("scr", [1, 8], F32)
        screp = sb("screp", [128, 2], F32)
        swhrep = sb("swhrep", [128, 1], F32)
        wamax = sb("wamax", [128, 1], F32)
        wgtmp = sb("wgtmp", [128, 1], F32)
        amax_acc = sb("amax_acc", [128, 1], F32)
        ggtmp2 = sb("ggtmp2", [128, 1], F32)
        crow = sb("crow", [1, 128], F32)

        xk = lambda par, k: xbuf[:, par * 4096 + T * k: par * 4096 + T * (k + 1)]
        phqk = lambda j: phq[:, T * j: T * (j + 1)]
        xnk = lambda k: xn_bf[:, T * k: T * (k + 1)]
        stAs = lambda s: stAB[:, 0:T]
        stBs = lambda s: stAB[:, T:2 * T]

        def dram_tile(dram_ap, t):
            return dram_ap.rearrange("(k p) n -> p k n", p=128)[:, :, T * t: T * (t + 1)]

        pv = prev or {}
        V = Marks(pv.get("V"))
        A = Marks(pv.get("A"))
        P = Marks(pv.get("P"))
        G = Marks(pv.get("G"))
        GC = Marks(pv.get("GC"))
        S = Marks(pv.get("S"))
        marks = {"V": V, "A": A, "P": P, "G": G, "GC": GC, "S": S}
        nc._marks_out = marks

        def _(sync):
            ct.inc(sync.dma_start(out=onc[:, :], in_=ones_col[:, :]), "S", 16)
            ct.inc(sync.dma_start(out=oncf[:, :], in_=ones_colf[:, :]), "S", 16)
            ndma(sync, "s", "R", onr[:, :], onesrow_f[:, :])
            S["onr"] = ndma(sync, "s", "R", onrb[:, :], onesrow_b[:, :])
            ct.inc(sync.dma_start(out=s1sb[:, :], in_=scale1_d[:, :]), "S", 16)
            ct.inc(sync.dma_start(out=s2sb[:, :], in_=scale2_d[:, :]), "S", 16)
            ct.inc(sync.dma_start(out=gmsb[:, :], in_=gamma_d[:, :]), "S", 16)
            ct.inc(sync.dma_start(out=c1bsb[:, :], in_=conv1b_d[:, :]), "S", 16)
            ins = sync.dma_start(out=fbsb[:, :], in_=ffnb_d[:, :])
            S["consts"] = ct.inc(ins, "S", 16)
            ins = sync.dma_start(out=xbuf[:, 0:4096].rearrange("p (k n) -> p k n", n=T),
                                 in_=dram_tile(xT, 0))
            S[("x1p", 0)] = ct.inc(ins, "S", 16)
        block.sync(_)

        def _(g):
            ndma(g, "g", "GN", ind16[:, :], ind16_d[:, :])
            for k in range(KC):
                ct.inc(g.dma_start(out=projw[:, C * k: C * (k + 1)],
                                   in_=projwT[128 * k:128 * (k + 1), :]), "G", 16)
            G["pw"] = ct.cnt["G"]
            ndma(g, "g", "GN", prow[:, :], projb_d.rearrange("(o n) -> o n", o=1))
            G["rows_c"] = ndma(g, "g", "GN", onr5[:, :], ones512[:, :])
        block.gpsimd(_)

        def _(sync):
            ins = sync.dma_start(out=c2bsb[:, :], in_=conv2b_d[:, :])
            S["c2b"] = ct.inc(ins, "S", 16)
        block.sync(_)

        def _(v):
            ct.w(v, "v", "S", S["c2b"])
            ins = v.tensor_tensor(out=gc2b[:, :], in0=gmsb[:, :], in1=c2bsb[:, :],
                                  op=ALU.mult)
            V["gc2b"] = ct.inc(ins, "V", 1)
        block.vector(_)
        block.flush()

        with contextlib.ExitStack() as p1:
            sb1 = lambda n_, s_, d_: p1.enter_context(nc.sbuf_tensor(n_, s_, d_))
            ps1 = lambda n_, s_: p1.enter_context(nc.psum_tensor(n_, s_, F32))

            qkvw = sb1("qkvw", [128, KC * 3 * C], BF16)
            xn1b = sb1("xn1b", [128, 4096], BF16)
            wstage = sb1("wstage", [128, 4 * C], F32)
            vtile = sb1("vtile", [1, T], F32)
            ksrow = sb1("ksrow", [1, C], F32)
            rowt2 = sb1("rowt2", [1, T], F32)
            wrow = sb1("wrow", [1, 128], F32)
            st1 = sb1("st1", [128, 4 * T], BF16)
            st1A = lambda s: st1[:, 2 * s * T: (2 * s + 1) * T]
            st1B = lambda s: st1[:, (2 * s + 1) * T: (2 * s + 2) * T]
            ktmp = sb1("ktmp", [128, 1024], F32)
            phik = sb1("phik", [128, 1024], BF16)
            v_bf = sb1("v_bf", [128, 1024], BF16)
            phitmp = sb1("phitmp", [128, T], F32)

            kvacc = ps1("kvacc", [128, 512])
            ksum_ps = ps1("ksum_ps", [1, 1024])
            kvtok = ps1("kvtok", [128, 1024])
            qps = ps1("qps", [128, 1024])
            qpsl = lambda j: qps[:, T * (j % 2): T * (j % 2 + 1)]
            bps = kvtok[:, 0:T]
            srow = kvtok[0:1, :]
            xn1 = lambda par, k: (xn_bf if par == 0 else xn1b)[:, T * k: T * (k + 1)]

            def _(g):
                for k in range(KC):
                    ins = g.dma_start(out=qkvw[:, 3 * C * k: 3 * C * (k + 1)],
                                      in_=qkvwT[128 * k:128 * (k + 1), :])
                    ct.inc(ins, "G", 16)
                G["qkv"] = ct.cnt["G"]
            block.gpsimd(_)

            for half in range(2):
                def _(sync, half=half):
                    if half == 1:
                        ct.w(sync, "s", "V", V[("wmaxh", 0)])
                    ins = sync.dma_start(
                        out=wstage[:, :].rearrange("p (k n) -> p k n", n=C),
                        in_=ffnwT.rearrange("(k p) n -> p k n", p=128)[:, 4 * half:4 * half + 4, :])
                    S[("ffnw", half)] = ct.inc(ins, "W", 16)
                block.sync(_)

                def _(v, half=half):
                    ct.w(v, "v", "W", S[("ffnw", half)])
                    for k in range(4):
                        ct.w(v, "v", "V",
                             ct.inc(v.reduce_max(wgtmp[:, 0:1],
                                                 wstage[:, C * k: C * (k + 1)],
                                                 axis=AXX, apply_absolute_value=True),
                                    "V", 1))
                        if half == 0 and k == 0:
                            ins = v.tensor_copy(out=wamax[:, 0:1], in_=wgtmp[:, 0:1])
                        else:
                            ins = v.tensor_tensor(out=wamax[:, 0:1], in0=wamax[:, 0:1],
                                                  in1=wgtmp[:, 0:1], op=ALU.max)
                        ct.w(v, "v", "V", ct.inc(ins, "V", 1))
                    V[("wmaxh", half)] = ct.cnt["V"]
                block.vector(_)

            def _(sync):
                ct.w(sync, "s", "V", V[("wmaxh", 1)])
                ndma(sync, "s", "R", wam_d.rearrange("(p o) -> p o", o=1), wamax[:, 0:1])
                S["wrow"] = ndma(sync, "s", "R", wrow[:, :],
                                 wam_d.rearrange("(o p) -> o p", o=1))
            block.sync(_)

            def _(v):
                ct.w(v, "v", "R", S["wrow"])
                def step(f):
                    ct.w(v, "v", "V", ct.inc(f(), "V", 1))
                step(lambda: v.reduce_max(scr[0:1, 0:1], wrow[:, :], axis=AXX))
                step(lambda: v.tensor_scalar_max(scr[0:1, 0:1], scr[0:1, 0:1], 1e-12))
                step(lambda: v.reciprocal(scr[0:1, 1:2], scr[0:1, 0:1]))
                step(lambda: v.tensor_scalar(out=scr[0:1, 2:3], in0=scr[0:1, 1:2],
                                             scalar1=448.0, scalar2=448.0,
                                             op0=ALU.mult, op1=ALU.min))
                step(lambda: v.tensor_scalar_mul(scr[0:1, 3:4], scr[0:1, 2:3], 0.5))
                step(lambda: v.reciprocal(scr[0:1, 4:5], scr[0:1, 2:3]))
                V["wsc"] = ct.cnt["V"]
            block.vector(_)

            def _(tn):
                ct.w(tn, "t", "V", V["wsc"])
                ct.w(tn, "t", "S", S["consts"])
                ct.w(tn, "t", "R", S["onr"])
                ins = tn.matmul(kvacc[:, 0:1], onr[0:1, 0:128], scr[0:1, 3:4],
                                start=True, stop=True)
                P["swhrep"] = ct.inc(ins, "PE", 1)
            block.tensor(_)

            def _(a):
                ct.w(a, "a", "PE", P["swhrep"])
                ins = a.copy(swhrep[:, 0:1], kvacc[:, 0:1])
                A["swhrep"] = ct.inc(ins, "A", 1)
            block.scalar(_)

            for half in range(2):
                def _(sync, half=half):
                    ct.w(sync, "s", "V", V[("wmaxh", 1)] if half == 0 else V[("wq8h", 0)])
                    ins = sync.dma_start(
                        out=wstage[:, :].rearrange("p (k n) -> p k n", n=C),
                        in_=ffnwT.rearrange("(k p) n -> p k n", p=128)[:, 4 * half:4 * half + 4, :])
                    S[("ffnw2", half)] = ct.inc(ins, "W", 16)
                block.sync(_)

                def _(v, half=half):
                    ct.w(v, "v", "W", S[("ffnw2", half)])
                    ct.w(v, "v", "A", A["swhrep"])
                    for k in range(4):
                        ins = v.tensor_scalar(out=wq8[:, C * (4 * half + k): C * (4 * half + k + 1)],
                                              in0=wstage[:, C * k: C * (k + 1)],
                                              scalar1=swhrep[:, 0:1], scalar2=None,
                                              op0=ALU.mult)
                    V[("wq8h", half)] = ct.inc(ins, "V", 1)
                block.vector(_)
            block.flush()

            def _spill1(sync, t):
                ct.w(sync, "s", "A", A[("phi", t, KC - 1)])
                ins = sync.dma_start(
                    out=dram_tile(phisp, t),
                    in_=phq[:, :].rearrange("p (j n) -> p j n", n=T))
                S[("phisp", t)] = ct.inc(ins, "SP", 16)

            for t in range(NT):
                par = t % 2
                if t > 0:
                    def _(sync, t=t, par=par):
                        if t >= 2:
                            ct.w(sync, "s", "V", V[("xn", t - 2)])
                        ins = sync.dma_start(
                            out=xbuf[:, par * 4096:(par + 1) * 4096].rearrange(
                                "p (k n) -> p k n", n=T),
                            in_=dram_tile(xT, t))
                        S[("x1p", t)] = ct.inc(ins, "S", 16)
                        _spill1(sync, t - 1)
                    block.sync(_)

                for k in range(KC):
                    s = k % 2
                    def _(v, t=t, par=par, k=k, s=s):
                        ct.w(v, "v", "S", S[("x1p", t)])
                        if k >= 2:
                            ct.w(v, "v", "PE", P[("st", t, k - 2)])
                        elif t > 0:
                            ct.w(v, "v", "PE", P[("st", t - 1, 6 + k)])
                        v.tensor_copy(out=st1A(s), in_=xk(par, k))
                        ins = v.tensor_tensor(out=st1B(s), in0=st1A(s), in1=st1A(s),
                                              op=ALU.mult)
                        V[("st", t, k)] = ct.inc(ins, "V", 1)
                    block.vector(_)

                    def _(tn, t=t, k=k, s=s):
                        ct.w(tn, "t", "V", V[("st", t, k)])
                        if k == 0:
                            ct.w(tn, "t", "S", S["consts"])
                            if t > 0:
                                ct.w(tn, "t", "A", A[("vbf", t - 1, NB - 1)])
                                ct.w(tn, "t", "V", V[("rows_a", t - 1)])
                        tn.matmul(srow[:, 0:T], onc[:, 0:1], st1B(s),
                                  start=(k == 0), stop=(k == KC - 1))
                        ins = tn.matmul(srow[:, T:2 * T], onc[:, 0:1], st1A(s),
                                        start=(k == 0), stop=(k == KC - 1))
                        P[("st", t, k)] = ct.inc(ins, "PE", 1)
                    block.tensor(_)

                def _(v, t=t):
                    ct.w(v, "v", "PE", P[("st", t, KC - 1)])
                    if t > 0:
                        ct.w(v, "v", "R", S[("rowsd", t - 1)])
                        ct.w(v, "v", "A", A[("gp1", t - 1)])
                    v.tensor_scalar_mul(mtile[0:1, :], srow[0:1, T:2 * T], 1.0 / C)
                    v.tensor_tensor(out=rowt0[0:1, :], in0=mtile[0:1, :],
                                    in1=mtile[0:1, :], op=ALU.mult)
                    v.scalar_tensor_tensor(
                        out=vtile[0:1, :], in0=srow[0:1, 0:T],
                        scalar=1.0 / C, in1=rowt0[0:1, :], op0=ALU.mult, op1=ALU.subtract)
                    ins = v.tensor_scalar_add(rowt0[0:1, :], vtile[0:1, :], 1e-5)
                    V[("rows_a", t)] = ct.inc(ins, "V", 1)
                block.vector(_)

                def _(a, t=t):
                    ct.w(a, "a", "V", V[("rows_a", t)])
                    if t > 0:
                        ct.w(a, "a", "R", S[("rowsd", t - 1)])
                    a.activation(out=rowt1[0:1, :], in_=vtile[0:1, :],
                                 func=AF.Ln, scale=float(C))
                    ins = a.activation(out=rowt2[0:1, :], in_=rowt1[0:1, :],
                                       func=AF.Exp, scale=-0.5)
                    A[("gx0", t)] = ct.inc(ins, "A", 1)
                    a.activation(out=rowt1[0:1, :], in_=rowt0[0:1, :], func=AF.Ln)
                    ins = a.activation(out=gptile[0:1, :], in_=rowt1[0:1, :],
                                       func=AF.Exp, scale=-0.5)
                    A[("gp1", t)] = ct.inc(ins, "A", 1)
                block.scalar(_)

                def _(sync, t=t):
                    ct.w(sync, "s", "A", A[("gp1", t)])
                    ndma(sync, "s", "R", rowsD[0:1, T * t: T * (t + 1)], mtile[0:1, :])
                    S[("rowsd", t)] = ndma(sync, "s", "R",
                                           rowsD[1:2, T * t: T * (t + 1)], gptile[0:1, :])
                block.sync(_)

                def _(v, t=t):
                    ct.w(v, "v", "A", A[("gx0", t)])
                    if t > 0:
                        ct.w(v, "v", "PE", P[("gx", t - 1)])
                    ins = v.tensor_scalar_mul(rowb[0:1, :], rowt2[0:1, :], SQC)
                    V[("rows_b", t)] = ct.inc(ins, "V", 1)
                block.vector(_)

                def _(tn, t=t):
                    ct.w(tn, "t", "V", V[("rows_a", t)])
                    ct.w(tn, "t", "R", S["onr"])
                    ins = tn.matmul(bps, onrb[0:1, 0:128], mtile[0:1, :],
                                    start=True, stop=True)
                    P[("mb", t)] = ct.inc(ins, "PE", 1)
                block.tensor(_)

                def _(a, t=t):
                    ct.w(a, "a", "PE", P[("mb", t)])
                    ins = a.copy(bc0[:, :], bps)
                    A[("mb", t)] = ct.inc(ins, "A", 1)
                block.scalar(_)

                def _(tn, t=t):
                    ct.w(tn, "t", "V", V[("rows_b", t)])
                    ct.w(tn, "t", "A", A[("mb", t)])
                    ins = tn.matmul(bps, onrb[0:1, 0:128], rowb[0:1, :],
                                    start=True, stop=True)
                    P[("gx", t)] = ct.inc(ins, "PE", 1)
                block.tensor(_)

                def _(a, t=t):
                    ct.w(a, "a", "PE", P[("gx", t)])
                    ins = a.copy(bc1[:, :], bps)
                    A[("gx", t)] = ct.inc(ins, "A", 1)
                block.scalar(_)

                def _(v, t=t, par=par):
                    ct.w(v, "v", "A", A[("gx", t)])
                    if t >= 2:
                        ct.w(v, "v", "PE", P[("q", t - 2, KC - 1)])
                    for k in range(KC):
                        v.tensor_sub(tsub[:, :], xk(par, k), bc0[:, :])
                        ins = v.scalar_tensor_tensor(out=xn1(par, k), in0=tsub[:, :],
                                                     scalar=s1sb[:, k:k + 1], in1=bc1[:, :],
                                                     op0=ALU.mult, op1=ALU.mult)
                    V[("xn", t)] = ct.inc(ins, "V", 1)
                block.vector(_)

                for b in range(NB):
                    def _(tn, t=t, b=b, par=par):
                        ct.w(tn, "t", "V", V[("xn", t)])
                        if b == 0:
                            ct.w(tn, "t", "A", A[("gx", t)])
                        else:
                            ct.w(tn, "t", "A", A[("vbf", t, b - 1)])
                        for k in range(KC):
                            lhs = xn1(par, 0)[0:0, 0:0]
                            lhs = (xn_bf if par == 0 else xn1b)[:, T * k + 128 * b: T * k + 128 * (b + 1)]
                            tn.matmul(kvtok[:, 0:512], lhs,
                                      qkvw[:, 3 * C * k + C: 3 * C * k + C + 512],
                                      start=(k == 0), stop=(k == KC - 1))
                            ins = tn.matmul(kvtok[:, 512:1024], lhs,
                                            qkvw[:, 3 * C * k + C + 512: 3 * C * k + 2 * C],
                                            start=(k == 0), stop=(k == KC - 1))
                        P[("ktok", t, b)] = ct.inc(ins, "PE", 1)
                    block.tensor(_)

                    def _(a, t=t, b=b):
                        ct.w(a, "a", "PE", P[("ktok", t, b)])
                        ins = a.activation(out=ktmp[:, :], in_=kvtok[:, :], func=AF.Square)
                        A[("ksq", t, b)] = ct.inc(ins, "A", 1)
                        if (t, b) != (0, 0):
                            pt, pb = (t, b - 1) if b > 0 else (t - 1, NB - 1)
                            ct.w(a, "a", "PE", P[("kv", pt, pb)])
                        ins = a.activation(out=phik[:, :], in_=ktmp[:, :],
                                           func=AF.Exp, scale=-0.5)
                        A[("phik", t, b)] = ct.inc(ins, "A", 1)
                    block.scalar(_)

                    def _(tn, t=t, b=b, par=par):
                        ct.w(tn, "t", "A", A[("ksq", t, b)])
                        for k in range(KC):
                            lhs = (xn_bf if par == 0 else xn1b)[:, T * k + 128 * b: T * k + 128 * (b + 1)]
                            tn.matmul(kvtok[:, 0:512], lhs,
                                      qkvw[:, 3 * C * k + 2 * C: 3 * C * k + 2 * C + 512],
                                      start=(k == 0), stop=(k == KC - 1))
                            ins = tn.matmul(kvtok[:, 512:1024], lhs,
                                            qkvw[:, 3 * C * k + 2 * C + 512: 3 * C * (k + 1)],
                                            start=(k == 0), stop=(k == KC - 1))
                        P[("vtok", t, b)] = ct.inc(ins, "PE", 1)
                    block.tensor(_)

                    def _(a, t=t, b=b):
                        ct.w(a, "a", "PE", P[("vtok", t, b)])
                        ins = a.copy(v_bf[:, :], kvtok[:, :])
                        A[("vbf", t, b)] = ct.inc(ins, "A", 1)
                    block.scalar(_)

                    def _(tn, t=t, b=b):
                        ct.w(tn, "t", "A", A[("phik", t, b)])
                        ct.w(tn, "t", "A", A[("vbf", t, b)])
                        first = (t == 0 and b == 0)
                        last = (t == NT - 1 and b == NB - 1)
                        for p in range(8):
                            h0, h1 = 2 * p, 2 * p + 1
                            tn.matmul(kvacc[0:64, 64 * p:64 * (p + 1)],
                                      phik[:, 64 * h0:64 * (h0 + 1)],
                                      v_bf[:, 64 * h0:64 * (h0 + 1)],
                                      start=first, stop=last, tile_position=(0, 0))
                            tn.matmul(kvacc[64:128, 64 * p:64 * (p + 1)],
                                      phik[:, 64 * h1:64 * (h1 + 1)],
                                      v_bf[:, 64 * h1:64 * (h1 + 1)],
                                      start=first, stop=last, tile_position=(0, 64))
                        tn.matmul(ksum_ps[0:1, 0:512], onc[:, 0:1], phik[:, 0:512],
                                  start=first, stop=last)
                        ins = tn.matmul(ksum_ps[0:1, 512:1024], onc[:, 0:1],
                                        phik[:, 512:1024], start=first, stop=last)
                        P[("kv", t, b)] = ct.inc(ins, "PE", 1)
                    block.tensor(_)

                for j in range(KC):
                    def _(tn, t=t, j=j, par=par):
                        ct.w(tn, "t", "V", V[("xn", t)])
                        ct.w(tn, "t", "G", G["qkv"])
                        if j < 2:
                            if t > 0:
                                ct.w(tn, "t", "A", A[("sqr", t - 1, 6 + j)])
                        else:
                            ct.w(tn, "t", "A", A[("sqr", t, j - 2)])
                        for k in range(KC):
                            ins = tn.matmul(
                                qpsl(j),
                                qkvw[:, 3 * C * k + 128 * j: 3 * C * k + 128 * (j + 1)],
                                xn1(par, k), start=(k == 0), stop=(k == KC - 1))
                        P[("q", t, j)] = ct.inc(ins, "PE", 1)
                    block.tensor(_)

                    def _(a, t=t, j=j, par=par):
                        ct.w(a, "a", "PE", P[("q", t, j)])
                        ins = a.activation(out=phitmp[:, :], in_=qpsl(j), func=AF.Square)
                        A[("sqr", t, j)] = ct.inc(ins, "A", 1)
                        if t >= 1 and j == 0:
                            ct.w(a, "a", "SP", S[("phisp", t - 1)])
                        ins = a.activation(out=phqk(j), in_=phitmp[:, :],
                                           func=AF.Exp, scale=-0.5)
                        A[("phi", t, j)] = ct.inc(ins, "A", 1)
                    block.scalar(_)

                block.flush()

            def _(v):
                ct.w(v, "v", "PE", P[("kv", NT - 1, NB - 1)])
                v.tensor_copy(out=kv_bf[:, :], in_=kvacc[:, :])
                v.tensor_copy(out=ksrow[:, :], in_=ksum_ps[0:1, :])
                ins = v.memset(ksblk[:, :], 0.0)
                V["kvevac"] = ct.inc(ins, "V", 1)
            block.vector(_)

            def _(sync):
                _spill1(sync, NT - 1)
                ct.w(sync, "s", "V", V["kvevac"])
                S["ksum_d"] = ndma(sync, "s", "R", ksum_d.rearrange("(o n) -> o n", o=1),
                                   ksrow[:, :])
            block.sync(_)

            def _(g):
                ct.w(g, "g", "R", S["ksum_d"])
                ct.w(g, "g", "V", V["kvevac"])
                ndma(g, "g", "GN", ksblk[0:64, 0:127:18],
                     ksum_d.rearrange("(p d) -> d p", p=8)[0:64, :],
                     allow_slow_non_contiguous=True)
                G["ksblk"] = ndma(g, "g", "GN", ksblk[64:128, 1:128:18],
                                  ksum_d.rearrange("(p d) -> d p", p=8)[64:128, :],
                                  allow_slow_non_contiguous=True)
            block.gpsimd(_)
            block.flush()

        if stage < 2:
            def _(sync):
                ct.w(sync, "s", "GN", G["ksblk"])
                ct.w(sync, "s", "S", ct.cnt["S"])
            block.sync(_)
            block.flush()
            return nc

        with contextlib.ExitStack() as p2:
            sb2 = lambda n_, s_, d_: p2.enter_context(nc.sbuf_tensor(n_, s_, d_))
            ps2 = lambda n_, s_: p2.enter_context(nc.psum_tensor(n_, s_, F32))

            conv1w = sb2("conv1w", [128, KC * HID], BF16)
            conv2w = sb2("conv2w", [128, HC * C], BF16)
            lnbuf = sb2("lnbuf", [128, 4096], F32)
            hdd = sb2("hdd", [128, HC * T], BF16)
            zrow_b = sb2("zrow_b", [16, T], BF16)
            qz = sb2("qz", [128, 2 * T], BF16)
            out_bf = sb2("out_bf", [128, 4096], BF16)

            bc2 = sb2("bc2", [128, T], F32)
            bc3 = sb2("bc3", [128, T], F32)
            den16 = ps2("den16", [16, 512])
            zqps = ps2("zqps", [128, 1024])
            outps = ps2("outps", [128, 1024])
            projps = ps2("projps", [128, 1024])
            zql = lambda p: zqps[:, 512 * (p % 2): 512 * (p % 2 + 1)]
            outl = lambda i: outps[:, 512 * (i % 2): 512 * (i % 2 + 1)]
            projl = lambda i: projps[:, 512 * (i % 2): 512 * (i % 2 + 1)]
            bps2 = projps[:, 0:T]
            r2row = den16[0:1, 0:T]
            srow2 = zqps[0:1, :]

            def _(g):
                ct.w(g, "g", "PE", P[("kv", NT - 1, NB - 1)])
                ct.w(g, "g", "V", V[("wq8h", 1)])
                for k in range(KC):
                    ct.inc(g.dma_start(out=conv1w[:, HID * k: HID * (k + 1)],
                                       in_=conv1wT[128 * k:128 * (k + 1), :]), "G", 16)
                for h in range(HC):
                    ins = g.dma_start(out=conv2w[:, C * h: C * (h + 1)],
                                      in_=conv2wT[128 * h:128 * (h + 1), :])
                    ct.inc(ins, "G", 16)
                G["conv"] = ct.cnt["G"]
            block.gpsimd(_)
            block.flush()

            def _spill2(sync, t, par):
                ct.w(sync, "s", "V", V[("x1", t, KC - 1)])
                ins = sync.dma_start(
                    out=dram_tile(x1sp, t),
                    in_=lnbuf[:, :].rearrange("p (k n) -> p k n", n=T))
                S[("x1sp", t)] = ct.inc(ins, "SP", 16)
                ct.w(sync, "s", "V", V[("ln2", t)])
                ins = sync.dma_start(
                    out=dram_tile(ln2sp, t),
                    in_=xbuf[:, par * 4096:(par + 1) * 4096].rearrange(
                        "p (k n) -> p k n", n=T))
                S[("ln2sp", t)] = ct.inc(ins, "SP", 16)

            for t in range(NT):
                par = t % 2

                def _(sync, t=t, par=par):
                    if t >= 2:
                        ct.w(sync, "s", "SP", S[("ln2sp", t - 2)])
                        ct.w(sync, "s", "V", V[("am", t - 2)])
                    else:
                        ct.w(sync, "s", "V", V[("xn", NT - 2 + t)])
                    ins = sync.dma_start(
                        out=xbuf[:, par * 4096:(par + 1) * 4096].rearrange(
                            "p (k n) -> p k n", n=T),
                        in_=dram_tile(xT, t))
                    S[("x2p", t)] = ct.inc(ins, "S", 16)
                    if t >= 1:
                        ct.w(sync, "s", "V", V[("qz", t - 1, 7)])
                    else:
                        ct.w(sync, "s", "SP", S[("phisp", NT - 1)])
                    ins = sync.dma_start(
                        out=phq[:, :].rearrange("p (j n) -> p j n", n=T),
                        in_=dram_tile(phisp, t))
                    S[("phi2", t)] = ct.inc(ins, "S", 16)
                    if t > 0:
                        ct.w(sync, "s", "PE", P[("i2b", t - 1)])
                    else:
                        ct.w(sync, "s", "PE", P[("gx", NT - 1)])
                    if t == 0:
                        ct.w(sync, "s", "R", S[("rowsd", NT - 1)])
                    ndma(sync, "s", "R", mtile[0:1, :], rowsD[0:1, T * t: T * (t + 1)])
                    S[("rows2", t)] = ndma(sync, "s", "R", gptile[0:1, :],
                                           rowsD[1:2, T * t: T * (t + 1)])
                    if t > 0:
                        _spill2(sync, t - 1, (t - 1) % 2)
                block.sync(_)

                def _(tn, t=t, par=par):
                    ct.w(tn, "t", "S", S[("phi2", t)])
                    ct.w(tn, "t", "GN", G["ksblk"])
                    if t > 0:
                        ct.w(tn, "t", "V", V[("z", t - 1)])
                        ct.w(tn, "t", "V", V[("f2", t - 1)])
                    for p in range(8):
                        ins = tn.matmul(den16[:, :], ksblk[:, 16 * p:16 * (p + 1)],
                                        phqk(p), start=(p == 0), stop=(p == 7))
                    P[("den", t)] = ct.inc(ins, "PE", 1)
                block.tensor(_)

                def _(v, t=t):
                    ct.w(v, "v", "PE", P[("den", t)])
                    v.tensor_scalar_add(tsub[0:16, :], den16[:, :], 1e-6)
                    v.reciprocal(tsub[0:16, :], tsub[0:16, :])
                    ins = v.tensor_copy(out=zrow_b[:, :], in_=tsub[0:16, :])
                    V[("z", t)] = ct.inc(ins, "V", 1)
                block.vector(_)

                for p in range(8):
                    def _(tn, t=t, p=p):
                        ct.w(tn, "t", "V", V[("z", t)])
                        if p < 2:
                            if t > 0:
                                ct.w(tn, "t", "V", V[("rows2", t - 1)])
                            ct.w(tn, "t", "GN", G["rows_c"])
                        else:
                            ct.w(tn, "t", "V", V[("qz", t, p - 2)])
                        ins = tn.matmul(zql(p), ind16[:, 128 * p:128 * (p + 1)],
                                        zrow_b[:, :], start=True, stop=True)
                        P[("zq", t, p)] = ct.inc(ins, "PE", 1)
                    block.tensor(_)

                    def _(v, t=t, p=p, par=par):
                        ct.w(v, "v", "PE", P[("zq", t, p)])
                        if p >= 2:
                            ct.w(v, "v", "PE", P[("out", t, p - 2)])
                        ins = v.tensor_mul(qz[:, T * (p % 2): T * (p % 2 + 1)],
                                           phqk(p), zql(p))
                        V[("qz", t, p)] = ct.inc(ins, "V", 1)
                    block.vector(_)

                    def _(tn, t=t, p=p):
                        ct.w(tn, "t", "V", V[("qz", t, p)])
                        ct.w(tn, "t", "V", V["kvevac"])
                        if p < 2 and t > 0:
                            ct.w(tn, "t", "V", V[("x1", t - 1, 6 + p)])
                        if p >= 2:
                            ct.w(tn, "t", "A", A[("outbf", t, p - 2)])
                        sl = qz[:, T * (p % 2): T * (p % 2 + 1)]
                        tn.matmul(outl(p)[0:64, :], kv_bf[0:64, 64 * p:64 * (p + 1)],
                                  sl[0:64, :], start=True, stop=True, tile_position=(0, 0))
                        ins = tn.matmul(outl(p)[64:128, :], kv_bf[64:128, 64 * p:64 * (p + 1)],
                                        sl[64:128, :], start=True, stop=True,
                                        tile_position=(64, 64))
                        P[("out", t, p)] = ct.inc(ins, "PE", 1)
                    block.tensor(_)

                    def _(a, t=t, p=p):
                        ct.w(a, "a", "PE", P[("out", t, p)])
                        ins = a.copy(out_bf[:, T * p: T * (p + 1)], outl(p))
                        A[("outbf", t, p)] = ct.inc(ins, "A", 1)
                    block.scalar(_)

                def _(tn, t=t):
                    ct.w(tn, "t", "R", S[("rows2", t)])
                    if t > 0:
                        ct.w(tn, "t", "A", A[("i2b", t - 1)])
                        ct.w(tn, "t", "V", V[("xn2", t - 1)])
                        ct.w(tn, "t", "V", V[("ln", t - 1)])
                    ins = tn.matmul(bps2, onrb[0:1, 0:128], mtile[0:1, :],
                                    start=True, stop=True)
                    P[("mb2", t)] = ct.inc(ins, "PE", 1)
                block.tensor(_)

                def _(a, t=t):
                    ct.w(a, "a", "PE", P[("mb2", t)])
                    ins = a.copy(bc0[:, :], bps2)
                    A[("mb2", t)] = ct.inc(ins, "A", 1)
                block.scalar(_)

                def _(tn, t=t):
                    ct.w(tn, "t", "A", A[("mb2", t)])
                    ins = tn.matmul(bps2, onrb[0:1, 0:128], gptile[0:1, :],
                                    start=True, stop=True)
                    P[("gpb", t)] = ct.inc(ins, "PE", 1)
                block.tensor(_)

                def _(a, t=t):
                    ct.w(a, "a", "PE", P[("gpb", t)])
                    ins = a.copy(bc1[:, :], bps2)
                    A[("gpb", t)] = ct.inc(ins, "A", 1)
                block.scalar(_)

                def _(v, t=t, par=par):
                    ct.w(v, "v", "A", A[("gpb", t)])
                    ct.w(v, "v", "S", S[("x2p", t)])
                    if t > 0:
                        ct.w(v, "v", "SP", S[("x1sp", t - 1)])
                    for k in range(KC):
                        v.tensor_sub(tsub[:, :], xk(par, k), bc0[:, :])
                        ins = v.tensor_mul(lnbuf[:, T * k: T * (k + 1)], tsub[:, :],
                                           bc1[:, :])
                    V[("ln", t)] = ct.inc(ins, "V", 1)
                block.vector(_)

                for j in range(KC):
                    def _(tn, t=t, j=j):
                        ct.w(tn, "t", "A", A[("outbf", t, 7)])
                        ct.w(tn, "t", "G", G["pw"])
                        ct.w(tn, "t", "GN", G["rows_c"])
                        if j == 0:
                            ct.w(tn, "t", "A", A[("gpb", t)])
                        if j >= 2:
                            ct.w(tn, "t", "V", V[("u", t, j - 2)])
                        for k in range(KC):
                            tn.matmul(projl(j),
                                      projw[:, C * k + 128 * j: C * k + 128 * (j + 1)],
                                      out_bf[:, T * k: T * (k + 1)],
                                      start=(k == 0), stop=False)
                        ins = tn.matmul(projl(j), prow[0:1, 128 * j:128 * (j + 1)],
                                        onr5[0:1, :], start=False, stop=True)
                        P[("proj", t, j)] = ct.inc(ins, "PE", 1)
                    block.tensor(_)

                    def _(v, t=t, j=j):
                        ct.w(v, "v", "PE", P[("proj", t, j)])
                        ins = v.scalar_tensor_tensor(
                            out=lnbuf[:, T * j: T * (j + 1)], in0=projl(j),
                            scalar=gmsb[:, j:j + 1], in1=lnbuf[:, T * j: T * (j + 1)],
                            op0=ALU.mult, op1=ALU.add)
                        V[("u", t, j)] = ct.inc(ins, "V", 1)
                    block.vector(_)

                for k in range(KC):
                    s = k % 2
                    def _(a, t=t, k=k, s=s):
                        ct.w(a, "a", "V", V[("u", t, k)])
                        if k >= 2:
                            ct.w(a, "a", "PE", P[("r2", t, k - 2)])
                        elif t > 0:
                            ct.w(a, "a", "PE", P[("st2", t - 1, 6 + k)])
                        ins = a.activation(out=stAB[:, T * s: T * (s + 1)],
                                           in_=lnbuf[:, T * k: T * (k + 1)],
                                           func=AF.Square)
                        A[("squ", t, k)] = ct.inc(ins, "A", 1)
                    block.scalar(_)

                    def _(tn, t=t, k=k, s=s):
                        ct.w(tn, "t", "A", A[("squ", t, k)])
                        if k == 0:
                            ct.w(tn, "t", "V", V[("z", t)])
                        ins = tn.matmul(r2row, onc[:, 0:1], stAB[:, T * s: T * (s + 1)],
                                        start=(k == 0), stop=(k == KC - 1))
                        P[("r2", t, k)] = ct.inc(ins, "PE", 1)
                    block.tensor(_)

                def _(a, t=t):
                    ct.w(a, "a", "PE", P[("r2", t, KC - 1)])
                    if t > 0:
                        ct.w(a, "a", "PE", P[("f2b", t - 1)])
                    a.activation(out=rowt1[0:1, :], in_=r2row, func=AF.Ln)
                    ins = a.activation(out=rowb[0:1, :], in_=rowt1[0:1, :],
                                       func=AF.Exp, scale=-0.5)
                    A[("f20", t)] = ct.inc(ins, "A", 1)
                block.scalar(_)

                def _(v, t=t):
                    ct.w(v, "v", "A", A[("f20", t)])
                    if t > 0:
                        ct.w(v, "v", "PE", P[("f2b", t - 1)])
                    ins = v.tensor_scalar_mul(rowb[0:1, :], rowb[0:1, :], SQC)
                    V[("f2", t)] = ct.inc(ins, "V", 1)
                block.vector(_)

                def _(tn, t=t):
                    ct.w(tn, "t", "V", V[("f2", t)])
                    ct.w(tn, "t", "V", V[("u", t, KC - 1)])
                    ins = tn.matmul(bps2, onrb[0:1, 0:128], rowb[0:1, :],
                                    start=True, stop=True)
                    P[("f2b", t)] = ct.inc(ins, "PE", 1)
                block.tensor(_)

                def _(a, t=t):
                    ct.w(a, "a", "PE", P[("f2b", t)])
                    ins = a.copy(bc0[:, :], bps2)
                    A[("f2b", t)] = ct.inc(ins, "A", 1)
                block.scalar(_)

                def _(v, t=t):
                    ct.w(v, "v", "A", A[("f2b", t)])
                    for k in range(KC):
                        ins = v.scalar_tensor_tensor(
                            out=xnk(k), in0=lnbuf[:, T * k: T * (k + 1)],
                            scalar=s2sb[:, k:k + 1], in1=bc0[:, :],
                            op0=ALU.mult, op1=ALU.mult)
                    V[("xn2", t)] = ct.inc(ins, "V", 1)
                block.vector(_)

                for h in range(HC):
                    def _(tn, t=t, h=h):
                        ct.w(tn, "t", "V", V[("xn2", t)])
                        ct.w(tn, "t", "G", G["conv"])
                        if h >= 2:
                            ct.w(tn, "t", "A", A[("hdd", t, h - 2)])
                        else:
                            ct.w(tn, "t", "A", A[("outbf", t, 6 + h)])
                        for k in range(KC):
                            ins = tn.matmul(
                                outl(h),
                                conv1w[:, HID * k + 128 * h: HID * k + 128 * (h + 1)],
                                xnk(k), start=(k == 0), stop=(k == KC - 1))
                        P[("c1", t, h)] = ct.inc(ins, "PE", 1)
                    block.tensor(_)

                    def _(a, t=t, h=h):
                        ct.w(a, "a", "PE", P[("c1", t, h)])
                        ins = a.activation(out=hdd[:, T * h: T * (h + 1)], in_=outl(h),
                                           func=AF.Gelu, bias=c1bsb[:, h:h + 1])
                        A[("hdd", t, h)] = ct.inc(ins, "A", 1)
                    block.scalar(_)

                for o in range(KC):
                    def _(tn, t=t, o=o):
                        ct.w(tn, "t", "A", A[("hdd", t, HC - 1)])
                        if o >= 2:
                            ct.w(tn, "t", "V", V[("x1", t, o - 2)])
                        for h in range(HC):
                            ins = tn.matmul(outl(o),
                                            conv2w[:, C * h + 128 * o: C * h + 128 * (o + 1)],
                                            hdd[:, T * h: T * (h + 1)],
                                            start=(h == 0), stop=(h == HC - 1))
                        P[("c2", t, o)] = ct.inc(ins, "PE", 1)
                    block.tensor(_)

                    def _(v, t=t, o=o, par=par):
                        ct.w(v, "v", "PE", P[("c2", t, o)])
                        ct.w(v, "v", "V", V["gc2b"])
                        v.scalar_tensor_tensor(
                            out=lnbuf[:, T * o: T * (o + 1)], in0=outl(o),
                            scalar=gmsb[:, o:o + 1], in1=lnbuf[:, T * o: T * (o + 1)],
                            op0=ALU.mult, op1=ALU.add)
                        ins = v.scalar_tensor_tensor(
                            out=lnbuf[:, T * o: T * (o + 1)],
                            in0=lnbuf[:, T * o: T * (o + 1)],
                            scalar=gc2b[:, o:o + 1], in1=xk(par, o),
                            op0=ALU.add, op1=ALU.add)
                        V[("x1", t, o)] = ct.inc(ins, "V", 1)
                    block.vector(_)

                for k in range(KC):
                    s = k % 2
                    def _(a, t=t, k=k, s=s):
                        ct.w(a, "a", "V", V[("x1", t, k)])
                        if k >= 2:
                            ct.w(a, "a", "PE", P[("st2", t, k - 2)])
                        else:
                            ct.w(a, "a", "PE", P[("r2", t, 6 + k)])
                        ins = a.activation(out=stAB[:, T * s: T * (s + 1)],
                                           in_=lnbuf[:, T * k: T * (k + 1)],
                                           func=AF.Square)
                        A[("st2", t, k)] = ct.inc(ins, "A", 1)
                    block.scalar(_)

                    def _(tn, t=t, k=k, s=s):
                        ct.w(tn, "t", "A", A[("st2", t, k)])
                        ct.w(tn, "t", "V", V[("x1", t, k)])
                        if k == 0:
                            ct.w(tn, "t", "V", V[("qz", t, 7)])
                        tn.matmul(srow2[:, 0:T], onc[:, 0:1], stAB[:, T * s: T * (s + 1)],
                                  start=(k == 0), stop=(k == KC - 1))
                        ins = tn.matmul(srow2[:, T:2 * T], oncf[:, 0:1],
                                        lnbuf[:, T * k: T * (k + 1)],
                                        start=(k == 0), stop=(k == KC - 1))
                        P[("st2", t, k)] = ct.inc(ins, "PE", 1)
                    block.tensor(_)

                def _(v, t=t):
                    ct.w(v, "v", "PE", P[("st2", t, KC - 1)])
                    v.tensor_scalar_mul(mtile[0:1, :], srow2[0:1, T:2 * T], 1.0 / C)
                    v.tensor_tensor(out=rowt0[0:1, :], in0=mtile[0:1, :],
                                    in1=mtile[0:1, :], op=ALU.mult)
                    v.scalar_tensor_tensor(out=rowt0[0:1, :], in0=srow2[0:1, 0:T],
                                            scalar=1.0 / C, in1=rowt0[0:1, :],
                                            op0=ALU.mult, op1=ALU.subtract)
                    ins = v.tensor_scalar_add(rowt0[0:1, :], rowt0[0:1, :], 1e-5)
                    V[("rows2", t)] = ct.inc(ins, "V", 1)
                block.vector(_)

                def _(a, t=t):
                    ct.w(a, "a", "V", V[("rows2", t)])
                    a.activation(out=rowt1[0:1, :], in_=rowt0[0:1, :], func=AF.Ln)
                    ins = a.activation(out=rowt1[0:1, :], in_=rowt1[0:1, :],
                                       func=AF.Exp, scale=-0.5)
                    A[("i2", t)] = ct.inc(ins, "A", 1)
                block.scalar(_)

                def _(tn, t=t):
                    ct.w(tn, "t", "V", V[("rows2", t)])
                    ct.w(tn, "t", "A", A[("f2b", t)])
                    if t > 0:
                        ct.w(tn, "t", "V", V[("ln2", t - 1)])
                    ins = tn.matmul(bps2, onrb[0:1, 0:128], mtile[0:1, :],
                                    start=True, stop=True)
                    P[("m2b", t)] = ct.inc(ins, "PE", 1)
                block.tensor(_)

                def _(a, t=t):
                    ct.w(a, "a", "PE", P[("m2b", t)])
                    ins = a.copy(bc2[:, :], bps2)
                    A[("m2b", t)] = ct.inc(ins, "A", 1)
                block.scalar(_)

                def _(tn, t=t):
                    ct.w(tn, "t", "A", A[("i2", t)])
                    ct.w(tn, "t", "A", A[("m2b", t)])
                    ins = tn.matmul(bps2, onr[0:1, 0:128], rowt1[0:1, :],
                                    start=True, stop=True)
                    P[("i2b", t)] = ct.inc(ins, "PE", 1)
                block.tensor(_)

                def _(a, t=t):
                    ct.w(a, "a", "PE", P[("i2b", t)])
                    ins = a.copy(bc3[:, :], bps2)
                    A[("i2b", t)] = ct.inc(ins, "A", 1)
                block.scalar(_)

                def _(v, t=t, par=par):
                    ct.w(v, "v", "A", A[("i2b", t)])
                    for k in range(KC):
                        v.tensor_sub(tsub[:, :], lnbuf[:, T * k: T * (k + 1)], bc2[:, :])
                        ins = v.tensor_mul(xk(par, k), tsub[:, :], bc3[:, :])
                    V[("ln2", t)] = ct.inc(ins, "V", 1)
                block.vector(_)

                def _(v, t=t, par=par):
                    if t == 0:
                        ins = v.reduce_max(amax_acc[:, 0:1],
                                           xbuf[:, par * 4096:(par + 1) * 4096],
                                           axis=AXX, apply_absolute_value=True)
                    else:
                        ct.w(v, "v", "V",
                             ct.inc(v.reduce_max(ggtmp2[:, 0:1],
                                                 xbuf[:, par * 4096:(par + 1) * 4096],
                                                 axis=AXX, apply_absolute_value=True),
                                    "V", 1))
                        ins = v.tensor_tensor(out=amax_acc[:, 0:1], in0=amax_acc[:, 0:1],
                                              in1=ggtmp2[:, 0:1], op=ALU.max)
                    V[("am", t)] = ct.inc(ins, "V", 1)
                block.vector(_)
                block.flush()

            def _(sync):
                _spill2(sync, NT - 1, (NT - 1) % 2)
            block.sync(_)
            block.flush()

        if stage < 3:
            def _(sync):
                ct.w(sync, "s", "S", ct.cnt["S"])
                ct.w(sync, "s", "V", V[("am", NT - 1)])
            block.sync(_)
            block.flush()
            return nc

        def _(g):
            ct.w(g, "g", "V", V[("am", NT - 1)])
            ins = g.dma_start(out=ccin.ap().rearrange("o p -> p o"), in_=amax_acc[:, 0:1],
                              allow_slow_non_contiguous=True)
            mk = ct.inc(ins, "G", 16)
            ct.w(g, "g", "G", mk)
            ins = g.collective_compute(
                "AllReduce", ALU.max, replica_groups=[list(range(8))],
                ins=[ccin.ap().opt()], outs=[ccout.ap().opt()])
            ct.inc(ins, "CC", 1)
            ct.w(g, "g", "CC", ct.cnt["CC"])
            G["crow"] = ndma(g, "g", "GN", crow[:, :], ccout.ap()[0:1, :])
        block.gpsimd(_)

        def _(v):
            ct.w(v, "v", "GN", G["crow"])
            def step(f):
                ct.w(v, "v", "V", ct.inc(f(), "V", 1))
            step(lambda: v.reduce_max(scr[0:1, 5:6], crow[:, :], axis=AXX))
            step(lambda: v.tensor_scalar_max(scr[0:1, 5:6], scr[0:1, 5:6], 1e-12))
            step(lambda: v.reciprocal(scr[0:1, 6:7], scr[0:1, 5:6]))
            step(lambda: v.tensor_scalar(out=scr[0:1, 6:7], in0=scr[0:1, 6:7],
                                         scalar1=448.0, scalar2=448.0,
                                         op0=ALU.mult, op1=ALU.min))
            step(lambda: v.tensor_scalar_mul(scr[0:1, 7:8], scr[0:1, 6:7], 0.5))
            step(lambda: v.reciprocal(scr[0:1, 5:6], scr[0:1, 6:7]))
            step(lambda: v.tensor_tensor(out=scr[0:1, 5:6], in0=scr[0:1, 5:6],
                                         in1=scr[0:1, 4:5], op=ALU.mult))
            step(lambda: v.tensor_scalar_mul(scr[0:1, 5:6], scr[0:1, 5:6], 4.0))
            V["xsc"] = ct.cnt["V"]
        block.vector(_)
        block.flush()

        with contextlib.ExitStack() as p3:
            sb3 = lambda n_, s_, d_: p3.enter_context(nc.sbuf_tensor(n_, s_, d_))
            ps3 = lambda n_, s_: p3.enter_context(nc.psum_tensor(n_, s_, F32))

            l2buf = sb3("l2buf", [128, 2 * 4096], F32)
            xq8 = sb3("xq8", [128, 2 * 4096], FP8)
            ffst = sb3("ffst", [128, 2 * T], BF16)
            ffps = ps3("ffps", [128, 1024])
            screp_ps = ps3("screp_ps", [128, 2])
            ffl = lambda o: ffps[:, T * (o % 2): T * (o % 2 + 1)]

            def _(tn):
                ct.w(tn, "t", "V", V["xsc"])
                tn.matmul(screp_ps[:, 0:1], onr[0:1, 0:128], scr[0:1, 7:8],
                          start=True, stop=False)
                ins = tn.matmul(screp_ps[:, 1:2], onr[0:1, 0:128], scr[0:1, 5:6],
                                start=False, stop=True)
                P["screp"] = ct.inc(ins, "PE", 1)
            block.tensor(_)

            def _(a):
                ct.w(a, "a", "PE", P["screp"])
                ins = a.copy(screp[:, :], screp_ps[:, :])
                A["screp"] = ct.inc(ins, "A", 1)
            block.scalar(_)
            block.flush()

            if taps:
                def _(sync):
                    ct.w(sync, "s", "V", V["xsc"])
                    ndma(sync, "s", "R", tap_sc[:, :], scr[:, :])
                block.sync(_)

                def _(g):
                    ct.w(g, "g", "V", V["kvevac"])
                    ins = g.dma_start(out=tap_kv[:, :], in_=kv_bf[:, :])
                    ct.inc(ins, "G", 16)
                block.gpsimd(_)

            def _ystore(sync, t, par):
                ct.w(sync, "s", "V", V[("y", t, KC - 1)])
                ins = sync.dma_start(
                    out=dram_tile(yT, t),
                    in_=xbuf[:, par * 4096:(par + 1) * 4096].rearrange(
                        "p (k n) -> p k n", n=T))
                S[("y", t)] = ct.inc(ins, "SP", 16)

            for t in range(NT):
                par = t % 2

                def _(sync, t=t, par=par):
                    if t >= 2:
                        ct.w(sync, "s", "SP", S[("y", t - 2)])
                    else:
                        ct.w(sync, "s", "SP", S[("ln2sp", NT - 1)])
                        ct.w(sync, "s", "SP", S[("x1sp", NT - 1)])
                        ct.w(sync, "s", "V", V[("ln2", NT - 1)])
                        ct.w(sync, "s", "PE", P[("st2", NT - 1, KC - 1)])
                        ct.w(sync, "s", "V", V[("am", NT - 2 + t)])
                    ins = sync.dma_start(
                        out=l2buf[:, par * 4096:(par + 1) * 4096].rearrange(
                            "p (k n) -> p k n", n=T),
                        in_=dram_tile(ln2sp, t))
                    S[("l2", t)] = ct.inc(ins, "S", 16)
                    ins = sync.dma_start(
                        out=xbuf[:, par * 4096:(par + 1) * 4096].rearrange(
                            "p (k n) -> p k n", n=T),
                        in_=dram_tile(x1sp, t))
                    S[("x3", t)] = ct.inc(ins, "S", 16)
                    if t > 0:
                        _ystore(sync, t - 1, (t - 1) % 2)
                block.sync(_)

                def _(v, t=t, par=par):
                    ct.w(v, "v", "S", S[("l2", t)])
                    ct.w(v, "v", "A", A["screp"])
                    if t >= 2:
                        ct.w(v, "v", "PE", P[("ffn", t - 2, KC - 1)])
                    for k in range(KC):
                        ins = v.tensor_scalar(
                            out=xq8[:, par * 4096 + T * k: par * 4096 + T * (k + 1)],
                            in0=l2buf[:, par * 4096 + T * k: par * 4096 + T * (k + 1)],
                            scalar1=screp[:, 0:1], scalar2=None, op0=ALU.mult)
                    V[("q8", t)] = ct.inc(ins, "V", 1)
                block.vector(_)

                for o in range(KC):
                    def _(tn, t=t, o=o, par=par):
                        ct.w(tn, "t", "V", V[("q8", t)])
                        ct.w(tn, "t", "V", V[("wq8h", 1)])
                        if o >= 2:
                            ct.w(tn, "t", "A", A[("ffn", t, o - 2)])
                        elif t > 0:
                            ct.w(tn, "t", "A", A[("ffn", t - 1, 6 + o)])
                        for k in range(KC):
                            ins = tn.matmul(
                                ffl(o), wq8[:, C * k + 128 * o: C * k + 128 * (o + 1)],
                                xq8[:, par * 4096 + T * k: par * 4096 + T * (k + 1)],
                                start=(k == 0), stop=(k == KC - 1))
                        P[("ffn", t, o)] = ct.inc(ins, "PE", 1)
                    block.tensor(_)

                    def _(a, t=t, o=o):
                        ct.w(a, "a", "PE", P[("ffn", t, o)])
                        if o >= 2:
                            ct.w(a, "a", "V", V[("y", t, o - 2)])
                        elif t > 0:
                            ct.w(a, "a", "V", V[("y", t - 1, 6 + o)])
                        ins = a.activation(out=ffst[:, T * (o % 2): T * (o % 2 + 1)],
                                           in_=ffl(o), func=AF.Identity,
                                           bias=fbsb[:, o:o + 1], scale=screp[:, 1:2])
                        A[("ffn", t, o)] = ct.inc(ins, "A", 1)
                    block.scalar(_)

                    def _(v, t=t, o=o, par=par):
                        ct.w(v, "v", "A", A[("ffn", t, o)])
                        ct.w(v, "v", "S", S[("x3", t)])
                        ins = v.tensor_add(xk(par, o), ffst[:, T * (o % 2): T * (o % 2 + 1)],
                                           xk(par, o))
                        V[("y", t, o)] = ct.inc(ins, "V", 1)
                    block.vector(_)

                block.flush()

            def _(sync):
                _ystore(sync, NT - 1, (NT - 1) % 2)
                ct.w(sync, "s", "S", ct.cnt["S"])
                ct.w(sync, "s", "SP", ct.cnt["SP"])
            block.sync(_)
            block.flush()

    return nc


def kernel(L0_attn_x, L0_attn_scale1, L0_attn_scale2, L0_attn_gamma, L0_attn_qkv_w,
           L0_attn_proj_w, L0_attn_proj_b, L0_attn_conv1_w, L0_attn_conv1_b,
           L0_attn_conv2_w, L0_attn_conv2_b, L0_ffn_weight, L0_ffn_bias):
    x = np.asarray(L0_attn_x, dtype=np.float32)
    assert x.shape == (8, N, C)

    shared = {
        "qkvwT": np.ascontiguousarray(np.asarray(L0_attn_qkv_w, np.float32).T),
        "projwT": np.ascontiguousarray(np.asarray(L0_attn_proj_w, np.float32).T),
        "conv1wT": np.ascontiguousarray(np.asarray(L0_attn_conv1_w, np.float32).T),
        "conv2wT": np.ascontiguousarray(np.asarray(L0_attn_conv2_w, np.float32).T),
        "ffnwT": np.ascontiguousarray(np.asarray(L0_ffn_weight, np.float32).T),
        "scale1": np.ascontiguousarray(np.asarray(L0_attn_scale1, np.float32).reshape(KC, 128).T),
        "scale2": np.ascontiguousarray(np.asarray(L0_attn_scale2, np.float32).reshape(KC, 128).T),
        "gamma": np.ascontiguousarray(np.asarray(L0_attn_gamma, np.float32).reshape(KC, 128).T),
        "projb": np.ascontiguousarray(np.asarray(L0_attn_proj_b, np.float32)),
        "conv1b": np.ascontiguousarray(np.asarray(L0_attn_conv1_b, np.float32).reshape(HC, 128).T),
        "conv2b": np.ascontiguousarray(np.asarray(L0_attn_conv2_b, np.float32).reshape(KC, 128).T),
        "ffnb": np.ascontiguousarray(np.asarray(L0_ffn_bias, np.float32).reshape(KC, 128).T),
    }
    in_maps = []
    for i in range(8):
        m = dict(shared)
        m["xT"] = np.ascontiguousarray(x[i].T)
        in_maps.append(m)

    nc1 = build_nc(stage=3, taps=False)
    ph1 = {k: dict(v) for k, v in nc1._marks_out.items()}
    del nc1
    nc = build_nc(stage=3, taps=False, prev=ph1)
    for k, v in nc._marks_out.items():
        assert dict(v) == ph1[k], f"phase mismatch in {k}"
    res = run_bass_kernel_spmd(nc, in_maps, core_ids=list(range(8)))
    out = np.stack([np.asarray(r["yT"]).T for r in res.results], axis=0)
    return np.ascontiguousarray(out.astype(np.float32))


# version 12
# speedup vs baseline: 1.0039x; 1.0039x over previous
import contextlib

import ml_dtypes
import numpy as np

import concourse.bass as bass
import concourse.mybir as mybir
from concourse.bass_utils import run_bass_kernel_spmd

F32 = mybir.dt.float32
BF16 = mybir.dt.bfloat16
FP8 = mybir.dt.float8e4
AF = mybir.ActivationFunctionType
ALU = mybir.AluOpType
AXX = mybir.AxisListType.X

N = 4096
C = 1024
HID = 2048
T = 512
NT = N // T
KC = C // 128
HC = HID // 128
NB = T // 128
SQC = 32.000001


class Marks(dict):

    def __init__(self, prev=None):
        super().__init__()
        self.prev = prev

    def __getitem__(self, k):
        if dict.__contains__(self, k):
            return dict.__getitem__(self, k)
        if self.prev is not None:
            return self.prev.get(k)
        return None


class Coll:

    ORDER = ("sync", "gpsimd", "tensor", "scalar", "vector")

    def __init__(self, real):
        self.real = real
        self.q = {k: [] for k in self.ORDER}

    def _add(self, name, f):
        self.q[name].append(f)

    def sync(self, f):
        self._add("sync", f)

    def gpsimd(self, f):
        self._add("gpsimd", f)

    def tensor(self, f):
        self._add("tensor", f)

    def scalar(self, f):
        self._add("scalar", f)

    def vector(self, f):
        self._add("vector", f)

    def flush(self):
        for k in self.ORDER:
            fs, self.q[k] = self.q[k], []
            if fs:
                getattr(self.real, k)(lambda e, fs=fs: [f(e) for f in fs])


class Ctr:
    def __init__(self, sems):
        self.sems = sems
        self.cnt = {k: 0 for k in sems}
        self.water = {}

    def inc(self, ins, sem, k=1):
        self.cnt[sem] += k
        ins.then_inc(self.sems[sem], k)
        return self.cnt[sem]

    def w(self, eng, engname, sem, val):
        if val is None or val <= 0:
            return
        key = (engname, sem)
        if self.water.get(key, 0) >= val:
            return
        eng.wait_ge(self.sems[sem], val)
        self.water[key] = val


def build_nc(stage=3, taps=False, prev=None):
    nc = bass.Bass("TRN2", target_bir_lowering=False, detect_race_conditions=False)

    xT = nc.dram_tensor("xT", [C, N], F32, kind="ExternalInput").ap()
    qkvwT = nc.dram_tensor("qkvwT", [C, 3 * C], F32, kind="ExternalInput").ap()
    projwT = nc.dram_tensor("projwT", [C, C], F32, kind="ExternalInput").ap()
    conv1wT = nc.dram_tensor("conv1wT", [C, HID], F32, kind="ExternalInput").ap()
    conv2wT = nc.dram_tensor("conv2wT", [HID, C], F32, kind="ExternalInput").ap()
    ffnwT = nc.dram_tensor("ffnwT", [C, C], F32, kind="ExternalInput").ap()
    scale1_d = nc.dram_tensor("scale1", [128, KC], F32, kind="ExternalInput").ap()
    scale2_d = nc.dram_tensor("scale2", [128, KC], F32, kind="ExternalInput").ap()
    gamma_d = nc.dram_tensor("gamma", [128, KC], F32, kind="ExternalInput").ap()
    projb_d = nc.dram_tensor("projb", [C], F32, kind="ExternalInput").ap()
    conv1b_d = nc.dram_tensor("conv1b", [128, HC], F32, kind="ExternalInput").ap()
    conv2b_d = nc.dram_tensor("conv2b", [128, KC], F32, kind="ExternalInput").ap()
    ffnb_d = nc.dram_tensor("ffnb", [128, KC], F32, kind="ExternalInput").ap()

    yT = nc.dram_tensor("yT", [C, N], F32, kind="ExternalOutput").ap()

    spk = "ExternalOutput" if taps else "Internal"
    phisp = nc.dram_tensor("phisp", [C, N], BF16, kind=spk).ap()
    x1sp = nc.dram_tensor("x1sp", [C, N], F32, kind=spk).ap()
    ln2sp = nc.dram_tensor("ln2sp", [C, N], F32, kind=spk).ap()
    ksum_d = nc.dram_tensor("ksum_d", [C], F32, kind=spk).ap()
    rowsD = nc.dram_tensor("rowsD", [2, N], BF16, kind="Internal").ap()
    wam_d = nc.dram_tensor("wam_d", [128], F32, kind="Internal").ap()
    ccin = nc.dram_tensor("ccin", [1, 128], F32, kind="Internal")
    ccout = nc.dram_tensor("ccout", [1, 128], F32, kind="Internal")
    if taps:
        tap_sc = nc.dram_tensor("tap_sc", [1, 8], F32, kind="ExternalOutput").ap()
        tap_kv = nc.dram_tensor("tap_kv", [128, 512], F32, kind="ExternalOutput").ap()

    ones_col = nc.inline_tensor(np.ones((128, 1), dtype=ml_dtypes.bfloat16), "c_onescol").ap()
    onesrow_f = nc.inline_tensor(np.ones((1, 128), dtype=np.float32), "c_onesrow").ap()
    onesrow_b = nc.inline_tensor(np.ones((1, 128), dtype=ml_dtypes.bfloat16), "c_onesrowb").ap()
    ones_colf = nc.inline_tensor(np.ones((128, 1), dtype=np.float32), "c_onescolf").ap()
    ones512 = nc.inline_tensor(np.ones((1, T), dtype=ml_dtypes.bfloat16), "c_ones512").ap()
    ind = np.zeros((16, 8 * 128), dtype=np.float32)
    for p in range(8):
        ind[2 * p, 128 * p:128 * p + 64] = 1.0
        ind[2 * p + 1, 128 * p + 64:128 * p + 128] = 1.0
    ind16_d = nc.inline_tensor(ind.astype(ml_dtypes.bfloat16), "c_ind16").ap()

    with contextlib.ExitStack() as st:
        sb = lambda n_, s_, d_: st.enter_context(nc.sbuf_tensor(n_, s_, d_))
        sems = {k: st.enter_context(nc.semaphore(k)) for k in
                ["S", "G", "R", "GN", "PE", "V", "A", "CC", "W", "SP"]}
        ct = Ctr(sems)
        blk_real = st.enter_context(nc.Block())
        block = Coll(blk_real)

        def ndma(eng, engname, sem, out, in_, **kw):
            ct.w(eng, engname, sem, ct.cnt[sem])
            ins = eng.dma_start(out=out, in_=in_, **kw)
            return ct.inc(ins, sem, 16)

        xbuf = sb("xbuf", [128, 2 * 4096], F32)
        phq = sb("phq", [128, 4096], BF16)
        xn_bf = sb("xn_bf", [128, 4096], BF16)
        stAB = sb("stAB", [128, 2 * T], BF16)
        projw = sb("projw", [128, KC * C], BF16)
        wq8 = sb("wq8", [128, KC * C], FP8)
        kv_bf = sb("kv_bf", [128, 512], BF16)
        ksblk = sb("ksblk", [128, 128], BF16)
        mtile = sb("mtile", [1, T], BF16)
        gptile = sb("gptile", [1, T], BF16)
        rowb = sb("rowb", [1, T], BF16)
        rowt0 = sb("rowt0", [1, T], F32)
        rowt1 = sb("rowt1", [1, T], F32)
        bc0 = sb("bc0", [128, T], F32)
        bc1 = sb("bc1", [128, T], F32)
        tsub = sb("tsub", [128, T], F32)
        onc = sb("onc", [128, 1], BF16)
        oncf = sb("oncf", [128, 1], F32)
        onr = sb("onr", [1, 128], F32)
        onrb = sb("onrb", [1, 128], BF16)
        onr5 = sb("onr5", [1, T], BF16)
        ind16 = sb("ind16", [16, 8 * 128], BF16)
        s1sb = sb("s1sb", [128, KC], F32)
        s2sb = sb("s2sb", [128, KC], F32)
        gmsb = sb("gmsb", [128, KC], F32)
        c1bsb = sb("c1bsb", [128, HC], F32)
        fbsb = sb("fbsb", [128, KC], F32)
        prow = sb("prow", [1, C], BF16)
        c2bsb = sb("c2bsb", [128, KC], F32)
        gc2b = sb("gc2b", [128, KC], F32)
        scr = sb("scr", [1, 8], F32)
        screp = sb("screp", [128, 2], F32)
        swhrep = sb("swhrep", [128, 1], F32)
        wamax = sb("wamax", [128, 1], F32)
        wgtmp = sb("wgtmp", [128, 1], F32)
        amax_acc = sb("amax_acc", [128, 1], F32)
        ggtmp2 = sb("ggtmp2", [128, 1], F32)
        crow = sb("crow", [1, 128], F32)

        xk = lambda par, k: xbuf[:, par * 4096 + T * k: par * 4096 + T * (k + 1)]
        phqk = lambda j: phq[:, T * j: T * (j + 1)]
        xnk = lambda k: xn_bf[:, T * k: T * (k + 1)]
        stAs = lambda s: stAB[:, 0:T]
        stBs = lambda s: stAB[:, T:2 * T]

        def dram_tile(dram_ap, t):
            return dram_ap.rearrange("(k p) n -> p k n", p=128)[:, :, T * t: T * (t + 1)]

        pv = prev or {}
        V = Marks(pv.get("V"))
        A = Marks(pv.get("A"))
        P = Marks(pv.get("P"))
        G = Marks(pv.get("G"))
        GC = Marks(pv.get("GC"))
        S = Marks(pv.get("S"))
        marks = {"V": V, "A": A, "P": P, "G": G, "GC": GC, "S": S}
        nc._marks_out = marks

        def _(sync):
            ct.inc(sync.dma_start(out=onc[:, :], in_=ones_col[:, :]), "S", 16)
            ct.inc(sync.dma_start(out=oncf[:, :], in_=ones_colf[:, :]), "S", 16)
            ndma(sync, "s", "R", onr[:, :], onesrow_f[:, :])
            S["onr"] = ndma(sync, "s", "R", onrb[:, :], onesrow_b[:, :])
            ct.inc(sync.dma_start(out=s1sb[:, :], in_=scale1_d[:, :]), "S", 16)
            ct.inc(sync.dma_start(out=s2sb[:, :], in_=scale2_d[:, :]), "S", 16)
            ct.inc(sync.dma_start(out=gmsb[:, :], in_=gamma_d[:, :]), "S", 16)
            ct.inc(sync.dma_start(out=c1bsb[:, :], in_=conv1b_d[:, :]), "S", 16)
            ins = sync.dma_start(out=fbsb[:, :], in_=ffnb_d[:, :])
            S["consts"] = ct.inc(ins, "S", 16)
            ins = sync.dma_start(out=xbuf[:, 0:4096].rearrange("p (k n) -> p k n", n=T),
                                 in_=dram_tile(xT, 0))
            S[("x1p", 0)] = ct.inc(ins, "S", 16)
        block.sync(_)

        def _(g):
            ndma(g, "g", "GN", ind16[:, :], ind16_d[:, :])
            for k in range(KC):
                ct.inc(g.dma_start(out=projw[:, C * k: C * (k + 1)],
                                   in_=projwT[128 * k:128 * (k + 1), :]), "G", 16)
            G["pw"] = ct.cnt["G"]
            ndma(g, "g", "GN", prow[:, :], projb_d.rearrange("(o n) -> o n", o=1))
            G["rows_c"] = ndma(g, "g", "GN", onr5[:, :], ones512[:, :])
        block.gpsimd(_)

        def _(sync):
            ins = sync.dma_start(out=c2bsb[:, :], in_=conv2b_d[:, :])
            S["c2b"] = ct.inc(ins, "S", 16)
        block.sync(_)

        def _(v):
            ct.w(v, "v", "S", S["c2b"])
            ins = v.tensor_tensor(out=gc2b[:, :], in0=gmsb[:, :], in1=c2bsb[:, :],
                                  op=ALU.mult)
            V["gc2b"] = ct.inc(ins, "V", 1)
        block.vector(_)
        block.flush()

        with contextlib.ExitStack() as p1:
            sb1 = lambda n_, s_, d_: p1.enter_context(nc.sbuf_tensor(n_, s_, d_))
            ps1 = lambda n_, s_: p1.enter_context(nc.psum_tensor(n_, s_, F32))

            qkvw = sb1("qkvw", [128, KC * 3 * C], BF16)
            xn1b = sb1("xn1b", [128, 4096], BF16)
            wstage = sb1("wstage", [128, 4 * C], F32)
            vtile = sb1("vtile", [1, T], F32)
            ksrow = sb1("ksrow", [1, C], F32)
            rowt2 = sb1("rowt2", [1, T], F32)
            wrow = sb1("wrow", [1, 128], F32)
            st1 = sb1("st1", [128, 4 * T], BF16)
            st1A = lambda s: st1[:, 2 * s * T: (2 * s + 1) * T]
            st1B = lambda s: st1[:, (2 * s + 1) * T: (2 * s + 2) * T]
            ktmp = sb1("ktmp", [128, 1024], F32)
            phik = sb1("phik", [128, 1024], BF16)
            v_bf = sb1("v_bf", [128, 1024], BF16)
            phitmp = sb1("phitmp", [128, T], F32)

            kvacc = ps1("kvacc", [128, 512])
            ksum_ps = ps1("ksum_ps", [1, 1024])
            kvtok = ps1("kvtok", [128, 1024])
            qps = ps1("qps", [128, 1024])
            qpsl = lambda j: qps[:, T * (j % 2): T * (j % 2 + 1)]
            bps = kvtok[:, 0:T]
            srow = kvtok[0:1, :]
            xn1 = lambda par, k: (xn_bf if par == 0 else xn1b)[:, T * k: T * (k + 1)]

            def _(g):
                for k in range(KC):
                    ins = g.dma_start(out=qkvw[:, 3 * C * k: 3 * C * (k + 1)],
                                      in_=qkvwT[128 * k:128 * (k + 1), :])
                    ct.inc(ins, "G", 16)
                G["qkv"] = ct.cnt["G"]
            block.gpsimd(_)

            for half in range(2):
                def _(sync, half=half):
                    if half == 1:
                        ct.w(sync, "s", "V", V[("wmaxh", 0)])
                    ins = sync.dma_start(
                        out=wstage[:, :].rearrange("p (k n) -> p k n", n=C),
                        in_=ffnwT.rearrange("(k p) n -> p k n", p=128)[:, 4 * half:4 * half + 4, :])
                    S[("ffnw", half)] = ct.inc(ins, "W", 16)
                block.sync(_)

                def _(v, half=half):
                    ct.w(v, "v", "W", S[("ffnw", half)])
                    for k in range(4):
                        ct.w(v, "v", "V",
                             ct.inc(v.reduce_max(wgtmp[:, 0:1],
                                                 wstage[:, C * k: C * (k + 1)],
                                                 axis=AXX, apply_absolute_value=True),
                                    "V", 1))
                        if half == 0 and k == 0:
                            ins = v.tensor_copy(out=wamax[:, 0:1], in_=wgtmp[:, 0:1])
                        else:
                            ins = v.tensor_tensor(out=wamax[:, 0:1], in0=wamax[:, 0:1],
                                                  in1=wgtmp[:, 0:1], op=ALU.max)
                        ct.w(v, "v", "V", ct.inc(ins, "V", 1))
                    V[("wmaxh", half)] = ct.cnt["V"]
                block.vector(_)

            def _(sync):
                ct.w(sync, "s", "V", V[("wmaxh", 1)])
                ndma(sync, "s", "R", wam_d.rearrange("(p o) -> p o", o=1), wamax[:, 0:1])
                S["wrow"] = ndma(sync, "s", "R", wrow[:, :],
                                 wam_d.rearrange("(o p) -> o p", o=1))
            block.sync(_)

            def _(v):
                ct.w(v, "v", "R", S["wrow"])
                def step(f):
                    ct.w(v, "v", "V", ct.inc(f(), "V", 1))
                step(lambda: v.reduce_max(scr[0:1, 0:1], wrow[:, :], axis=AXX))
                step(lambda: v.tensor_scalar_max(scr[0:1, 0:1], scr[0:1, 0:1], 1e-12))
                step(lambda: v.reciprocal(scr[0:1, 1:2], scr[0:1, 0:1]))
                step(lambda: v.tensor_scalar(out=scr[0:1, 2:3], in0=scr[0:1, 1:2],
                                             scalar1=448.0, scalar2=448.0,
                                             op0=ALU.mult, op1=ALU.min))
                step(lambda: v.tensor_scalar_mul(scr[0:1, 3:4], scr[0:1, 2:3], 0.5))
                step(lambda: v.reciprocal(scr[0:1, 4:5], scr[0:1, 2:3]))
                V["wsc"] = ct.cnt["V"]
            block.vector(_)

            def _(tn):
                ct.w(tn, "t", "V", V["wsc"])
                ct.w(tn, "t", "S", S["consts"])
                ct.w(tn, "t", "R", S["onr"])
                ins = tn.matmul(kvacc[:, 0:1], onr[0:1, 0:128], scr[0:1, 3:4],
                                start=True, stop=True)
                P["swhrep"] = ct.inc(ins, "PE", 1)
            block.tensor(_)

            def _(a):
                ct.w(a, "a", "PE", P["swhrep"])
                ins = a.copy(swhrep[:, 0:1], kvacc[:, 0:1])
                A["swhrep"] = ct.inc(ins, "A", 1)
            block.scalar(_)

            for half in range(2):
                def _(sync, half=half):
                    ct.w(sync, "s", "V", V[("wmaxh", 1)] if half == 0 else V[("wq8h", 0)])
                    ins = sync.dma_start(
                        out=wstage[:, :].rearrange("p (k n) -> p k n", n=C),
                        in_=ffnwT.rearrange("(k p) n -> p k n", p=128)[:, 4 * half:4 * half + 4, :])
                    S[("ffnw2", half)] = ct.inc(ins, "W", 16)
                block.sync(_)

                def _(v, half=half):
                    ct.w(v, "v", "W", S[("ffnw2", half)])
                    ct.w(v, "v", "A", A["swhrep"])
                    for k in range(4):
                        ins = v.tensor_scalar(out=wq8[:, C * (4 * half + k): C * (4 * half + k + 1)],
                                              in0=wstage[:, C * k: C * (k + 1)],
                                              scalar1=swhrep[:, 0:1], scalar2=None,
                                              op0=ALU.mult)
                    V[("wq8h", half)] = ct.inc(ins, "V", 1)
                block.vector(_)
            block.flush()

            def _spill1(sync, t):
                ct.w(sync, "s", "A", A[("phi", t, KC - 1)])
                ins = sync.dma_start(
                    out=dram_tile(phisp, t),
                    in_=phq[:, :].rearrange("p (j n) -> p j n", n=T))
                S[("phisp", t)] = ct.inc(ins, "SP", 16)

            for t in range(NT):
                par = t % 2
                if t > 0:
                    def _(sync, t=t, par=par):
                        if t >= 2:
                            ct.w(sync, "s", "V", V[("xn", t - 2)])
                        ins = sync.dma_start(
                            out=xbuf[:, par * 4096:(par + 1) * 4096].rearrange(
                                "p (k n) -> p k n", n=T),
                            in_=dram_tile(xT, t))
                        S[("x1p", t)] = ct.inc(ins, "S", 16)
                        _spill1(sync, t - 1)
                    block.sync(_)

                for k in range(KC):
                    s = k % 2
                    def _(v, t=t, par=par, k=k, s=s):
                        ct.w(v, "v", "S", S[("x1p", t)])
                        if k >= 2:
                            ct.w(v, "v", "PE", P[("st", t, k - 2)])
                        elif t > 0:
                            ct.w(v, "v", "PE", P[("st", t - 1, 6 + k)])
                        v.tensor_copy(out=st1A(s), in_=xk(par, k))
                        ins = v.tensor_tensor(out=st1B(s), in0=st1A(s), in1=st1A(s),
                                              op=ALU.mult)
                        V[("st", t, k)] = ct.inc(ins, "V", 1)
                    block.vector(_)

                    def _(tn, t=t, k=k, s=s):
                        ct.w(tn, "t", "V", V[("st", t, k)])
                        if k == 0:
                            ct.w(tn, "t", "S", S["consts"])
                            if t > 0:
                                ct.w(tn, "t", "A", A[("vbf", t - 1, NB - 1)])
                                ct.w(tn, "t", "V", V[("rows_a", t - 1)])
                        tn.matmul(srow[:, 0:T], onc[:, 0:1], st1B(s),
                                  start=(k == 0), stop=(k == KC - 1))
                        ins = tn.matmul(srow[:, T:2 * T], onc[:, 0:1], st1A(s),
                                        start=(k == 0), stop=(k == KC - 1))
                        P[("st", t, k)] = ct.inc(ins, "PE", 1)
                    block.tensor(_)

                def _(v, t=t):
                    ct.w(v, "v", "PE", P[("st", t, KC - 1)])
                    if t > 0:
                        ct.w(v, "v", "R", S[("rowsd", t - 1)])
                        ct.w(v, "v", "A", A[("gp1", t - 1)])
                    v.tensor_scalar_mul(mtile[0:1, :], srow[0:1, T:2 * T], 1.0 / C)
                    v.tensor_tensor(out=rowt0[0:1, :], in0=mtile[0:1, :],
                                    in1=mtile[0:1, :], op=ALU.mult)
                    v.scalar_tensor_tensor(
                        out=vtile[0:1, :], in0=srow[0:1, 0:T],
                        scalar=1.0 / C, in1=rowt0[0:1, :], op0=ALU.mult, op1=ALU.subtract)
                    ins = v.tensor_scalar_add(rowt0[0:1, :], vtile[0:1, :], 1e-5)
                    V[("rows_a", t)] = ct.inc(ins, "V", 1)
                block.vector(_)

                def _(a, t=t):
                    ct.w(a, "a", "V", V[("rows_a", t)])
                    if t > 0:
                        ct.w(a, "a", "R", S[("rowsd", t - 1)])
                    a.activation(out=rowt1[0:1, :], in_=vtile[0:1, :],
                                 func=AF.Ln, scale=float(C))
                    ins = a.activation(out=rowt2[0:1, :], in_=rowt1[0:1, :],
                                       func=AF.Exp, scale=-0.5)
                    A[("gx0", t)] = ct.inc(ins, "A", 1)
                    a.activation(out=rowt1[0:1, :], in_=rowt0[0:1, :], func=AF.Ln)
                    ins = a.activation(out=gptile[0:1, :], in_=rowt1[0:1, :],
                                       func=AF.Exp, scale=-0.5)
                    A[("gp1", t)] = ct.inc(ins, "A", 1)
                block.scalar(_)

                def _(sync, t=t):
                    ct.w(sync, "s", "A", A[("gp1", t)])
                    ndma(sync, "s", "R", rowsD[0:1, T * t: T * (t + 1)], mtile[0:1, :])
                    S[("rowsd", t)] = ndma(sync, "s", "R",
                                           rowsD[1:2, T * t: T * (t + 1)], gptile[0:1, :])
                block.sync(_)

                def _(v, t=t):
                    ct.w(v, "v", "A", A[("gx0", t)])
                    if t > 0:
                        ct.w(v, "v", "PE", P[("gx", t - 1)])
                    ins = v.tensor_scalar_mul(rowb[0:1, :], rowt2[0:1, :], SQC)
                    V[("rows_b", t)] = ct.inc(ins, "V", 1)
                block.vector(_)

                def _(tn, t=t):
                    ct.w(tn, "t", "V", V[("rows_a", t)])
                    ct.w(tn, "t", "R", S["onr"])
                    ins = tn.matmul(bps, onrb[0:1, 0:128], mtile[0:1, :],
                                    start=True, stop=True)
                    P[("mb", t)] = ct.inc(ins, "PE", 1)
                block.tensor(_)

                def _(a, t=t):
                    ct.w(a, "a", "PE", P[("mb", t)])
                    ins = a.copy(bc0[:, :], bps)
                    A[("mb", t)] = ct.inc(ins, "A", 1)
                block.scalar(_)

                def _(tn, t=t):
                    ct.w(tn, "t", "V", V[("rows_b", t)])
                    ct.w(tn, "t", "A", A[("mb", t)])
                    ins = tn.matmul(bps, onrb[0:1, 0:128], rowb[0:1, :],
                                    start=True, stop=True)
                    P[("gx", t)] = ct.inc(ins, "PE", 1)
                block.tensor(_)

                def _(a, t=t):
                    ct.w(a, "a", "PE", P[("gx", t)])
                    ins = a.copy(bc1[:, :], bps)
                    A[("gx", t)] = ct.inc(ins, "A", 1)
                block.scalar(_)

                def _(v, t=t, par=par):
                    ct.w(v, "v", "A", A[("gx", t)])
                    if t >= 2:
                        ct.w(v, "v", "PE", P[("q", t - 2, KC - 1)])
                    for k in range(KC):
                        v.tensor_sub(tsub[:, :], xk(par, k), bc0[:, :])
                        ins = v.scalar_tensor_tensor(out=xn1(par, k), in0=tsub[:, :],
                                                     scalar=s1sb[:, k:k + 1], in1=bc1[:, :],
                                                     op0=ALU.mult, op1=ALU.mult)
                    V[("xn", t)] = ct.inc(ins, "V", 1)
                block.vector(_)

                for b in range(NB):
                    def _(tn, t=t, b=b, par=par):
                        ct.w(tn, "t", "V", V[("xn", t)])
                        if b == 0:
                            ct.w(tn, "t", "A", A[("gx", t)])
                        else:
                            ct.w(tn, "t", "A", A[("vbf", t, b - 1)])
                        for k in range(KC):
                            lhs = xn1(par, 0)[0:0, 0:0]
                            lhs = (xn_bf if par == 0 else xn1b)[:, T * k + 128 * b: T * k + 128 * (b + 1)]
                            tn.matmul(kvtok[:, 0:512], lhs,
                                      qkvw[:, 3 * C * k + C: 3 * C * k + C + 512],
                                      start=(k == 0), stop=(k == KC - 1))
                            ins = tn.matmul(kvtok[:, 512:1024], lhs,
                                            qkvw[:, 3 * C * k + C + 512: 3 * C * k + 2 * C],
                                            start=(k == 0), stop=(k == KC - 1))
                        P[("ktok", t, b)] = ct.inc(ins, "PE", 1)
                    block.tensor(_)

                    def _(a, t=t, b=b):
                        ct.w(a, "a", "PE", P[("ktok", t, b)])
                        ins = a.activation(out=ktmp[:, :], in_=kvtok[:, :], func=AF.Square)
                        A[("ksq", t, b)] = ct.inc(ins, "A", 1)
                        if (t, b) != (0, 0):
                            pt, pb = (t, b - 1) if b > 0 else (t - 1, NB - 1)
                            ct.w(a, "a", "PE", P[("kv", pt, pb)])
                        ins = a.activation(out=phik[:, :], in_=ktmp[:, :],
                                           func=AF.Exp, scale=-0.5)
                        A[("phik", t, b)] = ct.inc(ins, "A", 1)
                    block.scalar(_)

                    def _(tn, t=t, b=b, par=par):
                        ct.w(tn, "t", "A", A[("ksq", t, b)])
                        for k in range(KC):
                            lhs = (xn_bf if par == 0 else xn1b)[:, T * k + 128 * b: T * k + 128 * (b + 1)]
                            tn.matmul(kvtok[:, 0:512], lhs,
                                      qkvw[:, 3 * C * k + 2 * C: 3 * C * k + 2 * C + 512],
                                      start=(k == 0), stop=(k == KC - 1))
                            ins = tn.matmul(kvtok[:, 512:1024], lhs,
                                            qkvw[:, 3 * C * k + 2 * C + 512: 3 * C * (k + 1)],
                                            start=(k == 0), stop=(k == KC - 1))
                        P[("vtok", t, b)] = ct.inc(ins, "PE", 1)
                    block.tensor(_)

                    def _(a, t=t, b=b):
                        ct.w(a, "a", "PE", P[("vtok", t, b)])
                        ins = a.copy(v_bf[:, :], kvtok[:, :])
                        A[("vbf", t, b)] = ct.inc(ins, "A", 1)
                    block.scalar(_)

                    def _(tn, t=t, b=b):
                        ct.w(tn, "t", "A", A[("phik", t, b)])
                        ct.w(tn, "t", "A", A[("vbf", t, b)])
                        first = (t == 0 and b == 0)
                        last = (t == NT - 1 and b == NB - 1)
                        for p in range(8):
                            h0, h1 = 2 * p, 2 * p + 1
                            tn.matmul(kvacc[0:64, 64 * p:64 * (p + 1)],
                                      phik[:, 64 * h0:64 * (h0 + 1)],
                                      v_bf[:, 64 * h0:64 * (h0 + 1)],
                                      start=first, stop=last, tile_position=(0, 0))
                            tn.matmul(kvacc[64:128, 64 * p:64 * (p + 1)],
                                      phik[:, 64 * h1:64 * (h1 + 1)],
                                      v_bf[:, 64 * h1:64 * (h1 + 1)],
                                      start=first, stop=last, tile_position=(0, 64))
                        tn.matmul(ksum_ps[0:1, 0:512], onc[:, 0:1], phik[:, 0:512],
                                  start=first, stop=last)
                        ins = tn.matmul(ksum_ps[0:1, 512:1024], onc[:, 0:1],
                                        phik[:, 512:1024], start=first, stop=last)
                        P[("kv", t, b)] = ct.inc(ins, "PE", 1)
                    block.tensor(_)

                for j in range(KC):
                    def _(tn, t=t, j=j, par=par):
                        ct.w(tn, "t", "V", V[("xn", t)])
                        ct.w(tn, "t", "G", G["qkv"])
                        if j < 2:
                            if t > 0:
                                ct.w(tn, "t", "A", A[("sqr", t - 1, 6 + j)])
                        else:
                            ct.w(tn, "t", "A", A[("sqr", t, j - 2)])
                        for k in range(KC):
                            ins = tn.matmul(
                                qpsl(j),
                                qkvw[:, 3 * C * k + 128 * j: 3 * C * k + 128 * (j + 1)],
                                xn1(par, k), start=(k == 0), stop=(k == KC - 1))
                        P[("q", t, j)] = ct.inc(ins, "PE", 1)
                    block.tensor(_)

                    def _(a, t=t, j=j, par=par):
                        ct.w(a, "a", "PE", P[("q", t, j)])
                        ins = a.activation(out=phitmp[:, :], in_=qpsl(j), func=AF.Square)
                        A[("sqr", t, j)] = ct.inc(ins, "A", 1)
                        if t >= 1 and j == 0:
                            ct.w(a, "a", "SP", S[("phisp", t - 1)])
                        ins = a.activation(out=phqk(j), in_=phitmp[:, :],
                                           func=AF.Exp, scale=-0.5)
                        A[("phi", t, j)] = ct.inc(ins, "A", 1)
                    block.scalar(_)

                block.flush()

            def _(v):
                ct.w(v, "v", "PE", P[("kv", NT - 1, NB - 1)])
                v.tensor_copy(out=kv_bf[:, :], in_=kvacc[:, :])
                v.tensor_copy(out=ksrow[:, :], in_=ksum_ps[0:1, :])
                ins = v.memset(ksblk[:, :], 0.0)
                V["kvevac"] = ct.inc(ins, "V", 1)
            block.vector(_)

            def _(sync):
                _spill1(sync, NT - 1)
                ct.w(sync, "s", "V", V["kvevac"])
                S["ksum_d"] = ndma(sync, "s", "R", ksum_d.rearrange("(o n) -> o n", o=1),
                                   ksrow[:, :])
            block.sync(_)

            def _(g):
                ct.w(g, "g", "R", S["ksum_d"])
                ct.w(g, "g", "V", V["kvevac"])
                ndma(g, "g", "GN", ksblk[0:64, 0:127:18],
                     ksum_d.rearrange("(p d) -> d p", p=8)[0:64, :],
                     allow_slow_non_contiguous=True)
                G["ksblk"] = ndma(g, "g", "GN", ksblk[64:128, 1:128:18],
                                  ksum_d.rearrange("(p d) -> d p", p=8)[64:128, :],
                                  allow_slow_non_contiguous=True)
            block.gpsimd(_)
            block.flush()

        if stage < 2:
            def _(sync):
                ct.w(sync, "s", "GN", G["ksblk"])
                ct.w(sync, "s", "S", ct.cnt["S"])
            block.sync(_)
            block.flush()
            return nc

        with contextlib.ExitStack() as p2:
            sb2 = lambda n_, s_, d_: p2.enter_context(nc.sbuf_tensor(n_, s_, d_))
            ps2 = lambda n_, s_: p2.enter_context(nc.psum_tensor(n_, s_, F32))

            conv1w = sb2("conv1w", [128, KC * HID], BF16)
            conv2w = sb2("conv2w", [128, HC * C], BF16)
            lnbuf = sb2("lnbuf", [128, 4096], F32)
            hdd = sb2("hdd", [128, HC * T], BF16)
            zrow_b = sb2("zrow_b", [16, T], BF16)
            qz = sb2("qz", [128, 2 * T], BF16)
            out_bf = sb2("out_bf", [128, 4096], BF16)

            bc2 = sb2("bc2", [128, T], F32)
            bc3 = sb2("bc3", [128, T], F32)
            den16 = ps2("den16", [16, 512])
            zqps = ps2("zqps", [128, 1024])
            outps = ps2("outps", [128, 1024])
            projps = ps2("projps", [128, 1024])
            zql = lambda p: zqps[:, 512 * (p % 2): 512 * (p % 2 + 1)]
            outl = lambda i: outps[:, 512 * (i % 2): 512 * (i % 2 + 1)]
            projl = lambda i: projps[:, 512 * (i % 2): 512 * (i % 2 + 1)]
            bps2 = projps[:, 0:T]
            r2row = den16[0:1, 0:T]
            srow2 = zqps[0:1, :]

            def _(g):
                ct.w(g, "g", "PE", P[("kv", NT - 1, NB - 1)])
                ct.w(g, "g", "V", V[("wq8h", 1)])
                for k in range(KC):
                    ct.inc(g.dma_start(out=conv1w[:, HID * k: HID * (k + 1)],
                                       in_=conv1wT[128 * k:128 * (k + 1), :]), "G", 16)
                for h in range(HC):
                    ins = g.dma_start(out=conv2w[:, C * h: C * (h + 1)],
                                      in_=conv2wT[128 * h:128 * (h + 1), :])
                    ct.inc(ins, "G", 16)
                G["conv"] = ct.cnt["G"]
            block.gpsimd(_)
            block.flush()

            def _ln2amax(v, t, par):
                ct.w(v, "v", "A", A[("i2b", t)])
                for k in range(KC):
                    v.tensor_sub(tsub[:, :], lnbuf[:, T * k: T * (k + 1)], bc2[:, :])
                    ins = v.tensor_mul(xk(par, k), tsub[:, :], bc3[:, :])
                V[("ln2", t)] = ct.inc(ins, "V", 1)
                if t == 0:
                    ins = v.reduce_max(amax_acc[:, 0:1],
                                       xbuf[:, par * 4096:(par + 1) * 4096],
                                       axis=AXX, apply_absolute_value=True)
                else:
                    ct.w(v, "v", "V",
                         ct.inc(v.reduce_max(ggtmp2[:, 0:1],
                                             xbuf[:, par * 4096:(par + 1) * 4096],
                                             axis=AXX, apply_absolute_value=True),
                                "V", 1))
                    ins = v.tensor_tensor(out=amax_acc[:, 0:1], in0=amax_acc[:, 0:1],
                                          in1=ggtmp2[:, 0:1], op=ALU.max)
                V[("am", t)] = ct.inc(ins, "V", 1)

            def _spill2(sync, t, par):
                ct.w(sync, "s", "V", V[("x1", t, KC - 1)])
                ins = sync.dma_start(
                    out=dram_tile(x1sp, t),
                    in_=lnbuf[:, :].rearrange("p (k n) -> p k n", n=T))
                S[("x1sp", t)] = ct.inc(ins, "SP", 16)
                ct.w(sync, "s", "V", V[("ln2", t)])
                ins = sync.dma_start(
                    out=dram_tile(ln2sp, t),
                    in_=xbuf[:, par * 4096:(par + 1) * 4096].rearrange(
                        "p (k n) -> p k n", n=T))
                S[("ln2sp", t)] = ct.inc(ins, "SP", 16)

            for t in range(NT):
                par = t % 2

                def _(sync, t=t, par=par):
                    if t >= 2:
                        ct.w(sync, "s", "SP", S[("ln2sp", t - 2)])
                        ct.w(sync, "s", "V", V[("am", t - 2)])
                    else:
                        ct.w(sync, "s", "V", V[("xn", NT - 2 + t)])
                    ins = sync.dma_start(
                        out=xbuf[:, par * 4096:(par + 1) * 4096].rearrange(
                            "p (k n) -> p k n", n=T),
                        in_=dram_tile(xT, t))
                    S[("x2p", t)] = ct.inc(ins, "S", 16)
                    if t >= 1:
                        ct.w(sync, "s", "V", V[("qz", t - 1, 7)])
                    else:
                        ct.w(sync, "s", "SP", S[("phisp", NT - 1)])
                    ins = sync.dma_start(
                        out=phq[:, :].rearrange("p (j n) -> p j n", n=T),
                        in_=dram_tile(phisp, t))
                    S[("phi2", t)] = ct.inc(ins, "S", 16)
                    if t > 0:
                        ct.w(sync, "s", "PE", P[("i2b", t - 1)])
                    else:
                        ct.w(sync, "s", "PE", P[("gx", NT - 1)])
                    if t == 0:
                        ct.w(sync, "s", "R", S[("rowsd", NT - 1)])
                    ndma(sync, "s", "R", mtile[0:1, :], rowsD[0:1, T * t: T * (t + 1)])
                    S[("rows2", t)] = ndma(sync, "s", "R", gptile[0:1, :],
                                           rowsD[1:2, T * t: T * (t + 1)])
                    if t > 0:
                        _spill2(sync, t - 1, (t - 1) % 2)
                block.sync(_)

                def _(tn, t=t, par=par):
                    ct.w(tn, "t", "S", S[("phi2", t)])
                    ct.w(tn, "t", "GN", G["ksblk"])
                    if t > 0:
                        ct.w(tn, "t", "V", V[("z", t - 1)])
                        ct.w(tn, "t", "V", V[("f2", t - 1)])
                    for p in range(8):
                        ins = tn.matmul(den16[:, :], ksblk[:, 16 * p:16 * (p + 1)],
                                        phqk(p), start=(p == 0), stop=(p == 7))
                    P[("den", t)] = ct.inc(ins, "PE", 1)
                block.tensor(_)

                def _(v, t=t):
                    ct.w(v, "v", "PE", P[("den", t)])
                    v.tensor_scalar_add(tsub[0:16, :], den16[:, :], 1e-6)
                    v.reciprocal(tsub[0:16, :], tsub[0:16, :])
                    ins = v.tensor_copy(out=zrow_b[:, :], in_=tsub[0:16, :])
                    V[("z", t)] = ct.inc(ins, "V", 1)
                block.vector(_)

                for p in range(8):
                    def _(tn, t=t, p=p):
                        ct.w(tn, "t", "V", V[("z", t)])
                        if p < 2:
                            if t > 0:
                                ct.w(tn, "t", "V", V[("rows2", t - 1)])
                            ct.w(tn, "t", "GN", G["rows_c"])
                        else:
                            ct.w(tn, "t", "V", V[("qz", t, p - 2)])
                        ins = tn.matmul(zql(p), ind16[:, 128 * p:128 * (p + 1)],
                                        zrow_b[:, :], start=True, stop=True)
                        P[("zq", t, p)] = ct.inc(ins, "PE", 1)
                    block.tensor(_)

                    def _(v, t=t, p=p, par=par):
                        ct.w(v, "v", "PE", P[("zq", t, p)])
                        if p >= 2:
                            ct.w(v, "v", "PE", P[("out", t, p - 2)])
                        ins = v.tensor_mul(qz[:, T * (p % 2): T * (p % 2 + 1)],
                                           phqk(p), zql(p))
                        V[("qz", t, p)] = ct.inc(ins, "V", 1)
                    block.vector(_)

                    def _(tn, t=t, p=p):
                        ct.w(tn, "t", "V", V[("qz", t, p)])
                        ct.w(tn, "t", "V", V["kvevac"])
                        if p < 2 and t > 0:
                            ct.w(tn, "t", "V", V[("x1", t - 1, 6 + p)])
                        if p >= 2:
                            ct.w(tn, "t", "A", A[("outbf", t, p - 2)])
                        sl = qz[:, T * (p % 2): T * (p % 2 + 1)]
                        tn.matmul(outl(p)[0:64, :], kv_bf[0:64, 64 * p:64 * (p + 1)],
                                  sl[0:64, :], start=True, stop=True, tile_position=(0, 0))
                        ins = tn.matmul(outl(p)[64:128, :], kv_bf[64:128, 64 * p:64 * (p + 1)],
                                        sl[64:128, :], start=True, stop=True,
                                        tile_position=(64, 64))
                        P[("out", t, p)] = ct.inc(ins, "PE", 1)
                    block.tensor(_)

                    def _(a, t=t, p=p):
                        ct.w(a, "a", "PE", P[("out", t, p)])
                        ins = a.copy(out_bf[:, T * p: T * (p + 1)], outl(p))
                        A[("outbf", t, p)] = ct.inc(ins, "A", 1)
                    block.scalar(_)

                if t >= 1:
                    def _(v, t=t):
                        _ln2amax(v, t - 1, (t - 1) % 2)
                    block.vector(_)

                def _(tn, t=t):
                    ct.w(tn, "t", "R", S[("rows2", t)])
                    if t > 0:
                        ct.w(tn, "t", "A", A[("i2b", t - 1)])
                        ct.w(tn, "t", "V", V[("xn2", t - 1)])
                        ct.w(tn, "t", "V", V[("ln", t - 1)])
                    ins = tn.matmul(bps2, onrb[0:1, 0:128], mtile[0:1, :],
                                    start=True, stop=True)
                    P[("mb2", t)] = ct.inc(ins, "PE", 1)
                block.tensor(_)

                def _(a, t=t):
                    ct.w(a, "a", "PE", P[("mb2", t)])
                    ins = a.copy(bc0[:, :], bps2)
                    A[("mb2", t)] = ct.inc(ins, "A", 1)
                block.scalar(_)

                def _(tn, t=t):
                    ct.w(tn, "t", "A", A[("mb2", t)])
                    ins = tn.matmul(bps2, onrb[0:1, 0:128], gptile[0:1, :],
                                    start=True, stop=True)
                    P[("gpb", t)] = ct.inc(ins, "PE", 1)
                block.tensor(_)

                def _(a, t=t):
                    ct.w(a, "a", "PE", P[("gpb", t)])
                    ins = a.copy(bc1[:, :], bps2)
                    A[("gpb", t)] = ct.inc(ins, "A", 1)
                block.scalar(_)

                def _(v, t=t, par=par):
                    ct.w(v, "v", "A", A[("gpb", t)])
                    ct.w(v, "v", "S", S[("x2p", t)])
                    if t > 0:
                        ct.w(v, "v", "SP", S[("x1sp", t - 1)])
                    for k in range(KC):
                        v.tensor_sub(tsub[:, :], xk(par, k), bc0[:, :])
                        ins = v.tensor_mul(lnbuf[:, T * k: T * (k + 1)], tsub[:, :],
                                           bc1[:, :])
                    V[("ln", t)] = ct.inc(ins, "V", 1)
                block.vector(_)

                for j in range(KC):
                    def _(tn, t=t, j=j):
                        ct.w(tn, "t", "A", A[("outbf", t, 7)])
                        ct.w(tn, "t", "G", G["pw"])
                        ct.w(tn, "t", "GN", G["rows_c"])
                        if j == 0:
                            ct.w(tn, "t", "A", A[("gpb", t)])
                        if j >= 2:
                            ct.w(tn, "t", "V", V[("u", t, j - 2)])
                        for k in range(KC):
                            tn.matmul(projl(j),
                                      projw[:, C * k + 128 * j: C * k + 128 * (j + 1)],
                                      out_bf[:, T * k: T * (k + 1)],
                                      start=(k == 0), stop=False)
                        ins = tn.matmul(projl(j), prow[0:1, 128 * j:128 * (j + 1)],
                                        onr5[0:1, :], start=False, stop=True)
                        P[("proj", t, j)] = ct.inc(ins, "PE", 1)
                    block.tensor(_)

                    def _(v, t=t, j=j):
                        ct.w(v, "v", "PE", P[("proj", t, j)])
                        ins = v.scalar_tensor_tensor(
                            out=lnbuf[:, T * j: T * (j + 1)], in0=projl(j),
                            scalar=gmsb[:, j:j + 1], in1=lnbuf[:, T * j: T * (j + 1)],
                            op0=ALU.mult, op1=ALU.add)
                        V[("u", t, j)] = ct.inc(ins, "V", 1)
                    block.vector(_)

                for k in range(KC):
                    s = k % 2
                    def _(a, t=t, k=k, s=s):
                        ct.w(a, "a", "V", V[("u", t, k)])
                        if k >= 2:
                            ct.w(a, "a", "PE", P[("r2", t, k - 2)])
                        elif t > 0:
                            ct.w(a, "a", "PE", P[("st2", t - 1, 6 + k)])
                        ins = a.activation(out=stAB[:, T * s: T * (s + 1)],
                                           in_=lnbuf[:, T * k: T * (k + 1)],
                                           func=AF.Square)
                        A[("squ", t, k)] = ct.inc(ins, "A", 1)
                    block.scalar(_)

                    def _(tn, t=t, k=k, s=s):
                        ct.w(tn, "t", "A", A[("squ", t, k)])
                        if k == 0:
                            ct.w(tn, "t", "V", V[("z", t)])
                        ins = tn.matmul(r2row, onc[:, 0:1], stAB[:, T * s: T * (s + 1)],
                                        start=(k == 0), stop=(k == KC - 1))
                        P[("r2", t, k)] = ct.inc(ins, "PE", 1)
                    block.tensor(_)

                def _(a, t=t):
                    ct.w(a, "a", "PE", P[("r2", t, KC - 1)])
                    if t > 0:
                        ct.w(a, "a", "PE", P[("f2b", t - 1)])
                    a.activation(out=rowt1[0:1, :], in_=r2row, func=AF.Ln)
                    ins = a.activation(out=rowb[0:1, :], in_=rowt1[0:1, :],
                                       func=AF.Exp, scale=-0.5)
                    A[("f20", t)] = ct.inc(ins, "A", 1)
                block.scalar(_)

                def _(v, t=t):
                    ct.w(v, "v", "A", A[("f20", t)])
                    if t > 0:
                        ct.w(v, "v", "PE", P[("f2b", t - 1)])
                    ins = v.tensor_scalar_mul(rowb[0:1, :], rowb[0:1, :], SQC)
                    V[("f2", t)] = ct.inc(ins, "V", 1)
                block.vector(_)

                def _(tn, t=t):
                    ct.w(tn, "t", "V", V[("f2", t)])
                    ct.w(tn, "t", "V", V[("u", t, KC - 1)])
                    ins = tn.matmul(bps2, onrb[0:1, 0:128], rowb[0:1, :],
                                    start=True, stop=True)
                    P[("f2b", t)] = ct.inc(ins, "PE", 1)
                block.tensor(_)

                def _(a, t=t):
                    ct.w(a, "a", "PE", P[("f2b", t)])
                    ins = a.copy(bc0[:, :], bps2)
                    A[("f2b", t)] = ct.inc(ins, "A", 1)
                block.scalar(_)

                def _(v, t=t):
                    ct.w(v, "v", "A", A[("f2b", t)])
                    for k in range(KC):
                        ins = v.scalar_tensor_tensor(
                            out=xnk(k), in0=lnbuf[:, T * k: T * (k + 1)],
                            scalar=s2sb[:, k:k + 1], in1=bc0[:, :],
                            op0=ALU.mult, op1=ALU.mult)
                    V[("xn2", t)] = ct.inc(ins, "V", 1)
                block.vector(_)

                for h in range(HC):
                    def _(tn, t=t, h=h):
                        ct.w(tn, "t", "V", V[("xn2", t)])
                        ct.w(tn, "t", "G", G["conv"])
                        if h >= 2:
                            ct.w(tn, "t", "A", A[("hdd", t, h - 2)])
                        else:
                            ct.w(tn, "t", "A", A[("outbf", t, 6 + h)])
                        for k in range(KC):
                            ins = tn.matmul(
                                outl(h),
                                conv1w[:, HID * k + 128 * h: HID * k + 128 * (h + 1)],
                                xnk(k), start=(k == 0), stop=(k == KC - 1))
                        P[("c1", t, h)] = ct.inc(ins, "PE", 1)
                    block.tensor(_)

                    def _(a, t=t, h=h):
                        ct.w(a, "a", "PE", P[("c1", t, h)])
                        ins = a.activation(out=hdd[:, T * h: T * (h + 1)], in_=outl(h),
                                           func=AF.Gelu, bias=c1bsb[:, h:h + 1])
                        A[("hdd", t, h)] = ct.inc(ins, "A", 1)
                    block.scalar(_)

                for o in range(KC):
                    def _(tn, t=t, o=o):
                        ct.w(tn, "t", "A", A[("hdd", t, HC - 1)])
                        if o >= 2:
                            ct.w(tn, "t", "V", V[("x1", t, o - 2)])
                        for h in range(HC):
                            ins = tn.matmul(outl(o),
                                            conv2w[:, C * h + 128 * o: C * h + 128 * (o + 1)],
                                            hdd[:, T * h: T * (h + 1)],
                                            start=(h == 0), stop=(h == HC - 1))
                        P[("c2", t, o)] = ct.inc(ins, "PE", 1)
                    block.tensor(_)

                    def _(v, t=t, o=o, par=par):
                        ct.w(v, "v", "PE", P[("c2", t, o)])
                        ct.w(v, "v", "V", V["gc2b"])
                        v.scalar_tensor_tensor(
                            out=lnbuf[:, T * o: T * (o + 1)], in0=outl(o),
                            scalar=gmsb[:, o:o + 1], in1=lnbuf[:, T * o: T * (o + 1)],
                            op0=ALU.mult, op1=ALU.add)
                        ins = v.scalar_tensor_tensor(
                            out=lnbuf[:, T * o: T * (o + 1)],
                            in0=lnbuf[:, T * o: T * (o + 1)],
                            scalar=gc2b[:, o:o + 1], in1=xk(par, o),
                            op0=ALU.add, op1=ALU.add)
                        V[("x1", t, o)] = ct.inc(ins, "V", 1)
                    block.vector(_)

                for k in range(KC):
                    s = k % 2
                    def _(a, t=t, k=k, s=s):
                        ct.w(a, "a", "V", V[("x1", t, k)])
                        if k >= 2:
                            ct.w(a, "a", "PE", P[("st2", t, k - 2)])
                        else:
                            ct.w(a, "a", "PE", P[("r2", t, 6 + k)])
                        ins = a.activation(out=stAB[:, T * s: T * (s + 1)],
                                           in_=lnbuf[:, T * k: T * (k + 1)],
                                           func=AF.Square)
                        A[("st2", t, k)] = ct.inc(ins, "A", 1)
                    block.scalar(_)

                    def _(tn, t=t, k=k, s=s):
                        ct.w(tn, "t", "A", A[("st2", t, k)])
                        ct.w(tn, "t", "V", V[("x1", t, k)])
                        if k == 0:
                            ct.w(tn, "t", "V", V[("qz", t, 7)])
                        tn.matmul(srow2[:, 0:T], onc[:, 0:1], stAB[:, T * s: T * (s + 1)],
                                  start=(k == 0), stop=(k == KC - 1))
                        ins = tn.matmul(srow2[:, T:2 * T], oncf[:, 0:1],
                                        lnbuf[:, T * k: T * (k + 1)],
                                        start=(k == 0), stop=(k == KC - 1))
                        P[("st2", t, k)] = ct.inc(ins, "PE", 1)
                    block.tensor(_)

                def _(v, t=t):
                    ct.w(v, "v", "PE", P[("st2", t, KC - 1)])
                    v.tensor_scalar_mul(mtile[0:1, :], srow2[0:1, T:2 * T], 1.0 / C)
                    v.tensor_tensor(out=rowt0[0:1, :], in0=mtile[0:1, :],
                                    in1=mtile[0:1, :], op=ALU.mult)
                    v.scalar_tensor_tensor(out=rowt0[0:1, :], in0=srow2[0:1, 0:T],
                                            scalar=1.0 / C, in1=rowt0[0:1, :],
                                            op0=ALU.mult, op1=ALU.subtract)
                    ins = v.tensor_scalar_add(rowt0[0:1, :], rowt0[0:1, :], 1e-5)
                    V[("rows2", t)] = ct.inc(ins, "V", 1)
                block.vector(_)

                def _(a, t=t):
                    ct.w(a, "a", "V", V[("rows2", t)])
                    a.activation(out=rowt1[0:1, :], in_=rowt0[0:1, :], func=AF.Ln)
                    ins = a.activation(out=rowt1[0:1, :], in_=rowt1[0:1, :],
                                       func=AF.Exp, scale=-0.5)
                    A[("i2", t)] = ct.inc(ins, "A", 1)
                block.scalar(_)

                def _(tn, t=t):
                    ct.w(tn, "t", "V", V[("rows2", t)])
                    ct.w(tn, "t", "A", A[("f2b", t)])
                    if t > 0:
                        ct.w(tn, "t", "V", V[("ln2", t - 1)])
                    ins = tn.matmul(bps2, onrb[0:1, 0:128], mtile[0:1, :],
                                    start=True, stop=True)
                    P[("m2b", t)] = ct.inc(ins, "PE", 1)
                block.tensor(_)

                def _(a, t=t):
                    ct.w(a, "a", "PE", P[("m2b", t)])
                    ins = a.copy(bc2[:, :], bps2)
                    A[("m2b", t)] = ct.inc(ins, "A", 1)
                block.scalar(_)

                def _(tn, t=t):
                    ct.w(tn, "t", "A", A[("i2", t)])
                    ct.w(tn, "t", "A", A[("m2b", t)])
                    ins = tn.matmul(bps2, onr[0:1, 0:128], rowt1[0:1, :],
                                    start=True, stop=True)
                    P[("i2b", t)] = ct.inc(ins, "PE", 1)
                block.tensor(_)

                def _(a, t=t):
                    ct.w(a, "a", "PE", P[("i2b", t)])
                    ins = a.copy(bc3[:, :], bps2)
                    A[("i2b", t)] = ct.inc(ins, "A", 1)
                block.scalar(_)

                block.flush()

            def _(v):
                _ln2amax(v, NT - 1, (NT - 1) % 2)
            block.vector(_)

            def _(sync):
                _spill2(sync, NT - 1, (NT - 1) % 2)
            block.sync(_)
            block.flush()

        if stage < 3:
            def _(sync):
                ct.w(sync, "s", "S", ct.cnt["S"])
                ct.w(sync, "s", "V", V[("am", NT - 1)])
            block.sync(_)
            block.flush()
            return nc

        def _(g):
            ct.w(g, "g", "V", V[("am", NT - 1)])
            ins = g.dma_start(out=ccin.ap().rearrange("o p -> p o"), in_=amax_acc[:, 0:1],
                              allow_slow_non_contiguous=True)
            mk = ct.inc(ins, "G", 16)
            ct.w(g, "g", "G", mk)
            ins = g.collective_compute(
                "AllReduce", ALU.max, replica_groups=[list(range(8))],
                ins=[ccin.ap().opt()], outs=[ccout.ap().opt()])
            ct.inc(ins, "CC", 1)
            ct.w(g, "g", "CC", ct.cnt["CC"])
            G["crow"] = ndma(g, "g", "GN", crow[:, :], ccout.ap()[0:1, :])
        block.gpsimd(_)

        def _(v):
            ct.w(v, "v", "GN", G["crow"])
            def step(f):
                ct.w(v, "v", "V", ct.inc(f(), "V", 1))
            step(lambda: v.reduce_max(scr[0:1, 5:6], crow[:, :], axis=AXX))
            step(lambda: v.tensor_scalar_max(scr[0:1, 5:6], scr[0:1, 5:6], 1e-12))
            step(lambda: v.reciprocal(scr[0:1, 6:7], scr[0:1, 5:6]))
            step(lambda: v.tensor_scalar(out=scr[0:1, 6:7], in0=scr[0:1, 6:7],
                                         scalar1=448.0, scalar2=448.0,
                                         op0=ALU.mult, op1=ALU.min))
            step(lambda: v.tensor_scalar_mul(scr[0:1, 7:8], scr[0:1, 6:7], 0.5))
            step(lambda: v.reciprocal(scr[0:1, 5:6], scr[0:1, 6:7]))
            step(lambda: v.tensor_tensor(out=scr[0:1, 5:6], in0=scr[0:1, 5:6],
                                         in1=scr[0:1, 4:5], op=ALU.mult))
            step(lambda: v.tensor_scalar_mul(scr[0:1, 5:6], scr[0:1, 5:6], 4.0))
            V["xsc"] = ct.cnt["V"]
        block.vector(_)
        block.flush()

        with contextlib.ExitStack() as p3:
            sb3 = lambda n_, s_, d_: p3.enter_context(nc.sbuf_tensor(n_, s_, d_))
            ps3 = lambda n_, s_: p3.enter_context(nc.psum_tensor(n_, s_, F32))

            l2buf = sb3("l2buf", [128, 2 * 4096], F32)
            xq8 = sb3("xq8", [128, 2 * 4096], FP8)
            ffst = sb3("ffst", [128, 2 * T], BF16)
            ffps = ps3("ffps", [128, 1024])
            screp_ps = ps3("screp_ps", [128, 2])
            ffl = lambda o: ffps[:, T * (o % 2): T * (o % 2 + 1)]

            def _(tn):
                ct.w(tn, "t", "V", V["xsc"])
                tn.matmul(screp_ps[:, 0:1], onr[0:1, 0:128], scr[0:1, 7:8],
                          start=True, stop=False)
                ins = tn.matmul(screp_ps[:, 1:2], onr[0:1, 0:128], scr[0:1, 5:6],
                                start=False, stop=True)
                P["screp"] = ct.inc(ins, "PE", 1)
            block.tensor(_)

            def _(a):
                ct.w(a, "a", "PE", P["screp"])
                ins = a.copy(screp[:, :], screp_ps[:, :])
                A["screp"] = ct.inc(ins, "A", 1)
            block.scalar(_)
            block.flush()

            if taps:
                def _(sync):
                    ct.w(sync, "s", "V", V["xsc"])
                    ndma(sync, "s", "R", tap_sc[:, :], scr[:, :])
                block.sync(_)

                def _(g):
                    ct.w(g, "g", "V", V["kvevac"])
                    ins = g.dma_start(out=tap_kv[:, :], in_=kv_bf[:, :])
                    ct.inc(ins, "G", 16)
                block.gpsimd(_)

            def _ystore(sync, t, par):
                ct.w(sync, "s", "V", V[("y", t, KC - 1)])
                ins = sync.dma_start(
                    out=dram_tile(yT, t),
                    in_=xbuf[:, par * 4096:(par + 1) * 4096].rearrange(
                        "p (k n) -> p k n", n=T))
                S[("y", t)] = ct.inc(ins, "SP", 16)

            for t in range(NT):
                par = t % 2

                def _(sync, t=t, par=par):
                    if t >= 2:
                        ct.w(sync, "s", "SP", S[("y", t - 2)])
                    else:
                        ct.w(sync, "s", "SP", S[("ln2sp", NT - 1)])
                        ct.w(sync, "s", "SP", S[("x1sp", NT - 1)])
                        ct.w(sync, "s", "V", V[("ln2", NT - 1)])
                        ct.w(sync, "s", "PE", P[("st2", NT - 1, KC - 1)])
                        ct.w(sync, "s", "V", V[("am", NT - 2 + t)])
                    ins = sync.dma_start(
                        out=l2buf[:, par * 4096:(par + 1) * 4096].rearrange(
                            "p (k n) -> p k n", n=T),
                        in_=dram_tile(ln2sp, t))
                    S[("l2", t)] = ct.inc(ins, "S", 16)
                    ins = sync.dma_start(
                        out=xbuf[:, par * 4096:(par + 1) * 4096].rearrange(
                            "p (k n) -> p k n", n=T),
                        in_=dram_tile(x1sp, t))
                    S[("x3", t)] = ct.inc(ins, "S", 16)
                    if t > 0:
                        _ystore(sync, t - 1, (t - 1) % 2)
                block.sync(_)

                def _(v, t=t, par=par):
                    ct.w(v, "v", "S", S[("l2", t)])
                    ct.w(v, "v", "A", A["screp"])
                    if t >= 2:
                        ct.w(v, "v", "PE", P[("ffn", t - 2, KC - 1)])
                    for k in range(KC):
                        ins = v.tensor_scalar(
                            out=xq8[:, par * 4096 + T * k: par * 4096 + T * (k + 1)],
                            in0=l2buf[:, par * 4096 + T * k: par * 4096 + T * (k + 1)],
                            scalar1=screp[:, 0:1], scalar2=None, op0=ALU.mult)
                    V[("q8", t)] = ct.inc(ins, "V", 1)
                block.vector(_)

                for o in range(KC):
                    def _(tn, t=t, o=o, par=par):
                        ct.w(tn, "t", "V", V[("q8", t)])
                        ct.w(tn, "t", "V", V[("wq8h", 1)])
                        if o >= 2:
                            ct.w(tn, "t", "A", A[("ffn", t, o - 2)])
                        elif t > 0:
                            ct.w(tn, "t", "A", A[("ffn", t - 1, 6 + o)])
                        for k in range(KC):
                            ins = tn.matmul(
                                ffl(o), wq8[:, C * k + 128 * o: C * k + 128 * (o + 1)],
                                xq8[:, par * 4096 + T * k: par * 4096 + T * (k + 1)],
                                start=(k == 0), stop=(k == KC - 1))
                        P[("ffn", t, o)] = ct.inc(ins, "PE", 1)
                    block.tensor(_)

                    def _(a, t=t, o=o):
                        ct.w(a, "a", "PE", P[("ffn", t, o)])
                        if o >= 2:
                            ct.w(a, "a", "V", V[("y", t, o - 2)])
                        elif t > 0:
                            ct.w(a, "a", "V", V[("y", t - 1, 6 + o)])
                        ins = a.activation(out=ffst[:, T * (o % 2): T * (o % 2 + 1)],
                                           in_=ffl(o), func=AF.Identity,
                                           bias=fbsb[:, o:o + 1], scale=screp[:, 1:2])
                        A[("ffn", t, o)] = ct.inc(ins, "A", 1)
                    block.scalar(_)

                    def _(v, t=t, o=o, par=par):
                        ct.w(v, "v", "A", A[("ffn", t, o)])
                        ct.w(v, "v", "S", S[("x3", t)])
                        ins = v.tensor_add(xk(par, o), ffst[:, T * (o % 2): T * (o % 2 + 1)],
                                           xk(par, o))
                        V[("y", t, o)] = ct.inc(ins, "V", 1)
                    block.vector(_)

                block.flush()

            def _(sync):
                _ystore(sync, NT - 1, (NT - 1) % 2)
                ct.w(sync, "s", "S", ct.cnt["S"])
                ct.w(sync, "s", "SP", ct.cnt["SP"])
            block.sync(_)
            block.flush()

    return nc


def kernel(L0_attn_x, L0_attn_scale1, L0_attn_scale2, L0_attn_gamma, L0_attn_qkv_w,
           L0_attn_proj_w, L0_attn_proj_b, L0_attn_conv1_w, L0_attn_conv1_b,
           L0_attn_conv2_w, L0_attn_conv2_b, L0_ffn_weight, L0_ffn_bias):
    x = np.asarray(L0_attn_x, dtype=np.float32)
    assert x.shape == (8, N, C)

    shared = {
        "qkvwT": np.ascontiguousarray(np.asarray(L0_attn_qkv_w, np.float32).T),
        "projwT": np.ascontiguousarray(np.asarray(L0_attn_proj_w, np.float32).T),
        "conv1wT": np.ascontiguousarray(np.asarray(L0_attn_conv1_w, np.float32).T),
        "conv2wT": np.ascontiguousarray(np.asarray(L0_attn_conv2_w, np.float32).T),
        "ffnwT": np.ascontiguousarray(np.asarray(L0_ffn_weight, np.float32).T),
        "scale1": np.ascontiguousarray(np.asarray(L0_attn_scale1, np.float32).reshape(KC, 128).T),
        "scale2": np.ascontiguousarray(np.asarray(L0_attn_scale2, np.float32).reshape(KC, 128).T),
        "gamma": np.ascontiguousarray(np.asarray(L0_attn_gamma, np.float32).reshape(KC, 128).T),
        "projb": np.ascontiguousarray(np.asarray(L0_attn_proj_b, np.float32)),
        "conv1b": np.ascontiguousarray(np.asarray(L0_attn_conv1_b, np.float32).reshape(HC, 128).T),
        "conv2b": np.ascontiguousarray(np.asarray(L0_attn_conv2_b, np.float32).reshape(KC, 128).T),
        "ffnb": np.ascontiguousarray(np.asarray(L0_ffn_bias, np.float32).reshape(KC, 128).T),
    }
    in_maps = []
    for i in range(8):
        m = dict(shared)
        m["xT"] = np.ascontiguousarray(x[i].T)
        in_maps.append(m)

    nc1 = build_nc(stage=3, taps=False)
    ph1 = {k: dict(v) for k, v in nc1._marks_out.items()}
    del nc1
    nc = build_nc(stage=3, taps=False, prev=ph1)
    for k, v in nc._marks_out.items():
        assert dict(v) == ph1[k], f"phase mismatch in {k}"
    res = run_bass_kernel_spmd(nc, in_maps, core_ids=list(range(8)))
    out = np.stack([np.asarray(r["yT"]).T for r in res.results], axis=0)
    return np.ascontiguousarray(out.astype(np.float32))


# version 13
# speedup vs baseline: 1.0351x; 1.0311x over previous
import contextlib

import ml_dtypes
import numpy as np

import concourse.bass as bass
import concourse.mybir as mybir
from concourse.bass_utils import run_bass_kernel_spmd

F32 = mybir.dt.float32
BF16 = mybir.dt.bfloat16
FP8 = mybir.dt.float8e4
AF = mybir.ActivationFunctionType
ALU = mybir.AluOpType
AXX = mybir.AxisListType.X

N = 4096
C = 1024
HID = 2048
T = 512
NT = N // T
KC = C // 128
HC = HID // 128
NB = T // 128
SQC = 32.000001


class Marks(dict):

    def __init__(self, prev=None):
        super().__init__()
        self.prev = prev

    def __getitem__(self, k):
        if dict.__contains__(self, k):
            return dict.__getitem__(self, k)
        if self.prev is not None:
            return self.prev.get(k)
        return None


class Coll:

    ORDER = ("sync", "gpsimd", "tensor", "scalar", "vector")

    def __init__(self, real):
        self.real = real
        self.q = {k: [] for k in self.ORDER}

    def _add(self, name, f):
        self.q[name].append(f)

    def sync(self, f):
        self._add("sync", f)

    def gpsimd(self, f):
        self._add("gpsimd", f)

    def tensor(self, f):
        self._add("tensor", f)

    def scalar(self, f):
        self._add("scalar", f)

    def vector(self, f):
        self._add("vector", f)

    def flush(self):
        for k in self.ORDER:
            fs, self.q[k] = self.q[k], []
            if fs:
                getattr(self.real, k)(lambda e, fs=fs: [f(e) for f in fs])


class Ctr:
    def __init__(self, sems):
        self.sems = sems
        self.cnt = {k: 0 for k in sems}
        self.water = {}

    def inc(self, ins, sem, k=1):
        self.cnt[sem] += k
        ins.then_inc(self.sems[sem], k)
        return self.cnt[sem]

    def w(self, eng, engname, sem, val):
        if val is None or val <= 0:
            return
        key = (engname, sem)
        if self.water.get(key, 0) >= val:
            return
        eng.wait_ge(self.sems[sem], val)
        self.water[key] = val


def build_nc(stage=3, taps=False, prev=None):
    nc = bass.Bass("TRN2", target_bir_lowering=False, detect_race_conditions=False)

    xT = nc.dram_tensor("xT", [C, N], F32, kind="ExternalInput").ap()
    qkvwT = nc.dram_tensor("qkvwT", [C, 3 * C], F32, kind="ExternalInput").ap()
    projwT = nc.dram_tensor("projwT", [C, C], F32, kind="ExternalInput").ap()
    conv1wT = nc.dram_tensor("conv1wT", [C, HID], F32, kind="ExternalInput").ap()
    conv2wT = nc.dram_tensor("conv2wT", [HID, C], F32, kind="ExternalInput").ap()
    ffnwT = nc.dram_tensor("ffnwT", [C, C], F32, kind="ExternalInput").ap()
    scale1_d = nc.dram_tensor("scale1", [128, KC], F32, kind="ExternalInput").ap()
    scale2_d = nc.dram_tensor("scale2", [128, KC], F32, kind="ExternalInput").ap()
    gamma_d = nc.dram_tensor("gamma", [128, KC], F32, kind="ExternalInput").ap()
    projb_d = nc.dram_tensor("projb", [C], F32, kind="ExternalInput").ap()
    conv1b_d = nc.dram_tensor("conv1b", [128, HC], F32, kind="ExternalInput").ap()
    conv2b_d = nc.dram_tensor("conv2b", [128, KC], F32, kind="ExternalInput").ap()
    ffnb_d = nc.dram_tensor("ffnb", [128, KC], F32, kind="ExternalInput").ap()

    yT = nc.dram_tensor("yT", [C, N], F32, kind="ExternalOutput").ap()

    spk = "ExternalOutput" if taps else "Internal"
    phisp = nc.dram_tensor("phisp", [C, N], BF16, kind=spk).ap()
    x1sp = nc.dram_tensor("x1sp", [C, N], F32, kind=spk).ap()
    ln2sp = nc.dram_tensor("ln2sp", [C, N], F32, kind=spk).ap()
    ksum_d = nc.dram_tensor("ksum_d", [C], F32, kind=spk).ap()
    rowsD = nc.dram_tensor("rowsD", [2, N], BF16, kind="Internal").ap()
    wam_d = nc.dram_tensor("wam_d", [128], F32, kind="Internal").ap()
    ccin = nc.dram_tensor("ccin", [1, 128], F32, kind="Internal")
    ccout = nc.dram_tensor("ccout", [1, 128], F32, kind="Internal")
    if taps:
        tap_sc = nc.dram_tensor("tap_sc", [1, 8], F32, kind="ExternalOutput").ap()
        tap_kv = nc.dram_tensor("tap_kv", [128, 512], F32, kind="ExternalOutput").ap()

    ones_col = nc.inline_tensor(np.ones((128, 1), dtype=ml_dtypes.bfloat16), "c_onescol").ap()
    onesrow_f = nc.inline_tensor(np.ones((1, 128), dtype=np.float32), "c_onesrow").ap()
    onesrow_b = nc.inline_tensor(np.ones((1, 128), dtype=ml_dtypes.bfloat16), "c_onesrowb").ap()
    ones_colf = nc.inline_tensor(np.ones((128, 1), dtype=np.float32), "c_onescolf").ap()
    ones512 = nc.inline_tensor(np.ones((1, T), dtype=ml_dtypes.bfloat16), "c_ones512").ap()
    ind = np.zeros((16, 8 * 128), dtype=np.float32)
    for p in range(8):
        ind[2 * p, 128 * p:128 * p + 64] = 1.0
        ind[2 * p + 1, 128 * p + 64:128 * p + 128] = 1.0
    ind16_d = nc.inline_tensor(ind.astype(ml_dtypes.bfloat16), "c_ind16").ap()

    with contextlib.ExitStack() as st:
        sb = lambda n_, s_, d_: st.enter_context(nc.sbuf_tensor(n_, s_, d_))
        sems = {k: st.enter_context(nc.semaphore(k)) for k in
                ["S", "G", "R", "GN", "PE", "V", "A", "CC", "W", "SP"]}
        ct = Ctr(sems)
        blk_real = st.enter_context(nc.Block())
        block = Coll(blk_real)

        def ndma(eng, engname, sem, out, in_, **kw):
            ct.w(eng, engname, sem, ct.cnt[sem])
            ins = eng.dma_start(out=out, in_=in_, **kw)
            return ct.inc(ins, sem, 16)

        xbuf = sb("xbuf", [128, 2 * 4096], F32)
        phq = sb("phq", [128, 4096], BF16)
        xn_bf = sb("xn_bf", [128, 4096], BF16)
        stAB = sb("stAB", [128, 2 * T], BF16)
        projw = sb("projw", [128, KC * C], BF16)
        wq8 = sb("wq8", [128, KC * C], FP8)
        kv_bf = sb("kv_bf", [128, 512], BF16)
        ksblk = sb("ksblk", [128, 128], BF16)
        mtile = sb("mtile", [1, T], BF16)
        gptile = sb("gptile", [1, T], BF16)
        rowb = sb("rowb", [1, T], BF16)
        rowt0 = sb("rowt0", [1, T], F32)
        rowt1 = sb("rowt1", [1, T], F32)
        bc0 = sb("bc0", [128, T], F32)
        bc1 = sb("bc1", [128, T], F32)
        tsub = sb("tsub", [128, T], F32)
        onc = sb("onc", [128, 1], BF16)
        oncf = sb("oncf", [128, 1], F32)
        onr = sb("onr", [1, 128], F32)
        onrb = sb("onrb", [1, 128], BF16)
        onr5 = sb("onr5", [1, T], BF16)
        ind16 = sb("ind16", [16, 8 * 128], BF16)
        s1sb = sb("s1sb", [128, KC], F32)
        s2sb = sb("s2sb", [128, KC], F32)
        gmsb = sb("gmsb", [128, KC], F32)
        c1bsb = sb("c1bsb", [128, HC], F32)
        fbsb = sb("fbsb", [128, KC], F32)
        prow = sb("prow", [1, C], BF16)
        c2bsb = sb("c2bsb", [128, KC], F32)
        gc2b = sb("gc2b", [128, KC], F32)
        scr = sb("scr", [1, 8], F32)
        screp = sb("screp", [128, 2], F32)
        swhrep = sb("swhrep", [128, 1], F32)
        wamax = sb("wamax", [128, 1], F32)
        wgtmp = sb("wgtmp", [128, 1], F32)
        amax_acc = sb("amax_acc", [128, 1], F32)
        ggtmp2 = sb("ggtmp2", [128, 1], F32)
        crow = sb("crow", [1, 128], F32)

        xk = lambda par, k: xbuf[:, par * 4096 + T * k: par * 4096 + T * (k + 1)]
        phqk = lambda j: phq[:, T * j: T * (j + 1)]
        xnk = lambda k: xn_bf[:, T * k: T * (k + 1)]
        stAs = lambda s: stAB[:, 0:T]
        stBs = lambda s: stAB[:, T:2 * T]

        def dram_tile(dram_ap, t):
            return dram_ap.rearrange("(k p) n -> p k n", p=128)[:, :, T * t: T * (t + 1)]

        pv = prev or {}
        V = Marks(pv.get("V"))
        A = Marks(pv.get("A"))
        P = Marks(pv.get("P"))
        G = Marks(pv.get("G"))
        GC = Marks(pv.get("GC"))
        S = Marks(pv.get("S"))
        marks = {"V": V, "A": A, "P": P, "G": G, "GC": GC, "S": S}
        nc._marks_out = marks

        def _(sync):
            ct.inc(sync.dma_start(out=onc[:, :], in_=ones_col[:, :]), "S", 16)
            ct.inc(sync.dma_start(out=oncf[:, :], in_=ones_colf[:, :]), "S", 16)
            ndma(sync, "s", "R", onr[:, :], onesrow_f[:, :])
            S["onr"] = ndma(sync, "s", "R", onrb[:, :], onesrow_b[:, :])
            ct.inc(sync.dma_start(out=s1sb[:, :], in_=scale1_d[:, :]), "S", 16)
            ct.inc(sync.dma_start(out=s2sb[:, :], in_=scale2_d[:, :]), "S", 16)
            ct.inc(sync.dma_start(out=gmsb[:, :], in_=gamma_d[:, :]), "S", 16)
            ct.inc(sync.dma_start(out=c1bsb[:, :], in_=conv1b_d[:, :]), "S", 16)
            ins = sync.dma_start(out=fbsb[:, :], in_=ffnb_d[:, :])
            S["consts"] = ct.inc(ins, "S", 16)
            ins = sync.dma_start(out=xbuf[:, 0:4096].rearrange("p (k n) -> p k n", n=T),
                                 in_=dram_tile(xT, 0))
            S[("x1p", 0)] = ct.inc(ins, "S", 16)
        block.sync(_)

        def _(g):
            ndma(g, "g", "GN", ind16[:, :], ind16_d[:, :])
            for k in range(KC):
                ct.inc(g.dma_start(out=projw[:, C * k: C * (k + 1)],
                                   in_=projwT[128 * k:128 * (k + 1), :]), "G", 16)
            G["pw"] = ct.cnt["G"]
            ndma(g, "g", "GN", prow[:, :], projb_d.rearrange("(o n) -> o n", o=1))
            G["rows_c"] = ndma(g, "g", "GN", onr5[:, :], ones512[:, :])
        block.gpsimd(_)

        def _(sync):
            ins = sync.dma_start(out=c2bsb[:, :], in_=conv2b_d[:, :])
            S["c2b"] = ct.inc(ins, "S", 16)
        block.sync(_)

        def _(v):
            ct.w(v, "v", "S", S["c2b"])
            ins = v.tensor_tensor(out=gc2b[:, :], in0=gmsb[:, :], in1=c2bsb[:, :],
                                  op=ALU.mult)
            V["gc2b"] = ct.inc(ins, "V", 1)
        block.vector(_)
        block.flush()

        with contextlib.ExitStack() as p1:
            sb1 = lambda n_, s_, d_: p1.enter_context(nc.sbuf_tensor(n_, s_, d_))
            ps1 = lambda n_, s_: p1.enter_context(nc.psum_tensor(n_, s_, F32))

            qkvw = sb1("qkvw", [128, KC * 3 * C], BF16)
            xn1b = sb1("xn1b", [128, 4096], BF16)
            wstage = sb1("wstage", [128, 4 * C], F32)
            vtile = sb1("vtile", [1, T], F32)
            ksrow = sb1("ksrow", [1, C], F32)
            rowt2 = sb1("rowt2", [1, T], F32)
            wrow = sb1("wrow", [1, 128], F32)
            st1 = sb1("st1", [128, 4 * T], BF16)
            st1A = lambda s: st1[:, 2 * s * T: (2 * s + 1) * T]
            st1B = lambda s: st1[:, (2 * s + 1) * T: (2 * s + 2) * T]
            ktmp = sb1("ktmp", [128, 1024], F32)
            phik = sb1("phik", [128, 1024], BF16)
            v_bf = sb1("v_bf", [128, 1024], BF16)
            phitmp = sb1("phitmp", [128, T], F32)

            kvacc = ps1("kvacc", [128, 512])
            ksum_ps = ps1("ksum_ps", [1, 1024])
            kvtok = ps1("kvtok", [128, 1024])
            qps = ps1("qps", [128, 1024])
            qpsl = lambda j: qps[:, T * (j % 2): T * (j % 2 + 1)]
            bps = kvtok[:, 0:T]
            srow = kvtok[0:1, :]
            xn1 = lambda par, k: (xn_bf if par == 0 else xn1b)[:, T * k: T * (k + 1)]

            def _(g):
                for k in range(KC):
                    ins = g.dma_start(out=qkvw[:, 3 * C * k: 3 * C * (k + 1)],
                                      in_=qkvwT[128 * k:128 * (k + 1), :])
                    ct.inc(ins, "G", 16)
                G["qkv"] = ct.cnt["G"]
            block.gpsimd(_)

            for half in range(2):
                def _(sync, half=half):
                    if half == 1:
                        ct.w(sync, "s", "V", V[("wmaxh", 0)])
                    ins = sync.dma_start(
                        out=wstage[:, :].rearrange("p (k n) -> p k n", n=C),
                        in_=ffnwT.rearrange("(k p) n -> p k n", p=128)[:, 4 * half:4 * half + 4, :])
                    S[("ffnw", half)] = ct.inc(ins, "W", 16)
                block.sync(_)

                def _(v, half=half):
                    ct.w(v, "v", "W", S[("ffnw", half)])
                    for k in range(4):
                        ct.w(v, "v", "V",
                             ct.inc(v.reduce_max(wgtmp[:, 0:1],
                                                 wstage[:, C * k: C * (k + 1)],
                                                 axis=AXX, apply_absolute_value=True),
                                    "V", 1))
                        if half == 0 and k == 0:
                            ins = v.tensor_copy(out=wamax[:, 0:1], in_=wgtmp[:, 0:1])
                        else:
                            ins = v.tensor_tensor(out=wamax[:, 0:1], in0=wamax[:, 0:1],
                                                  in1=wgtmp[:, 0:1], op=ALU.max)
                        ct.w(v, "v", "V", ct.inc(ins, "V", 1))
                    V[("wmaxh", half)] = ct.cnt["V"]
                block.vector(_)

            def _(sync):
                ct.w(sync, "s", "V", V[("wmaxh", 1)])
                ndma(sync, "s", "R", wam_d.rearrange("(p o) -> p o", o=1), wamax[:, 0:1])
                S["wrow"] = ndma(sync, "s", "R", wrow[:, :],
                                 wam_d.rearrange("(o p) -> o p", o=1))
            block.sync(_)

            def _(v):
                ct.w(v, "v", "R", S["wrow"])
                def step(f):
                    ct.w(v, "v", "V", ct.inc(f(), "V", 1))
                step(lambda: v.reduce_max(scr[0:1, 0:1], wrow[:, :], axis=AXX))
                step(lambda: v.tensor_scalar_max(scr[0:1, 0:1], scr[0:1, 0:1], 1e-12))
                step(lambda: v.reciprocal(scr[0:1, 1:2], scr[0:1, 0:1]))
                step(lambda: v.tensor_scalar(out=scr[0:1, 2:3], in0=scr[0:1, 1:2],
                                             scalar1=448.0, scalar2=448.0,
                                             op0=ALU.mult, op1=ALU.min))
                step(lambda: v.tensor_scalar_mul(scr[0:1, 3:4], scr[0:1, 2:3], 0.5))
                step(lambda: v.reciprocal(scr[0:1, 4:5], scr[0:1, 2:3]))
                V["wsc"] = ct.cnt["V"]
            block.vector(_)

            def _(tn):
                ct.w(tn, "t", "V", V["wsc"])
                ct.w(tn, "t", "S", S["consts"])
                ct.w(tn, "t", "R", S["onr"])
                ins = tn.matmul(kvacc[:, 0:1], onr[0:1, 0:128], scr[0:1, 3:4],
                                start=True, stop=True)
                P["swhrep"] = ct.inc(ins, "PE", 1)
            block.tensor(_)

            def _(a):
                ct.w(a, "a", "PE", P["swhrep"])
                ins = a.copy(swhrep[:, 0:1], kvacc[:, 0:1])
                A["swhrep"] = ct.inc(ins, "A", 1)
            block.scalar(_)

            for half in range(2):
                def _(sync, half=half):
                    ct.w(sync, "s", "V", V[("wmaxh", 1)] if half == 0 else V[("wq8h", 0)])
                    ins = sync.dma_start(
                        out=wstage[:, :].rearrange("p (k n) -> p k n", n=C),
                        in_=ffnwT.rearrange("(k p) n -> p k n", p=128)[:, 4 * half:4 * half + 4, :])
                    S[("ffnw2", half)] = ct.inc(ins, "W", 16)
                block.sync(_)

                def _(v, half=half):
                    ct.w(v, "v", "W", S[("ffnw2", half)])
                    ct.w(v, "v", "A", A["swhrep"])
                    for k in range(4):
                        ins = v.tensor_scalar(out=wq8[:, C * (4 * half + k): C * (4 * half + k + 1)],
                                              in0=wstage[:, C * k: C * (k + 1)],
                                              scalar1=swhrep[:, 0:1], scalar2=None,
                                              op0=ALU.mult)
                    V[("wq8h", half)] = ct.inc(ins, "V", 1)
                block.vector(_)
            block.flush()

            def _spill1(sync, t):
                ct.w(sync, "s", "A", A[("phi", t, KC - 1)])
                ins = sync.dma_start(
                    out=dram_tile(phisp, t),
                    in_=phq[:, :].rearrange("p (j n) -> p j n", n=T))
                S[("phisp", t)] = ct.inc(ins, "SP", 16)

            for t in range(NT):
                par = t % 2
                if t > 0:
                    def _(sync, t=t, par=par):
                        if t >= 2:
                            ct.w(sync, "s", "V", V[("xn", t - 2)])
                        ins = sync.dma_start(
                            out=xbuf[:, par * 4096:(par + 1) * 4096].rearrange(
                                "p (k n) -> p k n", n=T),
                            in_=dram_tile(xT, t))
                        S[("x1p", t)] = ct.inc(ins, "S", 16)
                        _spill1(sync, t - 1)
                    block.sync(_)

                for k in range(KC):
                    s = k % 2
                    def _(v, t=t, par=par, k=k, s=s):
                        ct.w(v, "v", "S", S[("x1p", t)])
                        if k >= 2:
                            ct.w(v, "v", "PE", P[("st", t, k - 2)])
                        elif t > 0:
                            ct.w(v, "v", "PE", P[("st", t - 1, 6 + k)])
                        v.tensor_copy(out=st1A(s), in_=xk(par, k))
                        ins = v.tensor_tensor(out=st1B(s), in0=st1A(s), in1=st1A(s),
                                              op=ALU.mult)
                        V[("st", t, k)] = ct.inc(ins, "V", 1)
                    block.vector(_)

                    def _(tn, t=t, k=k, s=s):
                        ct.w(tn, "t", "V", V[("st", t, k)])
                        if k == 0:
                            ct.w(tn, "t", "S", S["consts"])
                            if t > 0:
                                ct.w(tn, "t", "A", A[("vbf", t - 1, NB - 1)])
                                ct.w(tn, "t", "V", V[("rows_a", t - 1)])
                        tn.matmul(srow[:, 0:T], onc[:, 0:1], st1B(s),
                                  start=(k == 0), stop=(k == KC - 1))
                        ins = tn.matmul(srow[:, T:2 * T], onc[:, 0:1], st1A(s),
                                        start=(k == 0), stop=(k == KC - 1))
                        P[("st", t, k)] = ct.inc(ins, "PE", 1)
                    block.tensor(_)

                def _(v, t=t):
                    ct.w(v, "v", "PE", P[("st", t, KC - 1)])
                    if t > 0:
                        ct.w(v, "v", "R", S[("rowsd", t - 1)])
                        ct.w(v, "v", "A", A[("gp1", t - 1)])
                    v.tensor_scalar_mul(mtile[0:1, :], srow[0:1, T:2 * T], 1.0 / C)
                    v.tensor_tensor(out=rowt0[0:1, :], in0=mtile[0:1, :],
                                    in1=mtile[0:1, :], op=ALU.mult)
                    v.scalar_tensor_tensor(
                        out=vtile[0:1, :], in0=srow[0:1, 0:T],
                        scalar=1.0 / C, in1=rowt0[0:1, :], op0=ALU.mult, op1=ALU.subtract)
                    ins = v.tensor_scalar_add(rowt0[0:1, :], vtile[0:1, :], 1e-5)
                    V[("rows_a", t)] = ct.inc(ins, "V", 1)
                block.vector(_)

                def _(a, t=t):
                    ct.w(a, "a", "V", V[("rows_a", t)])
                    if t > 0:
                        ct.w(a, "a", "R", S[("rowsd", t - 1)])
                    a.activation(out=rowt1[0:1, :], in_=vtile[0:1, :],
                                 func=AF.Ln, scale=float(C))
                    ins = a.activation(out=rowt2[0:1, :], in_=rowt1[0:1, :],
                                       func=AF.Exp, scale=-0.5)
                    A[("gx0", t)] = ct.inc(ins, "A", 1)
                    a.activation(out=rowt1[0:1, :], in_=rowt0[0:1, :], func=AF.Ln)
                    ins = a.activation(out=gptile[0:1, :], in_=rowt1[0:1, :],
                                       func=AF.Exp, scale=-0.5)
                    A[("gp1", t)] = ct.inc(ins, "A", 1)
                block.scalar(_)

                def _(sync, t=t):
                    ct.w(sync, "s", "A", A[("gp1", t)])
                    ndma(sync, "s", "R", rowsD[0:1, T * t: T * (t + 1)], mtile[0:1, :])
                    S[("rowsd", t)] = ndma(sync, "s", "R",
                                           rowsD[1:2, T * t: T * (t + 1)], gptile[0:1, :])
                block.sync(_)

                def _(v, t=t):
                    ct.w(v, "v", "A", A[("gx0", t)])
                    if t > 0:
                        ct.w(v, "v", "PE", P[("gx", t - 1)])
                    ins = v.tensor_scalar_mul(rowb[0:1, :], rowt2[0:1, :], SQC)
                    V[("rows_b", t)] = ct.inc(ins, "V", 1)
                block.vector(_)

                def _(tn, t=t):
                    ct.w(tn, "t", "V", V[("rows_a", t)])
                    ct.w(tn, "t", "R", S["onr"])
                    ins = tn.matmul(bps, onrb[0:1, 0:128], mtile[0:1, :],
                                    start=True, stop=True)
                    P[("mb", t)] = ct.inc(ins, "PE", 1)
                block.tensor(_)

                def _(a, t=t):
                    ct.w(a, "a", "PE", P[("mb", t)])
                    ins = a.copy(bc0[:, :], bps)
                    A[("mb", t)] = ct.inc(ins, "A", 1)
                block.scalar(_)

                def _(tn, t=t):
                    ct.w(tn, "t", "V", V[("rows_b", t)])
                    ct.w(tn, "t", "A", A[("mb", t)])
                    ins = tn.matmul(bps, onrb[0:1, 0:128], rowb[0:1, :],
                                    start=True, stop=True)
                    P[("gx", t)] = ct.inc(ins, "PE", 1)
                block.tensor(_)

                def _(a, t=t):
                    ct.w(a, "a", "PE", P[("gx", t)])
                    ins = a.copy(bc1[:, :], bps)
                    A[("gx", t)] = ct.inc(ins, "A", 1)
                block.scalar(_)

                def _(v, t=t, par=par):
                    ct.w(v, "v", "A", A[("gx", t)])
                    if t >= 2:
                        ct.w(v, "v", "PE", P[("q", t - 2, KC - 1)])
                    for k in range(KC):
                        v.tensor_sub(tsub[:, :], xk(par, k), bc0[:, :])
                        ins = v.scalar_tensor_tensor(out=xn1(par, k), in0=tsub[:, :],
                                                     scalar=s1sb[:, k:k + 1], in1=bc1[:, :],
                                                     op0=ALU.mult, op1=ALU.mult)
                    V[("xn", t)] = ct.inc(ins, "V", 1)
                block.vector(_)

                for b in range(NB):
                    def _(tn, t=t, b=b, par=par):
                        ct.w(tn, "t", "V", V[("xn", t)])
                        if b == 0:
                            ct.w(tn, "t", "A", A[("gx", t)])
                        else:
                            ct.w(tn, "t", "A", A[("vbf", t, b - 1)])
                        for k in range(KC):
                            lhs = xn1(par, 0)[0:0, 0:0]
                            lhs = (xn_bf if par == 0 else xn1b)[:, T * k + 128 * b: T * k + 128 * (b + 1)]
                            tn.matmul(kvtok[:, 0:512], lhs,
                                      qkvw[:, 3 * C * k + C: 3 * C * k + C + 512],
                                      start=(k == 0), stop=(k == KC - 1))
                            ins = tn.matmul(kvtok[:, 512:1024], lhs,
                                            qkvw[:, 3 * C * k + C + 512: 3 * C * k + 2 * C],
                                            start=(k == 0), stop=(k == KC - 1))
                        P[("ktok", t, b)] = ct.inc(ins, "PE", 1)
                    block.tensor(_)

                    def _(a, t=t, b=b):
                        ct.w(a, "a", "PE", P[("ktok", t, b)])
                        ins = a.activation(out=ktmp[:, :], in_=kvtok[:, :], func=AF.Square)
                        A[("ksq", t, b)] = ct.inc(ins, "A", 1)
                        if (t, b) != (0, 0):
                            pt, pb = (t, b - 1) if b > 0 else (t - 1, NB - 1)
                            ct.w(a, "a", "PE", P[("kv", pt, pb)])
                        ins = a.activation(out=phik[:, :], in_=ktmp[:, :],
                                           func=AF.Exp, scale=-0.5)
                        A[("phik", t, b)] = ct.inc(ins, "A", 1)
                    block.scalar(_)

                    def _(tn, t=t, b=b, par=par):
                        ct.w(tn, "t", "A", A[("ksq", t, b)])
                        for k in range(KC):
                            lhs = (xn_bf if par == 0 else xn1b)[:, T * k + 128 * b: T * k + 128 * (b + 1)]
                            tn.matmul(kvtok[:, 0:512], lhs,
                                      qkvw[:, 3 * C * k + 2 * C: 3 * C * k + 2 * C + 512],
                                      start=(k == 0), stop=(k == KC - 1))
                            ins = tn.matmul(kvtok[:, 512:1024], lhs,
                                            qkvw[:, 3 * C * k + 2 * C + 512: 3 * C * (k + 1)],
                                            start=(k == 0), stop=(k == KC - 1))
                        P[("vtok", t, b)] = ct.inc(ins, "PE", 1)
                    block.tensor(_)

                    def _(a, t=t, b=b):
                        ct.w(a, "a", "PE", P[("vtok", t, b)])
                        ins = a.copy(v_bf[:, :], kvtok[:, :])
                        A[("vbf", t, b)] = ct.inc(ins, "A", 1)
                    block.scalar(_)

                    def _(tn, t=t, b=b):
                        ct.w(tn, "t", "A", A[("phik", t, b)])
                        ct.w(tn, "t", "A", A[("vbf", t, b)])
                        first = (t == 0 and b == 0)
                        last = (t == NT - 1 and b == NB - 1)
                        for p in range(8):
                            h0, h1 = 2 * p, 2 * p + 1
                            tn.matmul(kvacc[0:64, 64 * p:64 * (p + 1)],
                                      phik[:, 64 * h0:64 * (h0 + 1)],
                                      v_bf[:, 64 * h0:64 * (h0 + 1)],
                                      start=first, stop=last, tile_position=(0, 0))
                            tn.matmul(kvacc[64:128, 64 * p:64 * (p + 1)],
                                      phik[:, 64 * h1:64 * (h1 + 1)],
                                      v_bf[:, 64 * h1:64 * (h1 + 1)],
                                      start=first, stop=last, tile_position=(0, 64))
                        tn.matmul(ksum_ps[0:1, 0:512], onc[:, 0:1], phik[:, 0:512],
                                  start=first, stop=last)
                        ins = tn.matmul(ksum_ps[0:1, 512:1024], onc[:, 0:1],
                                        phik[:, 512:1024], start=first, stop=last)
                        P[("kv", t, b)] = ct.inc(ins, "PE", 1)
                    block.tensor(_)

                for j in range(KC):
                    def _(tn, t=t, j=j, par=par):
                        ct.w(tn, "t", "V", V[("xn", t)])
                        ct.w(tn, "t", "G", G["qkv"])
                        if j < 2:
                            if t > 0:
                                ct.w(tn, "t", "A", A[("sqr", t - 1, 6 + j)])
                        else:
                            ct.w(tn, "t", "A", A[("sqr", t, j - 2)])
                        for k in range(KC):
                            ins = tn.matmul(
                                qpsl(j),
                                qkvw[:, 3 * C * k + 128 * j: 3 * C * k + 128 * (j + 1)],
                                xn1(par, k), start=(k == 0), stop=(k == KC - 1))
                        P[("q", t, j)] = ct.inc(ins, "PE", 1)
                    block.tensor(_)

                    def _(a, t=t, j=j, par=par):
                        ct.w(a, "a", "PE", P[("q", t, j)])
                        ins = a.activation(out=phitmp[:, :], in_=qpsl(j), func=AF.Square)
                        A[("sqr", t, j)] = ct.inc(ins, "A", 1)
                        if t >= 1 and j == 0:
                            ct.w(a, "a", "SP", S[("phisp", t - 1)])
                        ins = a.activation(out=phqk(j), in_=phitmp[:, :],
                                           func=AF.Exp, scale=-0.5)
                        A[("phi", t, j)] = ct.inc(ins, "A", 1)
                    block.scalar(_)

                block.flush()

            def _(v):
                ct.w(v, "v", "PE", P[("kv", NT - 1, NB - 1)])
                v.tensor_copy(out=kv_bf[:, :], in_=kvacc[:, :])
                v.tensor_copy(out=ksrow[:, :], in_=ksum_ps[0:1, :])
                ins = v.memset(ksblk[:, :], 0.0)
                V["kvevac"] = ct.inc(ins, "V", 1)
            block.vector(_)

            def _(sync):
                _spill1(sync, NT - 1)
                ct.w(sync, "s", "V", V["kvevac"])
                S["ksum_d"] = ndma(sync, "s", "R", ksum_d.rearrange("(o n) -> o n", o=1),
                                   ksrow[:, :])
            block.sync(_)

            def _(g):
                ct.w(g, "g", "R", S["ksum_d"])
                ct.w(g, "g", "V", V["kvevac"])
                ndma(g, "g", "GN", ksblk[0:64, 0:127:18],
                     ksum_d.rearrange("(p d) -> d p", p=8)[0:64, :],
                     allow_slow_non_contiguous=True)
                G["ksblk"] = ndma(g, "g", "GN", ksblk[64:128, 1:128:18],
                                  ksum_d.rearrange("(p d) -> d p", p=8)[64:128, :],
                                  allow_slow_non_contiguous=True)
            block.gpsimd(_)
            block.flush()

        if stage < 2:
            def _(sync):
                ct.w(sync, "s", "GN", G["ksblk"])
                ct.w(sync, "s", "S", ct.cnt["S"])
            block.sync(_)
            block.flush()
            return nc

        with contextlib.ExitStack() as p2:
            sb2 = lambda n_, s_, d_: p2.enter_context(nc.sbuf_tensor(n_, s_, d_))
            ps2 = lambda n_, s_: p2.enter_context(nc.psum_tensor(n_, s_, F32))

            conv1w = sb2("conv1w", [128, KC * HID], BF16)
            conv2w = sb2("conv2w", [128, HC * C], BF16)
            lnbuf = sb2("lnbuf", [128, 4096], F32)
            hdd = sb2("hdd", [128, HC * T], BF16)
            zrow_b = sb2("zrow_b", [16, T], BF16)
            qz = sb2("qz", [128, 2 * T], BF16)
            out_bf = sb2("out_bf", [128, 4096], BF16)

            bc2 = sb2("bc2", [128, T], F32)
            bc3 = sb2("bc3", [128, T], F32)
            den16 = ps2("den16", [16, 512])
            zqps = ps2("zqps", [128, 1024])
            outps = ps2("outps", [128, 1024])
            projps = ps2("projps", [128, 1024])
            zql = lambda p: zqps[:, 512 * (p % 2): 512 * (p % 2 + 1)]
            outl = lambda i: outps[:, 512 * (i % 2): 512 * (i % 2 + 1)]
            projl = lambda i: projps[:, 512 * (i % 2): 512 * (i % 2 + 1)]
            bps2 = projps[:, 0:T]
            r2row = den16[0:1, 0:T]
            srow2 = zqps[0:1, :]

            def _(g):
                ct.w(g, "g", "PE", P[("kv", NT - 1, NB - 1)])
                ct.w(g, "g", "V", V[("wq8h", 1)])
                for k in range(KC):
                    ct.inc(g.dma_start(out=conv1w[:, HID * k: HID * (k + 1)],
                                       in_=conv1wT[128 * k:128 * (k + 1), :]), "G", 16)
                for h in range(HC):
                    ins = g.dma_start(out=conv2w[:, C * h: C * (h + 1)],
                                      in_=conv2wT[128 * h:128 * (h + 1), :])
                    ct.inc(ins, "G", 16)
                G["conv"] = ct.cnt["G"]
            block.gpsimd(_)
            block.flush()

            def _ln2amax(v, t, par):
                ct.w(v, "v", "A", A[("i2b", t)])
                ct.w(v, "v", "SP", S[("x1sp", t)])
                for k in range(KC):
                    v.tensor_sub(tsub[:, :], xk(par, k), bc2[:, :])
                    ins = v.tensor_mul(xk(par, k), tsub[:, :], bc3[:, :])
                V[("ln2", t)] = ct.inc(ins, "V", 1)
                if t == 0:
                    ins = v.reduce_max(amax_acc[:, 0:1],
                                       xbuf[:, par * 4096:(par + 1) * 4096],
                                       axis=AXX, apply_absolute_value=True)
                else:
                    ct.w(v, "v", "V",
                         ct.inc(v.reduce_max(ggtmp2[:, 0:1],
                                             xbuf[:, par * 4096:(par + 1) * 4096],
                                             axis=AXX, apply_absolute_value=True),
                                "V", 1))
                    ins = v.tensor_tensor(out=amax_acc[:, 0:1], in0=amax_acc[:, 0:1],
                                          in1=ggtmp2[:, 0:1], op=ALU.max)
                V[("am", t)] = ct.inc(ins, "V", 1)

            def _spill2(sync, t, par):
                ct.w(sync, "s", "V", V[("x1", t, KC - 1)])
                ins = sync.dma_start(
                    out=dram_tile(x1sp, t),
                    in_=xbuf[:, par * 4096:(par + 1) * 4096].rearrange(
                        "p (k n) -> p k n", n=T))
                S[("x1sp", t)] = ct.inc(ins, "SP", 16)
                ct.w(sync, "s", "V", V[("ln2", t)])
                ins = sync.dma_start(
                    out=dram_tile(ln2sp, t),
                    in_=xbuf[:, par * 4096:(par + 1) * 4096].rearrange(
                        "p (k n) -> p k n", n=T))
                S[("ln2sp", t)] = ct.inc(ins, "SP", 16)

            for t in range(NT):
                par = t % 2

                def _(sync, t=t, par=par):
                    if t >= 2:
                        ct.w(sync, "s", "SP", S[("ln2sp", t - 2)])
                        ct.w(sync, "s", "V", V[("am", t - 2)])
                    else:
                        ct.w(sync, "s", "V", V[("xn", NT - 2 + t)])
                    ins = sync.dma_start(
                        out=xbuf[:, par * 4096:(par + 1) * 4096].rearrange(
                            "p (k n) -> p k n", n=T),
                        in_=dram_tile(xT, t))
                    S[("x2p", t)] = ct.inc(ins, "S", 16)
                    if t >= 1:
                        ct.w(sync, "s", "V", V[("qz", t - 1, 7)])
                    else:
                        ct.w(sync, "s", "SP", S[("phisp", NT - 1)])
                    ins = sync.dma_start(
                        out=phq[:, :].rearrange("p (j n) -> p j n", n=T),
                        in_=dram_tile(phisp, t))
                    S[("phi2", t)] = ct.inc(ins, "S", 16)
                    if t > 0:
                        ct.w(sync, "s", "PE", P[("i2b", t - 1)])
                    else:
                        ct.w(sync, "s", "PE", P[("gx", NT - 1)])
                    if t == 0:
                        ct.w(sync, "s", "R", S[("rowsd", NT - 1)])
                    ndma(sync, "s", "R", mtile[0:1, :], rowsD[0:1, T * t: T * (t + 1)])
                    S[("rows2", t)] = ndma(sync, "s", "R", gptile[0:1, :],
                                           rowsD[1:2, T * t: T * (t + 1)])
                    if t > 0:
                        _spill2(sync, t - 1, (t - 1) % 2)
                block.sync(_)

                def _(tn, t=t, par=par):
                    ct.w(tn, "t", "S", S[("phi2", t)])
                    ct.w(tn, "t", "GN", G["ksblk"])
                    if t > 0:
                        ct.w(tn, "t", "V", V[("z", t - 1)])
                        ct.w(tn, "t", "V", V[("f2", t - 1)])
                    for p in range(8):
                        ins = tn.matmul(den16[:, :], ksblk[:, 16 * p:16 * (p + 1)],
                                        phqk(p), start=(p == 0), stop=(p == 7))
                    P[("den", t)] = ct.inc(ins, "PE", 1)
                block.tensor(_)

                def _(v, t=t):
                    ct.w(v, "v", "PE", P[("den", t)])
                    v.tensor_scalar_add(tsub[0:16, :], den16[:, :], 1e-6)
                    v.reciprocal(tsub[0:16, :], tsub[0:16, :])
                    ins = v.tensor_copy(out=zrow_b[:, :], in_=tsub[0:16, :])
                    V[("z", t)] = ct.inc(ins, "V", 1)
                block.vector(_)

                for p in range(8):
                    def _(tn, t=t, p=p):
                        ct.w(tn, "t", "V", V[("z", t)])
                        if p < 2:
                            if t > 0:
                                ct.w(tn, "t", "V", V[("rows2", t - 1)])
                            ct.w(tn, "t", "GN", G["rows_c"])
                        else:
                            ct.w(tn, "t", "V", V[("qz", t, p - 2)])
                        ins = tn.matmul(zql(p), ind16[:, 128 * p:128 * (p + 1)],
                                        zrow_b[:, :], start=True, stop=True)
                        P[("zq", t, p)] = ct.inc(ins, "PE", 1)
                    block.tensor(_)

                    def _(v, t=t, p=p, par=par):
                        ct.w(v, "v", "PE", P[("zq", t, p)])
                        if p >= 2:
                            ct.w(v, "v", "PE", P[("out", t, p - 2)])
                        ins = v.tensor_mul(qz[:, T * (p % 2): T * (p % 2 + 1)],
                                           phqk(p), zql(p))
                        V[("qz", t, p)] = ct.inc(ins, "V", 1)
                    block.vector(_)

                    def _(tn, t=t, p=p):
                        ct.w(tn, "t", "V", V[("qz", t, p)])
                        ct.w(tn, "t", "V", V["kvevac"])
                        if p < 2 and t > 0:
                            ct.w(tn, "t", "V", V[("x1", t - 1, 6 + p)])
                        if p >= 2:
                            ct.w(tn, "t", "A", A[("outbf", t, p - 2)])
                        sl = qz[:, T * (p % 2): T * (p % 2 + 1)]
                        tn.matmul(outl(p)[0:64, :], kv_bf[0:64, 64 * p:64 * (p + 1)],
                                  sl[0:64, :], start=True, stop=True, tile_position=(0, 0))
                        ins = tn.matmul(outl(p)[64:128, :], kv_bf[64:128, 64 * p:64 * (p + 1)],
                                        sl[64:128, :], start=True, stop=True,
                                        tile_position=(64, 64))
                        P[("out", t, p)] = ct.inc(ins, "PE", 1)
                    block.tensor(_)

                    def _(a, t=t, p=p):
                        ct.w(a, "a", "PE", P[("out", t, p)])
                        ins = a.copy(out_bf[:, T * p: T * (p + 1)], outl(p))
                        A[("outbf", t, p)] = ct.inc(ins, "A", 1)
                    block.scalar(_)

                def _(tn, t=t):
                    ct.w(tn, "t", "R", S[("rows2", t)])
                    if t > 0:
                        ct.w(tn, "t", "A", A[("i2b", t - 1)])
                        ct.w(tn, "t", "V", V[("xn2", t - 1)])
                        ct.w(tn, "t", "V", V[("ln", t - 1)])
                    ins = tn.matmul(bps2, onrb[0:1, 0:128], mtile[0:1, :],
                                    start=True, stop=True)
                    P[("mb2", t)] = ct.inc(ins, "PE", 1)
                block.tensor(_)

                def _(a, t=t):
                    ct.w(a, "a", "PE", P[("mb2", t)])
                    ins = a.copy(bc0[:, :], bps2)
                    A[("mb2", t)] = ct.inc(ins, "A", 1)
                block.scalar(_)

                def _(tn, t=t):
                    ct.w(tn, "t", "A", A[("mb2", t)])
                    ins = tn.matmul(bps2, onrb[0:1, 0:128], gptile[0:1, :],
                                    start=True, stop=True)
                    P[("gpb", t)] = ct.inc(ins, "PE", 1)
                block.tensor(_)

                def _(a, t=t):
                    ct.w(a, "a", "PE", P[("gpb", t)])
                    ins = a.copy(bc1[:, :], bps2)
                    A[("gpb", t)] = ct.inc(ins, "A", 1)
                block.scalar(_)

                def _(v, t=t, par=par):
                    ct.w(v, "v", "A", A[("gpb", t)])
                    ct.w(v, "v", "S", S[("x2p", t)])
                    for k in range(KC):
                        v.tensor_sub(tsub[:, :], xk(par, k), bc0[:, :])
                        ins = v.tensor_mul(lnbuf[:, T * k: T * (k + 1)], tsub[:, :],
                                           bc1[:, :])
                    V[("ln", t)] = ct.inc(ins, "V", 1)
                block.vector(_)

                for j in range(KC):
                    def _(tn, t=t, j=j):
                        ct.w(tn, "t", "A", A[("outbf", t, 7)])
                        ct.w(tn, "t", "G", G["pw"])
                        ct.w(tn, "t", "GN", G["rows_c"])
                        if j == 0:
                            ct.w(tn, "t", "A", A[("gpb", t)])
                        if j >= 2:
                            ct.w(tn, "t", "V", V[("u", t, j - 2)])
                        for k in range(KC):
                            tn.matmul(projl(j),
                                      projw[:, C * k + 128 * j: C * k + 128 * (j + 1)],
                                      out_bf[:, T * k: T * (k + 1)],
                                      start=(k == 0), stop=False)
                        ins = tn.matmul(projl(j), prow[0:1, 128 * j:128 * (j + 1)],
                                        onr5[0:1, :], start=False, stop=True)
                        P[("proj", t, j)] = ct.inc(ins, "PE", 1)
                    block.tensor(_)

                    def _(v, t=t, j=j):
                        ct.w(v, "v", "PE", P[("proj", t, j)])
                        ins = v.scalar_tensor_tensor(
                            out=lnbuf[:, T * j: T * (j + 1)], in0=projl(j),
                            scalar=gmsb[:, j:j + 1], in1=lnbuf[:, T * j: T * (j + 1)],
                            op0=ALU.mult, op1=ALU.add)
                        V[("u", t, j)] = ct.inc(ins, "V", 1)
                    block.vector(_)

                for k in range(KC):
                    s = k % 2
                    def _(a, t=t, k=k, s=s):
                        ct.w(a, "a", "V", V[("u", t, k)])
                        if k >= 2:
                            ct.w(a, "a", "PE", P[("r2", t, k - 2)])
                        elif t > 0:
                            ct.w(a, "a", "PE", P[("st2", t - 1, 6 + k)])
                        ins = a.activation(out=stAB[:, T * s: T * (s + 1)],
                                           in_=lnbuf[:, T * k: T * (k + 1)],
                                           func=AF.Square)
                        A[("squ", t, k)] = ct.inc(ins, "A", 1)
                    block.scalar(_)

                    def _(tn, t=t, k=k, s=s):
                        ct.w(tn, "t", "A", A[("squ", t, k)])
                        if k == 0:
                            ct.w(tn, "t", "V", V[("z", t)])
                        ins = tn.matmul(r2row, onc[:, 0:1], stAB[:, T * s: T * (s + 1)],
                                        start=(k == 0), stop=(k == KC - 1))
                        P[("r2", t, k)] = ct.inc(ins, "PE", 1)
                    block.tensor(_)

                def _(a, t=t):
                    ct.w(a, "a", "PE", P[("r2", t, KC - 1)])
                    if t > 0:
                        ct.w(a, "a", "PE", P[("f2b", t - 1)])
                    a.activation(out=rowt1[0:1, :], in_=r2row, func=AF.Ln)
                    ins = a.activation(out=rowb[0:1, :], in_=rowt1[0:1, :],
                                       func=AF.Exp, scale=-0.5)
                    A[("f20", t)] = ct.inc(ins, "A", 1)
                block.scalar(_)

                def _(v, t=t):
                    ct.w(v, "v", "A", A[("f20", t)])
                    if t > 0:
                        ct.w(v, "v", "PE", P[("f2b", t - 1)])
                    ins = v.tensor_scalar_mul(rowb[0:1, :], rowb[0:1, :], SQC)
                    V[("f2", t)] = ct.inc(ins, "V", 1)
                block.vector(_)

                def _(tn, t=t):
                    ct.w(tn, "t", "V", V[("f2", t)])
                    ct.w(tn, "t", "V", V[("u", t, KC - 1)])
                    ins = tn.matmul(bps2, onrb[0:1, 0:128], rowb[0:1, :],
                                    start=True, stop=True)
                    P[("f2b", t)] = ct.inc(ins, "PE", 1)
                block.tensor(_)

                def _(a, t=t):
                    ct.w(a, "a", "PE", P[("f2b", t)])
                    ins = a.copy(bc0[:, :], bps2)
                    A[("f2b", t)] = ct.inc(ins, "A", 1)
                block.scalar(_)

                def _(v, t=t):
                    ct.w(v, "v", "A", A[("f2b", t)])
                    for k in range(KC):
                        ins = v.scalar_tensor_tensor(
                            out=xnk(k), in0=lnbuf[:, T * k: T * (k + 1)],
                            scalar=s2sb[:, k:k + 1], in1=bc0[:, :],
                            op0=ALU.mult, op1=ALU.mult)
                    V[("xn2", t)] = ct.inc(ins, "V", 1)
                block.vector(_)

                for h in range(HC):
                    def _(tn, t=t, h=h):
                        ct.w(tn, "t", "V", V[("xn2", t)])
                        ct.w(tn, "t", "G", G["conv"])
                        if h >= 2:
                            ct.w(tn, "t", "A", A[("hdd", t, h - 2)])
                        else:
                            ct.w(tn, "t", "A", A[("outbf", t, 6 + h)])
                        for k in range(KC):
                            ins = tn.matmul(
                                outl(h),
                                conv1w[:, HID * k + 128 * h: HID * k + 128 * (h + 1)],
                                xnk(k), start=(k == 0), stop=(k == KC - 1))
                        P[("c1", t, h)] = ct.inc(ins, "PE", 1)
                    block.tensor(_)

                    def _(a, t=t, h=h):
                        ct.w(a, "a", "PE", P[("c1", t, h)])
                        ins = a.activation(out=hdd[:, T * h: T * (h + 1)], in_=outl(h),
                                           func=AF.Gelu, bias=c1bsb[:, h:h + 1])
                        A[("hdd", t, h)] = ct.inc(ins, "A", 1)
                    block.scalar(_)

                if t >= 1:
                    def _(v, t=t):
                        _ln2amax(v, t - 1, (t - 1) % 2)
                    block.vector(_)

                for o in range(KC):
                    def _(tn, t=t, o=o):
                        ct.w(tn, "t", "A", A[("hdd", t, HC - 1)])
                        if o >= 2:
                            ct.w(tn, "t", "V", V[("x1", t, o - 2)])
                        for h in range(HC):
                            ins = tn.matmul(outl(o),
                                            conv2w[:, C * h + 128 * o: C * h + 128 * (o + 1)],
                                            hdd[:, T * h: T * (h + 1)],
                                            start=(h == 0), stop=(h == HC - 1))
                        P[("c2", t, o)] = ct.inc(ins, "PE", 1)
                    block.tensor(_)

                    def _(v, t=t, o=o, par=par):
                        ct.w(v, "v", "PE", P[("c2", t, o)])
                        ct.w(v, "v", "V", V["gc2b"])
                        v.scalar_tensor_tensor(
                            out=lnbuf[:, T * o: T * (o + 1)], in0=outl(o),
                            scalar=gmsb[:, o:o + 1], in1=lnbuf[:, T * o: T * (o + 1)],
                            op0=ALU.mult, op1=ALU.add)
                        ins = v.scalar_tensor_tensor(
                            out=xk(par, o),
                            in0=lnbuf[:, T * o: T * (o + 1)],
                            scalar=gc2b[:, o:o + 1], in1=xk(par, o),
                            op0=ALU.add, op1=ALU.add)
                        V[("x1", t, o)] = ct.inc(ins, "V", 1)
                    block.vector(_)

                for k in range(KC):
                    s = k % 2
                    def _(a, t=t, k=k, s=s):
                        ct.w(a, "a", "V", V[("x1", t, k)])
                        if k >= 2:
                            ct.w(a, "a", "PE", P[("st2", t, k - 2)])
                        else:
                            ct.w(a, "a", "PE", P[("r2", t, 6 + k)])
                        ins = a.activation(out=stAB[:, T * s: T * (s + 1)],
                                           in_=xk(t % 2, k),
                                           func=AF.Square)
                        A[("st2", t, k)] = ct.inc(ins, "A", 1)
                    block.scalar(_)

                    def _(tn, t=t, k=k, s=s):
                        ct.w(tn, "t", "A", A[("st2", t, k)])
                        ct.w(tn, "t", "V", V[("x1", t, k)])
                        if k == 0:
                            ct.w(tn, "t", "V", V[("qz", t, 7)])
                        tn.matmul(srow2[:, 0:T], onc[:, 0:1], stAB[:, T * s: T * (s + 1)],
                                  start=(k == 0), stop=(k == KC - 1))
                        ins = tn.matmul(srow2[:, T:2 * T], oncf[:, 0:1],
                                        xk(t % 2, k),
                                        start=(k == 0), stop=(k == KC - 1))
                        P[("st2", t, k)] = ct.inc(ins, "PE", 1)
                    block.tensor(_)

                def _(v, t=t):
                    ct.w(v, "v", "PE", P[("st2", t, KC - 1)])
                    v.tensor_scalar_mul(mtile[0:1, :], srow2[0:1, T:2 * T], 1.0 / C)
                    v.tensor_tensor(out=rowt0[0:1, :], in0=mtile[0:1, :],
                                    in1=mtile[0:1, :], op=ALU.mult)
                    v.scalar_tensor_tensor(out=rowt0[0:1, :], in0=srow2[0:1, 0:T],
                                            scalar=1.0 / C, in1=rowt0[0:1, :],
                                            op0=ALU.mult, op1=ALU.subtract)
                    ins = v.tensor_scalar_add(rowt0[0:1, :], rowt0[0:1, :], 1e-5)
                    V[("rows2", t)] = ct.inc(ins, "V", 1)
                block.vector(_)

                def _(a, t=t):
                    ct.w(a, "a", "V", V[("rows2", t)])
                    a.activation(out=rowt1[0:1, :], in_=rowt0[0:1, :], func=AF.Ln)
                    ins = a.activation(out=rowt1[0:1, :], in_=rowt1[0:1, :],
                                       func=AF.Exp, scale=-0.5)
                    A[("i2", t)] = ct.inc(ins, "A", 1)
                block.scalar(_)

                def _(tn, t=t):
                    ct.w(tn, "t", "V", V[("rows2", t)])
                    ct.w(tn, "t", "A", A[("f2b", t)])
                    if t > 0:
                        ct.w(tn, "t", "V", V[("ln2", t - 1)])
                    ins = tn.matmul(bps2, onrb[0:1, 0:128], mtile[0:1, :],
                                    start=True, stop=True)
                    P[("m2b", t)] = ct.inc(ins, "PE", 1)
                block.tensor(_)

                def _(a, t=t):
                    ct.w(a, "a", "PE", P[("m2b", t)])
                    ins = a.copy(bc2[:, :], bps2)
                    A[("m2b", t)] = ct.inc(ins, "A", 1)
                block.scalar(_)

                def _(tn, t=t):
                    ct.w(tn, "t", "A", A[("i2", t)])
                    ct.w(tn, "t", "A", A[("m2b", t)])
                    ins = tn.matmul(bps2, onr[0:1, 0:128], rowt1[0:1, :],
                                    start=True, stop=True)
                    P[("i2b", t)] = ct.inc(ins, "PE", 1)
                block.tensor(_)

                def _(a, t=t):
                    ct.w(a, "a", "PE", P[("i2b", t)])
                    ins = a.copy(bc3[:, :], bps2)
                    A[("i2b", t)] = ct.inc(ins, "A", 1)
                block.scalar(_)

                block.flush()

            def _(v):
                _ln2amax(v, NT - 1, (NT - 1) % 2)
            block.vector(_)

            def _(sync):
                _spill2(sync, NT - 1, (NT - 1) % 2)
            block.sync(_)
            block.flush()

        if stage < 3:
            def _(sync):
                ct.w(sync, "s", "S", ct.cnt["S"])
                ct.w(sync, "s", "V", V[("am", NT - 1)])
            block.sync(_)
            block.flush()
            return nc

        def _(g):
            ct.w(g, "g", "V", V[("am", NT - 1)])
            ins = g.dma_start(out=ccin.ap().rearrange("o p -> p o"), in_=amax_acc[:, 0:1],
                              allow_slow_non_contiguous=True)
            mk = ct.inc(ins, "G", 16)
            ct.w(g, "g", "G", mk)
            ins = g.collective_compute(
                "AllReduce", ALU.max, replica_groups=[list(range(8))],
                ins=[ccin.ap().opt()], outs=[ccout.ap().opt()])
            ct.inc(ins, "CC", 1)
            ct.w(g, "g", "CC", ct.cnt["CC"])
            G["crow"] = ndma(g, "g", "GN", crow[:, :], ccout.ap()[0:1, :])
        block.gpsimd(_)

        def _(v):
            ct.w(v, "v", "GN", G["crow"])
            def step(f):
                ct.w(v, "v", "V", ct.inc(f(), "V", 1))
            step(lambda: v.reduce_max(scr[0:1, 5:6], crow[:, :], axis=AXX))
            step(lambda: v.tensor_scalar_max(scr[0:1, 5:6], scr[0:1, 5:6], 1e-12))
            step(lambda: v.reciprocal(scr[0:1, 6:7], scr[0:1, 5:6]))
            step(lambda: v.tensor_scalar(out=scr[0:1, 6:7], in0=scr[0:1, 6:7],
                                         scalar1=448.0, scalar2=448.0,
                                         op0=ALU.mult, op1=ALU.min))
            step(lambda: v.tensor_scalar_mul(scr[0:1, 7:8], scr[0:1, 6:7], 0.5))
            step(lambda: v.reciprocal(scr[0:1, 5:6], scr[0:1, 6:7]))
            step(lambda: v.tensor_tensor(out=scr[0:1, 5:6], in0=scr[0:1, 5:6],
                                         in1=scr[0:1, 4:5], op=ALU.mult))
            step(lambda: v.tensor_scalar_mul(scr[0:1, 5:6], scr[0:1, 5:6], 4.0))
            V["xsc"] = ct.cnt["V"]
        block.vector(_)
        block.flush()

        with contextlib.ExitStack() as p3:
            sb3 = lambda n_, s_, d_: p3.enter_context(nc.sbuf_tensor(n_, s_, d_))
            ps3 = lambda n_, s_: p3.enter_context(nc.psum_tensor(n_, s_, F32))

            l2buf = sb3("l2buf", [128, 2 * 4096], F32)
            xq8 = sb3("xq8", [128, 2 * 4096], FP8)
            ffst = sb3("ffst", [128, 2 * T], BF16)
            ffps = ps3("ffps", [128, 1024])
            screp_ps = ps3("screp_ps", [128, 2])
            ffl = lambda o: ffps[:, T * (o % 2): T * (o % 2 + 1)]

            def _(tn):
                ct.w(tn, "t", "V", V["xsc"])
                tn.matmul(screp_ps[:, 0:1], onr[0:1, 0:128], scr[0:1, 7:8],
                          start=True, stop=False)
                ins = tn.matmul(screp_ps[:, 1:2], onr[0:1, 0:128], scr[0:1, 5:6],
                                start=False, stop=True)
                P["screp"] = ct.inc(ins, "PE", 1)
            block.tensor(_)

            def _(a):
                ct.w(a, "a", "PE", P["screp"])
                ins = a.copy(screp[:, :], screp_ps[:, :])
                A["screp"] = ct.inc(ins, "A", 1)
            block.scalar(_)
            block.flush()

            if taps:
                def _(sync):
                    ct.w(sync, "s", "V", V["xsc"])
                    ndma(sync, "s", "R", tap_sc[:, :], scr[:, :])
                block.sync(_)

                def _(g):
                    ct.w(g, "g", "V", V["kvevac"])
                    ins = g.dma_start(out=tap_kv[:, :], in_=kv_bf[:, :])
                    ct.inc(ins, "G", 16)
                block.gpsimd(_)

            def _ystore(sync, t, par):
                ct.w(sync, "s", "V", V[("y", t, KC - 1)])
                ins = sync.dma_start(
                    out=dram_tile(yT, t),
                    in_=xbuf[:, par * 4096:(par + 1) * 4096].rearrange(
                        "p (k n) -> p k n", n=T))
                S[("y", t)] = ct.inc(ins, "SP", 16)

            for t in range(NT):
                par = t % 2

                def _(sync, t=t, par=par):
                    if t >= 2:
                        ct.w(sync, "s", "SP", S[("y", t - 2)])
                    else:
                        ct.w(sync, "s", "SP", S[("ln2sp", NT - 1)])
                        ct.w(sync, "s", "SP", S[("x1sp", NT - 1)])
                        ct.w(sync, "s", "V", V[("ln2", NT - 1)])
                        ct.w(sync, "s", "PE", P[("st2", NT - 1, KC - 1)])
                        ct.w(sync, "s", "V", V[("am", NT - 2 + t)])
                    ins = sync.dma_start(
                        out=l2buf[:, par * 4096:(par + 1) * 4096].rearrange(
                            "p (k n) -> p k n", n=T),
                        in_=dram_tile(ln2sp, t))
                    S[("l2", t)] = ct.inc(ins, "S", 16)
                    ins = sync.dma_start(
                        out=xbuf[:, par * 4096:(par + 1) * 4096].rearrange(
                            "p (k n) -> p k n", n=T),
                        in_=dram_tile(x1sp, t))
                    S[("x3", t)] = ct.inc(ins, "S", 16)
                    if t > 0:
                        _ystore(sync, t - 1, (t - 1) % 2)
                block.sync(_)

                def _(v, t=t, par=par):
                    ct.w(v, "v", "S", S[("l2", t)])
                    ct.w(v, "v", "A", A["screp"])
                    if t >= 2:
                        ct.w(v, "v", "PE", P[("ffn", t - 2, KC - 1)])
                    for k in range(KC):
                        ins = v.tensor_scalar(
                            out=xq8[:, par * 4096 + T * k: par * 4096 + T * (k + 1)],
                            in0=l2buf[:, par * 4096 + T * k: par * 4096 + T * (k + 1)],
                            scalar1=screp[:, 0:1], scalar2=None, op0=ALU.mult)
                    V[("q8", t)] = ct.inc(ins, "V", 1)
                block.vector(_)

                for o in range(KC):
                    def _(tn, t=t, o=o, par=par):
                        ct.w(tn, "t", "V", V[("q8", t)])
                        ct.w(tn, "t", "V", V[("wq8h", 1)])
                        if o >= 2:
                            ct.w(tn, "t", "A", A[("ffn", t, o - 2)])
                        elif t > 0:
                            ct.w(tn, "t", "A", A[("ffn", t - 1, 6 + o)])
                        for k in range(KC):
                            ins = tn.matmul(
                                ffl(o), wq8[:, C * k + 128 * o: C * k + 128 * (o + 1)],
                                xq8[:, par * 4096 + T * k: par * 4096 + T * (k + 1)],
                                start=(k == 0), stop=(k == KC - 1))
                        P[("ffn", t, o)] = ct.inc(ins, "PE", 1)
                    block.tensor(_)

                    def _(a, t=t, o=o):
                        ct.w(a, "a", "PE", P[("ffn", t, o)])
                        if o >= 2:
                            ct.w(a, "a", "V", V[("y", t, o - 2)])
                        elif t > 0:
                            ct.w(a, "a", "V", V[("y", t - 1, 6 + o)])
                        ins = a.activation(out=ffst[:, T * (o % 2): T * (o % 2 + 1)],
                                           in_=ffl(o), func=AF.Identity,
                                           bias=fbsb[:, o:o + 1], scale=screp[:, 1:2])
                        A[("ffn", t, o)] = ct.inc(ins, "A", 1)
                    block.scalar(_)

                    def _(v, t=t, o=o, par=par):
                        ct.w(v, "v", "A", A[("ffn", t, o)])
                        ct.w(v, "v", "S", S[("x3", t)])
                        ins = v.tensor_add(xk(par, o), ffst[:, T * (o % 2): T * (o % 2 + 1)],
                                           xk(par, o))
                        V[("y", t, o)] = ct.inc(ins, "V", 1)
                    block.vector(_)

                block.flush()

            def _(sync):
                _ystore(sync, NT - 1, (NT - 1) % 2)
                ct.w(sync, "s", "S", ct.cnt["S"])
                ct.w(sync, "s", "SP", ct.cnt["SP"])
            block.sync(_)
            block.flush()

    return nc


def kernel(L0_attn_x, L0_attn_scale1, L0_attn_scale2, L0_attn_gamma, L0_attn_qkv_w,
           L0_attn_proj_w, L0_attn_proj_b, L0_attn_conv1_w, L0_attn_conv1_b,
           L0_attn_conv2_w, L0_attn_conv2_b, L0_ffn_weight, L0_ffn_bias):
    x = np.asarray(L0_attn_x, dtype=np.float32)
    assert x.shape == (8, N, C)

    shared = {
        "qkvwT": np.ascontiguousarray(np.asarray(L0_attn_qkv_w, np.float32).T),
        "projwT": np.ascontiguousarray(np.asarray(L0_attn_proj_w, np.float32).T),
        "conv1wT": np.ascontiguousarray(np.asarray(L0_attn_conv1_w, np.float32).T),
        "conv2wT": np.ascontiguousarray(np.asarray(L0_attn_conv2_w, np.float32).T),
        "ffnwT": np.ascontiguousarray(np.asarray(L0_ffn_weight, np.float32).T),
        "scale1": np.ascontiguousarray(np.asarray(L0_attn_scale1, np.float32).reshape(KC, 128).T),
        "scale2": np.ascontiguousarray(np.asarray(L0_attn_scale2, np.float32).reshape(KC, 128).T),
        "gamma": np.ascontiguousarray(np.asarray(L0_attn_gamma, np.float32).reshape(KC, 128).T),
        "projb": np.ascontiguousarray(np.asarray(L0_attn_proj_b, np.float32)),
        "conv1b": np.ascontiguousarray(np.asarray(L0_attn_conv1_b, np.float32).reshape(HC, 128).T),
        "conv2b": np.ascontiguousarray(np.asarray(L0_attn_conv2_b, np.float32).reshape(KC, 128).T),
        "ffnb": np.ascontiguousarray(np.asarray(L0_ffn_bias, np.float32).reshape(KC, 128).T),
    }
    in_maps = []
    for i in range(8):
        m = dict(shared)
        m["xT"] = np.ascontiguousarray(x[i].T)
        in_maps.append(m)

    nc1 = build_nc(stage=3, taps=False)
    ph1 = {k: dict(v) for k, v in nc1._marks_out.items()}
    del nc1
    nc = build_nc(stage=3, taps=False, prev=ph1)
    for k, v in nc._marks_out.items():
        assert dict(v) == ph1[k], f"phase mismatch in {k}"
    res = run_bass_kernel_spmd(nc, in_maps, core_ids=list(range(8)))
    out = np.stack([np.asarray(r["yT"]).T for r in res.results], axis=0)
    return np.ascontiguousarray(out.astype(np.float32))


# version 14
# speedup vs baseline: 1.0434x; 1.0080x over previous
import contextlib

import ml_dtypes
import numpy as np

import concourse.bass as bass
import concourse.mybir as mybir
from concourse.bass_utils import run_bass_kernel_spmd

F32 = mybir.dt.float32
BF16 = mybir.dt.bfloat16
FP8 = mybir.dt.float8e4
AF = mybir.ActivationFunctionType
ALU = mybir.AluOpType
AXX = mybir.AxisListType.X

N = 4096
C = 1024
HID = 2048
T = 512
NT = N // T
KC = C // 128
HC = HID // 128
NB = T // 128
SQC = 32.000001


class Marks(dict):

    def __init__(self, prev=None):
        super().__init__()
        self.prev = prev

    def __getitem__(self, k):
        if dict.__contains__(self, k):
            return dict.__getitem__(self, k)
        if self.prev is not None:
            return self.prev.get(k)
        return None


class Coll:

    ORDER = ("sync", "gpsimd", "tensor", "scalar", "vector")

    def __init__(self, real):
        self.real = real
        self.q = {k: [] for k in self.ORDER}

    def _add(self, name, f):
        self.q[name].append(f)

    def sync(self, f):
        self._add("sync", f)

    def gpsimd(self, f):
        self._add("gpsimd", f)

    def tensor(self, f):
        self._add("tensor", f)

    def scalar(self, f):
        self._add("scalar", f)

    def vector(self, f):
        self._add("vector", f)

    def flush(self):
        for k in self.ORDER:
            fs, self.q[k] = self.q[k], []
            if fs:
                getattr(self.real, k)(lambda e, fs=fs: [f(e) for f in fs])


class Ctr:
    def __init__(self, sems):
        self.sems = sems
        self.cnt = {k: 0 for k in sems}
        self.water = {}

    def inc(self, ins, sem, k=1):
        self.cnt[sem] += k
        ins.then_inc(self.sems[sem], k)
        return self.cnt[sem]

    def w(self, eng, engname, sem, val):
        if val is None or val <= 0:
            return
        key = (engname, sem)
        if self.water.get(key, 0) >= val:
            return
        eng.wait_ge(self.sems[sem], val)
        self.water[key] = val


def build_nc(stage=3, taps=False, prev=None):
    nc = bass.Bass("TRN2", target_bir_lowering=False, detect_race_conditions=False)

    xT = nc.dram_tensor("xT", [C, N], F32, kind="ExternalInput").ap()
    qkvwT = nc.dram_tensor("qkvwT", [C, 3 * C], F32, kind="ExternalInput").ap()
    projwT = nc.dram_tensor("projwT", [C, C], F32, kind="ExternalInput").ap()
    conv1wT = nc.dram_tensor("conv1wT", [C, HID], F32, kind="ExternalInput").ap()
    conv2wT = nc.dram_tensor("conv2wT", [HID, C], F32, kind="ExternalInput").ap()
    ffnwT = nc.dram_tensor("ffnwT", [C, C], F32, kind="ExternalInput").ap()
    scale1_d = nc.dram_tensor("scale1", [128, KC], F32, kind="ExternalInput").ap()
    scale2_d = nc.dram_tensor("scale2", [128, KC], F32, kind="ExternalInput").ap()
    gamma_d = nc.dram_tensor("gamma", [128, KC], F32, kind="ExternalInput").ap()
    projb_d = nc.dram_tensor("projb", [C], F32, kind="ExternalInput").ap()
    conv1b_d = nc.dram_tensor("conv1b", [128, HC], F32, kind="ExternalInput").ap()
    conv2b_d = nc.dram_tensor("conv2b", [128, KC], F32, kind="ExternalInput").ap()
    ffnb_d = nc.dram_tensor("ffnb", [128, KC], F32, kind="ExternalInput").ap()

    yT = nc.dram_tensor("yT", [C, N], F32, kind="ExternalOutput").ap()

    spk = "ExternalOutput" if taps else "Internal"
    phisp = nc.dram_tensor("phisp", [C, N], BF16, kind=spk).ap()
    x1sp = nc.dram_tensor("x1sp", [C, N], F32, kind=spk).ap()
    ln2sp = nc.dram_tensor("ln2sp", [C, N], F32, kind=spk).ap()
    ksum_d = nc.dram_tensor("ksum_d", [C], F32, kind=spk).ap()
    rowsD = nc.dram_tensor("rowsD", [2, N], BF16, kind="Internal").ap()
    wam_d = nc.dram_tensor("wam_d", [128], F32, kind="Internal").ap()
    ccin = nc.dram_tensor("ccin", [1, 128], F32, kind="Internal")
    ccout = nc.dram_tensor("ccout", [1, 128], F32, kind="Internal")
    if taps:
        tap_sc = nc.dram_tensor("tap_sc", [1, 8], F32, kind="ExternalOutput").ap()
        tap_kv = nc.dram_tensor("tap_kv", [128, 512], F32, kind="ExternalOutput").ap()

    ones_col = nc.inline_tensor(np.ones((128, 1), dtype=ml_dtypes.bfloat16), "c_onescol").ap()
    onesrow_f = nc.inline_tensor(np.ones((1, 128), dtype=np.float32), "c_onesrow").ap()
    onesrow_b = nc.inline_tensor(np.ones((1, 128), dtype=ml_dtypes.bfloat16), "c_onesrowb").ap()
    ones_colf = nc.inline_tensor(np.ones((128, 1), dtype=np.float32), "c_onescolf").ap()
    ones512 = nc.inline_tensor(np.ones((1, T), dtype=ml_dtypes.bfloat16), "c_ones512").ap()
    ind = np.zeros((16, 8 * 128), dtype=np.float32)
    for p in range(8):
        ind[2 * p, 128 * p:128 * p + 64] = 1.0
        ind[2 * p + 1, 128 * p + 64:128 * p + 128] = 1.0
    ind16_d = nc.inline_tensor(ind.astype(ml_dtypes.bfloat16), "c_ind16").ap()

    with contextlib.ExitStack() as st:
        sb = lambda n_, s_, d_: st.enter_context(nc.sbuf_tensor(n_, s_, d_))
        sems = {k: st.enter_context(nc.semaphore(k)) for k in
                ["S", "G", "R", "GN", "PE", "V", "A", "CC", "W", "SP"]}
        ct = Ctr(sems)
        blk_real = st.enter_context(nc.Block())
        block = Coll(blk_real)

        def ndma(eng, engname, sem, out, in_, **kw):
            ct.w(eng, engname, sem, ct.cnt[sem])
            ins = eng.dma_start(out=out, in_=in_, **kw)
            return ct.inc(ins, sem, 16)

        xbuf = sb("xbuf", [128, 2 * 4096], F32)
        phq = sb("phq", [128, 4096], BF16)
        xn_bf = sb("xn_bf", [128, 4096], BF16)
        stAB = sb("stAB", [128, 2 * T], BF16)
        projw = sb("projw", [128, KC * C], BF16)
        wq8 = sb("wq8", [128, KC * C], FP8)
        kv_bf = sb("kv_bf", [128, 512], BF16)
        ksblk = sb("ksblk", [128, 128], BF16)
        mtile = sb("mtile", [1, T], BF16)
        gptile = sb("gptile", [1, T], BF16)
        rowb = sb("rowb", [1, T], BF16)
        rowt0 = sb("rowt0", [1, T], F32)
        rowt1 = sb("rowt1", [1, T], F32)
        bc0 = sb("bc0", [128, T], F32)
        bc1 = sb("bc1", [128, T], F32)
        tsub = sb("tsub", [128, T], F32)
        onc = sb("onc", [128, 1], BF16)
        oncf = sb("oncf", [128, 1], F32)
        onr = sb("onr", [1, 128], F32)
        onrb = sb("onrb", [1, 128], BF16)
        onr5 = sb("onr5", [1, T], BF16)
        ind16 = sb("ind16", [16, 8 * 128], BF16)
        s1sb = sb("s1sb", [128, KC], F32)
        s2sb = sb("s2sb", [128, KC], F32)
        gmsb = sb("gmsb", [128, KC], F32)
        c1bsb = sb("c1bsb", [128, HC], F32)
        fbsb = sb("fbsb", [128, KC], F32)
        prow = sb("prow", [1, C], BF16)
        c2bsb = sb("c2bsb", [128, KC], F32)
        gc2b = sb("gc2b", [128, KC], F32)
        scr = sb("scr", [1, 8], F32)
        screp = sb("screp", [128, 2], F32)
        swhrep = sb("swhrep", [128, 1], F32)
        wamax = sb("wamax", [128, 1], F32)
        wgtmp = sb("wgtmp", [128, 1], F32)
        amax_acc = sb("amax_acc", [128, 1], F32)
        ggtmp2 = sb("ggtmp2", [128, 1], F32)
        crow = sb("crow", [1, 128], F32)

        xk = lambda par, k: xbuf[:, par * 4096 + T * k: par * 4096 + T * (k + 1)]
        phqk = lambda j: phq[:, T * j: T * (j + 1)]
        xnk = lambda k: xn_bf[:, T * k: T * (k + 1)]
        stAs = lambda s: stAB[:, 0:T]
        stBs = lambda s: stAB[:, T:2 * T]

        def dram_tile(dram_ap, t):
            return dram_ap.rearrange("(k p) n -> p k n", p=128)[:, :, T * t: T * (t + 1)]

        pv = prev or {}
        V = Marks(pv.get("V"))
        A = Marks(pv.get("A"))
        P = Marks(pv.get("P"))
        G = Marks(pv.get("G"))
        GC = Marks(pv.get("GC"))
        S = Marks(pv.get("S"))
        marks = {"V": V, "A": A, "P": P, "G": G, "GC": GC, "S": S}
        nc._marks_out = marks

        def _(sync):
            ct.inc(sync.dma_start(out=onc[:, :], in_=ones_col[:, :]), "S", 16)
            ct.inc(sync.dma_start(out=oncf[:, :], in_=ones_colf[:, :]), "S", 16)
            ndma(sync, "s", "R", onr[:, :], onesrow_f[:, :])
            S["onr"] = ndma(sync, "s", "R", onrb[:, :], onesrow_b[:, :])
            ct.inc(sync.dma_start(out=s1sb[:, :], in_=scale1_d[:, :]), "S", 16)
            ct.inc(sync.dma_start(out=s2sb[:, :], in_=scale2_d[:, :]), "S", 16)
            ct.inc(sync.dma_start(out=gmsb[:, :], in_=gamma_d[:, :]), "S", 16)
            ct.inc(sync.dma_start(out=c1bsb[:, :], in_=conv1b_d[:, :]), "S", 16)
            ins = sync.dma_start(out=fbsb[:, :], in_=ffnb_d[:, :])
            S["consts"] = ct.inc(ins, "S", 16)
            ins = sync.dma_start(out=xbuf[:, 0:4096].rearrange("p (k n) -> p k n", n=T),
                                 in_=dram_tile(xT, 0))
            S[("x1p", 0)] = ct.inc(ins, "S", 16)
        block.sync(_)

        def _(g):
            ndma(g, "g", "GN", ind16[:, :], ind16_d[:, :])
            for k in range(KC):
                ct.inc(g.dma_start(out=projw[:, C * k: C * (k + 1)],
                                   in_=projwT[128 * k:128 * (k + 1), :]), "G", 16)
            G["pw"] = ct.cnt["G"]
            ndma(g, "g", "GN", prow[:, :], projb_d.rearrange("(o n) -> o n", o=1))
            G["rows_c"] = ndma(g, "g", "GN", onr5[:, :], ones512[:, :])
        block.gpsimd(_)

        def _(sync):
            ins = sync.dma_start(out=c2bsb[:, :], in_=conv2b_d[:, :])
            S["c2b"] = ct.inc(ins, "S", 16)
        block.sync(_)

        def _(v):
            ct.w(v, "v", "S", S["c2b"])
            ins = v.tensor_tensor(out=gc2b[:, :], in0=gmsb[:, :], in1=c2bsb[:, :],
                                  op=ALU.mult)
            V["gc2b"] = ct.inc(ins, "V", 1)
        block.vector(_)
        block.flush()

        with contextlib.ExitStack() as p1:
            sb1 = lambda n_, s_, d_: p1.enter_context(nc.sbuf_tensor(n_, s_, d_))
            ps1 = lambda n_, s_: p1.enter_context(nc.psum_tensor(n_, s_, F32))

            qkvw = sb1("qkvw", [128, KC * 3 * C], BF16)
            xn1b = sb1("xn1b", [128, 4096], BF16)
            wstage = sb1("wstage", [128, 4 * C], F32)
            vtile = sb1("vtile", [1, T], F32)
            ksrow = sb1("ksrow", [1, C], F32)
            rowt2 = sb1("rowt2", [1, T], F32)
            wrow = sb1("wrow", [1, 128], F32)
            st1 = sb1("st1", [128, 4 * T], BF16)
            st1A = lambda s: st1[:, 2 * s * T: (2 * s + 1) * T]
            st1B = lambda s: st1[:, (2 * s + 1) * T: (2 * s + 2) * T]
            ktmp = sb1("ktmp", [128, 1024], F32)
            phik = sb1("phik", [128, 1024], BF16)
            v_bf = sb1("v_bf", [128, 1024], BF16)
            phitmp = sb1("phitmp", [128, T], F32)

            kvacc = ps1("kvacc", [128, 512])
            ksum_ps = ps1("ksum_ps", [1, 1024])
            kvtok = ps1("kvtok", [128, 1024])
            qps = ps1("qps", [128, 1024])
            qpsl = lambda j: qps[:, T * (j % 2): T * (j % 2 + 1)]
            bps = kvtok[:, 0:T]
            srow = kvtok[0:1, :]
            xn1 = lambda par, k: (xn_bf if par == 0 else xn1b)[:, T * k: T * (k + 1)]

            def _(g):
                for k in range(KC):
                    ins = g.dma_start(out=qkvw[:, 3 * C * k: 3 * C * (k + 1)],
                                      in_=qkvwT[128 * k:128 * (k + 1), :])
                    ct.inc(ins, "G", 16)
                G["qkv"] = ct.cnt["G"]
            block.gpsimd(_)

            for half in range(2):
                def _(sync, half=half):
                    if half == 1:
                        ct.w(sync, "s", "V", V[("wmaxh", 0)])
                    ins = sync.dma_start(
                        out=wstage[:, :].rearrange("p (k n) -> p k n", n=C),
                        in_=ffnwT.rearrange("(k p) n -> p k n", p=128)[:, 4 * half:4 * half + 4, :])
                    S[("ffnw", half)] = ct.inc(ins, "W", 16)
                block.sync(_)

                def _(v, half=half):
                    ct.w(v, "v", "W", S[("ffnw", half)])
                    for k in range(4):
                        ct.w(v, "v", "V",
                             ct.inc(v.reduce_max(wgtmp[:, 0:1],
                                                 wstage[:, C * k: C * (k + 1)],
                                                 axis=AXX, apply_absolute_value=True),
                                    "V", 1))
                        if half == 0 and k == 0:
                            ins = v.tensor_copy(out=wamax[:, 0:1], in_=wgtmp[:, 0:1])
                        else:
                            ins = v.tensor_tensor(out=wamax[:, 0:1], in0=wamax[:, 0:1],
                                                  in1=wgtmp[:, 0:1], op=ALU.max)
                        ct.w(v, "v", "V", ct.inc(ins, "V", 1))
                    V[("wmaxh", half)] = ct.cnt["V"]
                block.vector(_)

            def _(sync):
                ct.w(sync, "s", "V", V[("wmaxh", 1)])
                ndma(sync, "s", "R", wam_d.rearrange("(p o) -> p o", o=1), wamax[:, 0:1])
                S["wrow"] = ndma(sync, "s", "R", wrow[:, :],
                                 wam_d.rearrange("(o p) -> o p", o=1))
            block.sync(_)

            def _(v):
                ct.w(v, "v", "R", S["wrow"])
                def step(f):
                    ct.w(v, "v", "V", ct.inc(f(), "V", 1))
                step(lambda: v.reduce_max(scr[0:1, 0:1], wrow[:, :], axis=AXX))
                step(lambda: v.tensor_scalar_max(scr[0:1, 0:1], scr[0:1, 0:1], 1e-12))
                step(lambda: v.reciprocal(scr[0:1, 1:2], scr[0:1, 0:1]))
                step(lambda: v.tensor_scalar(out=scr[0:1, 2:3], in0=scr[0:1, 1:2],
                                             scalar1=448.0, scalar2=448.0,
                                             op0=ALU.mult, op1=ALU.min))
                step(lambda: v.tensor_scalar_mul(scr[0:1, 3:4], scr[0:1, 2:3], 0.5))
                step(lambda: v.reciprocal(scr[0:1, 4:5], scr[0:1, 2:3]))
                V["wsc"] = ct.cnt["V"]
            block.vector(_)

            def _(tn):
                ct.w(tn, "t", "V", V["wsc"])
                ct.w(tn, "t", "S", S["consts"])
                ct.w(tn, "t", "R", S["onr"])
                ins = tn.matmul(kvacc[:, 0:1], onr[0:1, 0:128], scr[0:1, 3:4],
                                start=True, stop=True)
                P["swhrep"] = ct.inc(ins, "PE", 1)
            block.tensor(_)

            def _(a):
                ct.w(a, "a", "PE", P["swhrep"])
                ins = a.copy(swhrep[:, 0:1], kvacc[:, 0:1])
                A["swhrep"] = ct.inc(ins, "A", 1)
            block.scalar(_)

            for half in range(2):
                def _(sync, half=half):
                    ct.w(sync, "s", "V", V[("wmaxh", 1)] if half == 0 else V[("wq8h", 0)])
                    ins = sync.dma_start(
                        out=wstage[:, :].rearrange("p (k n) -> p k n", n=C),
                        in_=ffnwT.rearrange("(k p) n -> p k n", p=128)[:, 4 * half:4 * half + 4, :])
                    S[("ffnw2", half)] = ct.inc(ins, "W", 16)
                block.sync(_)

                def _(v, half=half):
                    ct.w(v, "v", "W", S[("ffnw2", half)])
                    ct.w(v, "v", "A", A["swhrep"])
                    for k in range(4):
                        ins = v.tensor_scalar(out=wq8[:, C * (4 * half + k): C * (4 * half + k + 1)],
                                              in0=wstage[:, C * k: C * (k + 1)],
                                              scalar1=swhrep[:, 0:1], scalar2=None,
                                              op0=ALU.mult)
                    V[("wq8h", half)] = ct.inc(ins, "V", 1)
                block.vector(_)
            block.flush()

            def _spill1(sync, t):
                ct.w(sync, "s", "A", A[("phi", t, KC - 1)])
                ins = sync.dma_start(
                    out=dram_tile(phisp, t),
                    in_=phq[:, :].rearrange("p (j n) -> p j n", n=T))
                S[("phisp", t)] = ct.inc(ins, "SP", 16)

            for t in range(NT):
                par = t % 2
                if t > 0:
                    def _(sync, t=t, par=par):
                        if t >= 2:
                            ct.w(sync, "s", "V", V[("xn", t - 2)])
                        ins = sync.dma_start(
                            out=xbuf[:, par * 4096:(par + 1) * 4096].rearrange(
                                "p (k n) -> p k n", n=T),
                            in_=dram_tile(xT, t))
                        S[("x1p", t)] = ct.inc(ins, "S", 16)
                        _spill1(sync, t - 1)
                    block.sync(_)

                for k in range(KC):
                    s = k % 2
                    def _(v, t=t, par=par, k=k, s=s):
                        ct.w(v, "v", "S", S[("x1p", t)])
                        if k >= 2:
                            ct.w(v, "v", "PE", P[("st", t, k - 2)])
                        elif t > 0:
                            ct.w(v, "v", "PE", P[("st", t - 1, 6 + k)])
                        v.tensor_copy(out=st1A(s), in_=xk(par, k))
                        ins = v.tensor_tensor(out=st1B(s), in0=st1A(s), in1=st1A(s),
                                              op=ALU.mult)
                        V[("st", t, k)] = ct.inc(ins, "V", 1)
                    block.vector(_)

                    def _(tn, t=t, k=k, s=s):
                        ct.w(tn, "t", "V", V[("st", t, k)])
                        if k == 0:
                            ct.w(tn, "t", "S", S["consts"])
                            if t > 0:
                                ct.w(tn, "t", "A", A[("vbf", t - 1, NB - 1)])
                                ct.w(tn, "t", "V", V[("rows_a", t - 1)])
                        tn.matmul(srow[:, 0:T], onc[:, 0:1], st1B(s),
                                  start=(k == 0), stop=(k == KC - 1))
                        ins = tn.matmul(srow[:, T:2 * T], onc[:, 0:1], st1A(s),
                                        start=(k == 0), stop=(k == KC - 1))
                        P[("st", t, k)] = ct.inc(ins, "PE", 1)
                    block.tensor(_)

                def _(v, t=t):
                    ct.w(v, "v", "PE", P[("st", t, KC - 1)])
                    if t > 0:
                        ct.w(v, "v", "R", S[("rowsd", t - 1)])
                        ct.w(v, "v", "A", A[("gp1", t - 1)])
                    v.tensor_scalar_mul(mtile[0:1, :], srow[0:1, T:2 * T], 1.0 / C)
                    v.tensor_tensor(out=rowt0[0:1, :], in0=mtile[0:1, :],
                                    in1=mtile[0:1, :], op=ALU.mult)
                    v.scalar_tensor_tensor(
                        out=vtile[0:1, :], in0=srow[0:1, 0:T],
                        scalar=1.0 / C, in1=rowt0[0:1, :], op0=ALU.mult, op1=ALU.subtract)
                    ins = v.tensor_scalar_add(rowt0[0:1, :], vtile[0:1, :], 1e-5)
                    V[("rows_a", t)] = ct.inc(ins, "V", 1)
                block.vector(_)

                def _(a, t=t):
                    ct.w(a, "a", "V", V[("rows_a", t)])
                    if t > 0:
                        ct.w(a, "a", "R", S[("rowsd", t - 1)])
                    a.activation(out=rowt1[0:1, :], in_=vtile[0:1, :],
                                 func=AF.Ln, scale=float(C))
                    ins = a.activation(out=rowt2[0:1, :], in_=rowt1[0:1, :],
                                       func=AF.Exp, scale=-0.5)
                    A[("gx0", t)] = ct.inc(ins, "A", 1)
                    a.activation(out=rowt1[0:1, :], in_=rowt0[0:1, :], func=AF.Ln)
                    ins = a.activation(out=gptile[0:1, :], in_=rowt1[0:1, :],
                                       func=AF.Exp, scale=-0.5)
                    A[("gp1", t)] = ct.inc(ins, "A", 1)
                block.scalar(_)

                def _(sync, t=t):
                    ct.w(sync, "s", "A", A[("gp1", t)])
                    ndma(sync, "s", "R", rowsD[0:1, T * t: T * (t + 1)], mtile[0:1, :])
                    S[("rowsd", t)] = ndma(sync, "s", "R",
                                           rowsD[1:2, T * t: T * (t + 1)], gptile[0:1, :])
                block.sync(_)

                def _(v, t=t):
                    ct.w(v, "v", "A", A[("gx0", t)])
                    if t > 0:
                        ct.w(v, "v", "PE", P[("gx", t - 1)])
                    ins = v.tensor_scalar_mul(rowb[0:1, :], rowt2[0:1, :], SQC)
                    V[("rows_b", t)] = ct.inc(ins, "V", 1)
                block.vector(_)

                def _(tn, t=t):
                    ct.w(tn, "t", "V", V[("rows_a", t)])
                    ct.w(tn, "t", "R", S["onr"])
                    ins = tn.matmul(bps, onrb[0:1, 0:128], mtile[0:1, :],
                                    start=True, stop=True)
                    P[("mb", t)] = ct.inc(ins, "PE", 1)
                block.tensor(_)

                def _(a, t=t):
                    ct.w(a, "a", "PE", P[("mb", t)])
                    ins = a.copy(bc0[:, :], bps)
                    A[("mb", t)] = ct.inc(ins, "A", 1)
                block.scalar(_)

                def _(tn, t=t):
                    ct.w(tn, "t", "V", V[("rows_b", t)])
                    ct.w(tn, "t", "A", A[("mb", t)])
                    ins = tn.matmul(bps, onrb[0:1, 0:128], rowb[0:1, :],
                                    start=True, stop=True)
                    P[("gx", t)] = ct.inc(ins, "PE", 1)
                block.tensor(_)

                def _(a, t=t):
                    ct.w(a, "a", "PE", P[("gx", t)])
                    ins = a.copy(bc1[:, :], bps)
                    A[("gx", t)] = ct.inc(ins, "A", 1)
                block.scalar(_)

                def _(v, t=t, par=par):
                    ct.w(v, "v", "A", A[("gx", t)])
                    if t >= 2:
                        ct.w(v, "v", "PE", P[("q", t - 2, KC - 1)])
                    for k in range(KC):
                        v.tensor_sub(tsub[:, :], xk(par, k), bc0[:, :])
                        ins = v.scalar_tensor_tensor(out=xn1(par, k), in0=tsub[:, :],
                                                     scalar=s1sb[:, k:k + 1], in1=bc1[:, :],
                                                     op0=ALU.mult, op1=ALU.mult)
                    V[("xn", t)] = ct.inc(ins, "V", 1)
                block.vector(_)

                for b in range(NB):
                    def _(tn, t=t, b=b, par=par):
                        ct.w(tn, "t", "V", V[("xn", t)])
                        if b == 0:
                            ct.w(tn, "t", "A", A[("gx", t)])
                        else:
                            ct.w(tn, "t", "A", A[("vbf", t, b - 1)])
                        for k in range(KC):
                            lhs = xn1(par, 0)[0:0, 0:0]
                            lhs = (xn_bf if par == 0 else xn1b)[:, T * k + 128 * b: T * k + 128 * (b + 1)]
                            tn.matmul(kvtok[:, 0:512], lhs,
                                      qkvw[:, 3 * C * k + C: 3 * C * k + C + 512],
                                      start=(k == 0), stop=(k == KC - 1))
                            ins = tn.matmul(kvtok[:, 512:1024], lhs,
                                            qkvw[:, 3 * C * k + C + 512: 3 * C * k + 2 * C],
                                            start=(k == 0), stop=(k == KC - 1))
                        P[("ktok", t, b)] = ct.inc(ins, "PE", 1)
                    block.tensor(_)

                    def _(a, t=t, b=b):
                        ct.w(a, "a", "PE", P[("ktok", t, b)])
                        ins = a.activation(out=ktmp[:, :], in_=kvtok[:, :], func=AF.Square)
                        A[("ksq", t, b)] = ct.inc(ins, "A", 1)
                        if (t, b) != (0, 0):
                            pt, pb = (t, b - 1) if b > 0 else (t - 1, NB - 1)
                            ct.w(a, "a", "PE", P[("kv", pt, pb)])
                        ins = a.activation(out=phik[:, :], in_=ktmp[:, :],
                                           func=AF.Exp, scale=-0.5)
                        A[("phik", t, b)] = ct.inc(ins, "A", 1)
                    block.scalar(_)

                    def _(tn, t=t, b=b, par=par):
                        ct.w(tn, "t", "A", A[("ksq", t, b)])
                        for k in range(KC):
                            lhs = (xn_bf if par == 0 else xn1b)[:, T * k + 128 * b: T * k + 128 * (b + 1)]
                            tn.matmul(kvtok[:, 0:512], lhs,
                                      qkvw[:, 3 * C * k + 2 * C: 3 * C * k + 2 * C + 512],
                                      start=(k == 0), stop=(k == KC - 1))
                            ins = tn.matmul(kvtok[:, 512:1024], lhs,
                                            qkvw[:, 3 * C * k + 2 * C + 512: 3 * C * (k + 1)],
                                            start=(k == 0), stop=(k == KC - 1))
                        P[("vtok", t, b)] = ct.inc(ins, "PE", 1)
                    block.tensor(_)

                    def _(a, t=t, b=b):
                        ct.w(a, "a", "PE", P[("vtok", t, b)])
                        ins = a.copy(v_bf[:, :], kvtok[:, :])
                        A[("vbf", t, b)] = ct.inc(ins, "A", 1)
                    block.scalar(_)

                    def _(tn, t=t, b=b):
                        ct.w(tn, "t", "A", A[("phik", t, b)])
                        ct.w(tn, "t", "A", A[("vbf", t, b)])
                        first = (t == 0 and b == 0)
                        last = (t == NT - 1 and b == NB - 1)
                        for p in range(8):
                            h0, h1 = 2 * p, 2 * p + 1
                            tn.matmul(kvacc[0:64, 64 * p:64 * (p + 1)],
                                      phik[:, 64 * h0:64 * (h0 + 1)],
                                      v_bf[:, 64 * h0:64 * (h0 + 1)],
                                      start=first, stop=last, tile_position=(0, 0))
                            tn.matmul(kvacc[64:128, 64 * p:64 * (p + 1)],
                                      phik[:, 64 * h1:64 * (h1 + 1)],
                                      v_bf[:, 64 * h1:64 * (h1 + 1)],
                                      start=first, stop=last, tile_position=(0, 64))
                        tn.matmul(ksum_ps[0:1, 0:512], onc[:, 0:1], phik[:, 0:512],
                                  start=first, stop=last)
                        ins = tn.matmul(ksum_ps[0:1, 512:1024], onc[:, 0:1],
                                        phik[:, 512:1024], start=first, stop=last)
                        P[("kv", t, b)] = ct.inc(ins, "PE", 1)
                    block.tensor(_)

                for j in range(KC):
                    def _(tn, t=t, j=j, par=par):
                        ct.w(tn, "t", "V", V[("xn", t)])
                        ct.w(tn, "t", "G", G["qkv"])
                        if j < 2:
                            if t > 0:
                                ct.w(tn, "t", "A", A[("sqr", t - 1, 6 + j)])
                        else:
                            ct.w(tn, "t", "A", A[("sqr", t, j - 2)])
                        for k in range(KC):
                            ins = tn.matmul(
                                qpsl(j),
                                qkvw[:, 3 * C * k + 128 * j: 3 * C * k + 128 * (j + 1)],
                                xn1(par, k), start=(k == 0), stop=(k == KC - 1))
                        P[("q", t, j)] = ct.inc(ins, "PE", 1)
                    block.tensor(_)

                    def _(a, t=t, j=j, par=par):
                        ct.w(a, "a", "PE", P[("q", t, j)])
                        ins = a.activation(out=phitmp[:, :], in_=qpsl(j), func=AF.Square)
                        A[("sqr", t, j)] = ct.inc(ins, "A", 1)
                        if t >= 1 and j == 0:
                            ct.w(a, "a", "SP", S[("phisp", t - 1)])
                        ins = a.activation(out=phqk(j), in_=phitmp[:, :],
                                           func=AF.Exp, scale=-0.5)
                        A[("phi", t, j)] = ct.inc(ins, "A", 1)
                    block.scalar(_)

                block.flush()

            def _(v):
                ct.w(v, "v", "PE", P[("kv", NT - 1, NB - 1)])
                v.tensor_copy(out=kv_bf[:, :], in_=kvacc[:, :])
                v.tensor_copy(out=ksrow[:, :], in_=ksum_ps[0:1, :])
                ins = v.memset(ksblk[:, :], 0.0)
                V["kvevac"] = ct.inc(ins, "V", 1)
            block.vector(_)

            def _(sync):
                _spill1(sync, NT - 1)
                ct.w(sync, "s", "V", V["kvevac"])
                S["ksum_d"] = ndma(sync, "s", "R", ksum_d.rearrange("(o n) -> o n", o=1),
                                   ksrow[:, :])
            block.sync(_)

            def _(g):
                ct.w(g, "g", "R", S["ksum_d"])
                ct.w(g, "g", "V", V["kvevac"])
                ndma(g, "g", "GN", ksblk[0:64, 0:127:18],
                     ksum_d.rearrange("(p d) -> d p", p=8)[0:64, :],
                     allow_slow_non_contiguous=True)
                G["ksblk"] = ndma(g, "g", "GN", ksblk[64:128, 1:128:18],
                                  ksum_d.rearrange("(p d) -> d p", p=8)[64:128, :],
                                  allow_slow_non_contiguous=True)
            block.gpsimd(_)
            block.flush()

        if stage < 2:
            def _(sync):
                ct.w(sync, "s", "GN", G["ksblk"])
                ct.w(sync, "s", "S", ct.cnt["S"])
            block.sync(_)
            block.flush()
            return nc

        with contextlib.ExitStack() as p2:
            sb2 = lambda n_, s_, d_: p2.enter_context(nc.sbuf_tensor(n_, s_, d_))
            ps2 = lambda n_, s_: p2.enter_context(nc.psum_tensor(n_, s_, F32))

            conv1w = sb2("conv1w", [128, KC * HID], BF16)
            conv2w = sb2("conv2w", [128, HC * C], BF16)
            lnbuf = sb2("lnbuf", [128, 4096], F32)
            hdd = sb2("hdd", [128, HC * T], BF16)
            zrow_b = sb2("zrow_b", [16, T], BF16)
            qz = sb2("qz", [128, 2 * T], BF16)
            out_bf = sb2("out_bf", [128, 4096], BF16)

            bc2 = sb2("bc2", [128, T], F32)
            bc3 = sb2("bc3", [128, T], F32)
            den16 = ps2("den16", [16, 512])
            zqps = ps2("zqps", [128, 1024])
            outps = ps2("outps", [128, 1024])
            projps = ps2("projps", [128, 1024])
            zql = lambda p: zqps[:, 512 * (p % 2): 512 * (p % 2 + 1)]
            outl = lambda i: outps[:, 512 * (i % 2): 512 * (i % 2 + 1)]
            projl = lambda i: projps[:, 512 * (i % 2): 512 * (i % 2 + 1)]
            bps2 = projps[:, 0:T]
            r2row = den16[0:1, 0:T]
            srow2 = zqps[0:1, :]

            def _(g):
                ct.w(g, "g", "PE", P[("kv", NT - 1, NB - 1)])
                ct.w(g, "g", "V", V[("wq8h", 1)])
                for k in range(KC):
                    ct.inc(g.dma_start(out=conv1w[:, HID * k: HID * (k + 1)],
                                       in_=conv1wT[128 * k:128 * (k + 1), :]), "G", 16)
                for h in range(HC):
                    ins = g.dma_start(out=conv2w[:, C * h: C * (h + 1)],
                                      in_=conv2wT[128 * h:128 * (h + 1), :])
                    ct.inc(ins, "G", 16)
                G["conv"] = ct.cnt["G"]
            block.gpsimd(_)
            block.flush()

            def _ln2amax(v, t, par):
                ct.w(v, "v", "A", A[("i2b", t)])
                ct.w(v, "v", "SP", S[("x1sp", t)])
                for k in range(KC):
                    v.tensor_sub(tsub[:, :], xk(par, k), bc2[:, :])
                    ins = v.tensor_mul(xk(par, k), tsub[:, :], bc3[:, :])
                V[("ln2", t)] = ct.inc(ins, "V", 1)
                if t == 0:
                    ins = v.reduce_max(amax_acc[:, 0:1],
                                       xbuf[:, par * 4096:(par + 1) * 4096],
                                       axis=AXX, apply_absolute_value=True)
                else:
                    ct.w(v, "v", "V",
                         ct.inc(v.reduce_max(ggtmp2[:, 0:1],
                                             xbuf[:, par * 4096:(par + 1) * 4096],
                                             axis=AXX, apply_absolute_value=True),
                                "V", 1))
                    ins = v.tensor_tensor(out=amax_acc[:, 0:1], in0=amax_acc[:, 0:1],
                                          in1=ggtmp2[:, 0:1], op=ALU.max)
                V[("am", t)] = ct.inc(ins, "V", 1)

            def _spill2(sync, t, par):
                ct.w(sync, "s", "V", V[("x1", t, KC - 1)])
                ins = sync.dma_start(
                    out=dram_tile(x1sp, t),
                    in_=xbuf[:, par * 4096:(par + 1) * 4096].rearrange(
                        "p (k n) -> p k n", n=T))
                S[("x1sp", t)] = ct.inc(ins, "SP", 16)
                ct.w(sync, "s", "V", V[("ln2", t)])
                ins = sync.dma_start(
                    out=dram_tile(ln2sp, t),
                    in_=xbuf[:, par * 4096:(par + 1) * 4096].rearrange(
                        "p (k n) -> p k n", n=T))
                S[("ln2sp", t)] = ct.inc(ins, "SP", 16)

            for t in range(NT):
                par = t % 2

                def _(sync, t=t, par=par):
                    if t >= 2:
                        ct.w(sync, "s", "SP", S[("ln2sp", t - 2)])
                        ct.w(sync, "s", "V", V[("am", t - 2)])
                    else:
                        ct.w(sync, "s", "V", V[("xn", NT - 2 + t)])
                    ins = sync.dma_start(
                        out=xbuf[:, par * 4096:(par + 1) * 4096].rearrange(
                            "p (k n) -> p k n", n=T),
                        in_=dram_tile(xT, t))
                    S[("x2p", t)] = ct.inc(ins, "S", 16)
                    if t >= 1:
                        ct.w(sync, "s", "V", V[("qz", t - 1, 7)])
                    else:
                        ct.w(sync, "s", "SP", S[("phisp", NT - 1)])
                    ins = sync.dma_start(
                        out=phq[:, :].rearrange("p (j n) -> p j n", n=T),
                        in_=dram_tile(phisp, t))
                    S[("phi2", t)] = ct.inc(ins, "S", 16)
                    if t > 0:
                        ct.w(sync, "s", "PE", P[("i2b", t - 1)])
                    else:
                        ct.w(sync, "s", "PE", P[("gx", NT - 1)])
                    if t == 0:
                        ct.w(sync, "s", "R", S[("rowsd", NT - 1)])
                    ndma(sync, "s", "R", mtile[0:1, :], rowsD[0:1, T * t: T * (t + 1)])
                    S[("rows2", t)] = ndma(sync, "s", "R", gptile[0:1, :],
                                           rowsD[1:2, T * t: T * (t + 1)])
                    if t > 0:
                        _spill2(sync, t - 1, (t - 1) % 2)
                block.sync(_)

                def _(tn, t=t, par=par):
                    ct.w(tn, "t", "S", S[("phi2", t)])
                    ct.w(tn, "t", "GN", G["ksblk"])
                    if t > 0:
                        ct.w(tn, "t", "V", V[("z", t - 1)])
                        ct.w(tn, "t", "V", V[("f2", t - 1)])
                    for p in range(8):
                        ins = tn.matmul(den16[:, :], ksblk[:, 16 * p:16 * (p + 1)],
                                        phqk(p), start=(p == 0), stop=(p == 7))
                    P[("den", t)] = ct.inc(ins, "PE", 1)
                block.tensor(_)

                def _(v, t=t):
                    ct.w(v, "v", "PE", P[("den", t)])
                    v.tensor_scalar_add(tsub[0:16, :], den16[:, :], 1e-6)
                    v.reciprocal(tsub[0:16, :], tsub[0:16, :])
                    ins = v.tensor_copy(out=zrow_b[:, :], in_=tsub[0:16, :])
                    V[("z", t)] = ct.inc(ins, "V", 1)
                block.vector(_)

                for p in range(8):
                    def _(tn, t=t, p=p):
                        ct.w(tn, "t", "V", V[("z", t)])
                        if p < 2:
                            if t > 0:
                                ct.w(tn, "t", "V", V[("rows2", t - 1)])
                            ct.w(tn, "t", "GN", G["rows_c"])
                        else:
                            ct.w(tn, "t", "V", V[("qz", t, p - 2)])
                        ins = tn.matmul(zql(p), ind16[:, 128 * p:128 * (p + 1)],
                                        zrow_b[:, :], start=True, stop=True)
                        P[("zq", t, p)] = ct.inc(ins, "PE", 1)
                    block.tensor(_)

                    def _(v, t=t, p=p, par=par):
                        ct.w(v, "v", "PE", P[("zq", t, p)])
                        if p >= 2:
                            ct.w(v, "v", "PE", P[("out", t, p - 2)])
                        ins = v.tensor_mul(qz[:, T * (p % 2): T * (p % 2 + 1)],
                                           phqk(p), zql(p))
                        V[("qz", t, p)] = ct.inc(ins, "V", 1)
                    block.vector(_)

                    def _(tn, t=t, p=p):
                        ct.w(tn, "t", "V", V[("qz", t, p)])
                        ct.w(tn, "t", "V", V["kvevac"])
                        if p < 2 and t > 0:
                            ct.w(tn, "t", "V", V[("x1", t - 1, 6 + p)])
                        if p >= 2:
                            ct.w(tn, "t", "A", A[("outbf", t, p - 2)])
                        sl = qz[:, T * (p % 2): T * (p % 2 + 1)]
                        tn.matmul(outl(p)[0:64, :], kv_bf[0:64, 64 * p:64 * (p + 1)],
                                  sl[0:64, :], start=True, stop=True, tile_position=(0, 0))
                        ins = tn.matmul(outl(p)[64:128, :], kv_bf[64:128, 64 * p:64 * (p + 1)],
                                        sl[64:128, :], start=True, stop=True,
                                        tile_position=(64, 64))
                        P[("out", t, p)] = ct.inc(ins, "PE", 1)
                    block.tensor(_)

                    def _(a, t=t, p=p):
                        ct.w(a, "a", "PE", P[("out", t, p)])
                        ins = a.copy(out_bf[:, T * p: T * (p + 1)], outl(p))
                        A[("outbf", t, p)] = ct.inc(ins, "A", 1)
                    block.scalar(_)

                def _(tn, t=t):
                    ct.w(tn, "t", "R", S[("rows2", t)])
                    if t > 0:
                        ct.w(tn, "t", "A", A[("i2b", t - 1)])
                        ct.w(tn, "t", "V", V[("xn2", t - 1)])
                        ct.w(tn, "t", "V", V[("ln", t - 1)])
                    ins = tn.matmul(bps2, onrb[0:1, 0:128], mtile[0:1, :],
                                    start=True, stop=True)
                    P[("mb2", t)] = ct.inc(ins, "PE", 1)
                block.tensor(_)

                def _(a, t=t):
                    ct.w(a, "a", "PE", P[("mb2", t)])
                    ins = a.copy(bc0[:, :], bps2)
                    A[("mb2", t)] = ct.inc(ins, "A", 1)
                block.scalar(_)

                def _(tn, t=t):
                    ct.w(tn, "t", "A", A[("mb2", t)])
                    ins = tn.matmul(bps2, onrb[0:1, 0:128], gptile[0:1, :],
                                    start=True, stop=True)
                    P[("gpb", t)] = ct.inc(ins, "PE", 1)
                block.tensor(_)

                def _(a, t=t):
                    ct.w(a, "a", "PE", P[("gpb", t)])
                    ins = a.copy(bc1[:, :], bps2)
                    A[("gpb", t)] = ct.inc(ins, "A", 1)
                block.scalar(_)

                def _(v, t=t, par=par):
                    ct.w(v, "v", "A", A[("gpb", t)])
                    ct.w(v, "v", "S", S[("x2p", t)])
                    for k in range(KC):
                        v.tensor_sub(tsub[:, :], xk(par, k), bc0[:, :])
                        ins = v.tensor_mul(lnbuf[:, T * k: T * (k + 1)], tsub[:, :],
                                           bc1[:, :])
                    V[("ln", t)] = ct.inc(ins, "V", 1)
                block.vector(_)

                for j in range(KC):
                    def _(tn, t=t, j=j):
                        ct.w(tn, "t", "A", A[("outbf", t, 7)])
                        ct.w(tn, "t", "G", G["pw"])
                        ct.w(tn, "t", "GN", G["rows_c"])
                        if j == 0:
                            ct.w(tn, "t", "A", A[("gpb", t)])
                        if j >= 2:
                            ct.w(tn, "t", "V", V[("u", t, j - 2)])
                        for k in range(KC):
                            tn.matmul(projl(j),
                                      projw[:, C * k + 128 * j: C * k + 128 * (j + 1)],
                                      out_bf[:, T * k: T * (k + 1)],
                                      start=(k == 0), stop=False)
                        ins = tn.matmul(projl(j), prow[0:1, 128 * j:128 * (j + 1)],
                                        onr5[0:1, :], start=False, stop=True)
                        P[("proj", t, j)] = ct.inc(ins, "PE", 1)
                    block.tensor(_)

                    def _(v, t=t, j=j):
                        ct.w(v, "v", "PE", P[("proj", t, j)])
                        ins = v.scalar_tensor_tensor(
                            out=lnbuf[:, T * j: T * (j + 1)], in0=projl(j),
                            scalar=gmsb[:, j:j + 1], in1=lnbuf[:, T * j: T * (j + 1)],
                            op0=ALU.mult, op1=ALU.add)
                        V[("u", t, j)] = ct.inc(ins, "V", 1)
                    block.vector(_)

                for k in range(KC):
                    s = k % 2
                    def _(a, t=t, k=k, s=s):
                        ct.w(a, "a", "V", V[("u", t, k)])
                        if k >= 2:
                            ct.w(a, "a", "PE", P[("r2", t, k - 2)])
                        elif t > 0:
                            ct.w(a, "a", "PE", P[("st2", t - 1, 6 + k)])
                        ins = a.activation(out=stAB[:, T * s: T * (s + 1)],
                                           in_=lnbuf[:, T * k: T * (k + 1)],
                                           func=AF.Square)
                        A[("squ", t, k)] = ct.inc(ins, "A", 1)
                    block.scalar(_)

                    def _(tn, t=t, k=k, s=s):
                        ct.w(tn, "t", "A", A[("squ", t, k)])
                        if k == 0:
                            ct.w(tn, "t", "V", V[("z", t)])
                        ins = tn.matmul(r2row, onc[:, 0:1], stAB[:, T * s: T * (s + 1)],
                                        start=(k == 0), stop=(k == KC - 1))
                        P[("r2", t, k)] = ct.inc(ins, "PE", 1)
                    block.tensor(_)

                def _(a, t=t):
                    ct.w(a, "a", "PE", P[("r2", t, KC - 1)])
                    if t > 0:
                        ct.w(a, "a", "PE", P[("f2b", t - 1)])
                    a.activation(out=rowt1[0:1, :], in_=r2row, func=AF.Ln)
                    ins = a.activation(out=rowb[0:1, :], in_=rowt1[0:1, :],
                                       func=AF.Exp, scale=-0.5)
                    A[("f20", t)] = ct.inc(ins, "A", 1)
                block.scalar(_)

                def _(v, t=t):
                    ct.w(v, "v", "A", A[("f20", t)])
                    if t > 0:
                        ct.w(v, "v", "PE", P[("f2b", t - 1)])
                    ins = v.tensor_scalar_mul(rowb[0:1, :], rowb[0:1, :], SQC)
                    V[("f2", t)] = ct.inc(ins, "V", 1)
                block.vector(_)

                def _(tn, t=t):
                    ct.w(tn, "t", "V", V[("f2", t)])
                    ct.w(tn, "t", "V", V[("u", t, KC - 1)])
                    ins = tn.matmul(bps2, onrb[0:1, 0:128], rowb[0:1, :],
                                    start=True, stop=True)
                    P[("f2b", t)] = ct.inc(ins, "PE", 1)
                block.tensor(_)

                def _(a, t=t):
                    ct.w(a, "a", "PE", P[("f2b", t)])
                    ins = a.copy(bc0[:, :], bps2)
                    A[("f2b", t)] = ct.inc(ins, "A", 1)
                block.scalar(_)

                def _(v, t=t):
                    ct.w(v, "v", "A", A[("f2b", t)])
                    for k in range(KC):
                        ins = v.scalar_tensor_tensor(
                            out=xnk(k), in0=lnbuf[:, T * k: T * (k + 1)],
                            scalar=s2sb[:, k:k + 1], in1=bc0[:, :],
                            op0=ALU.mult, op1=ALU.mult)
                    V[("xn2", t)] = ct.inc(ins, "V", 1)
                block.vector(_)

                for h in range(HC):
                    def _(tn, t=t, h=h):
                        ct.w(tn, "t", "V", V[("xn2", t)])
                        ct.w(tn, "t", "G", G["conv"])
                        if h >= 2:
                            ct.w(tn, "t", "A", A[("hdd", t, h - 2)])
                        else:
                            ct.w(tn, "t", "A", A[("outbf", t, 6 + h)])
                        for k in range(KC):
                            ins = tn.matmul(
                                outl(h),
                                conv1w[:, HID * k + 128 * h: HID * k + 128 * (h + 1)],
                                xnk(k), start=(k == 0), stop=(k == KC - 1))
                        P[("c1", t, h)] = ct.inc(ins, "PE", 1)
                    block.tensor(_)

                    def _(a, t=t, h=h):
                        ct.w(a, "a", "PE", P[("c1", t, h)])
                        ins = a.activation(out=hdd[:, T * h: T * (h + 1)], in_=outl(h),
                                           func=AF.Gelu, bias=c1bsb[:, h:h + 1])
                        A[("hdd", t, h)] = ct.inc(ins, "A", 1)
                    block.scalar(_)

                if t >= 1:
                    def _(v, t=t):
                        _ln2amax(v, t - 1, (t - 1) % 2)
                    block.vector(_)

                for o in range(KC):
                    def _(tn, t=t, o=o):
                        ct.w(tn, "t", "A", A[("hdd", t, HC - 1)])
                        if o >= 2:
                            ct.w(tn, "t", "V", V[("x1", t, o - 2)])
                        for h in range(HC):
                            ins = tn.matmul(outl(o),
                                            conv2w[:, C * h + 128 * o: C * h + 128 * (o + 1)],
                                            hdd[:, T * h: T * (h + 1)],
                                            start=(h == 0), stop=(h == HC - 1))
                        P[("c2", t, o)] = ct.inc(ins, "PE", 1)
                    block.tensor(_)

                    def _(v, t=t, o=o, par=par):
                        ct.w(v, "v", "PE", P[("c2", t, o)])
                        ct.w(v, "v", "V", V["gc2b"])
                        v.scalar_tensor_tensor(
                            out=lnbuf[:, T * o: T * (o + 1)], in0=outl(o),
                            scalar=gmsb[:, o:o + 1], in1=lnbuf[:, T * o: T * (o + 1)],
                            op0=ALU.mult, op1=ALU.add)
                        ins = v.scalar_tensor_tensor(
                            out=xk(par, o),
                            in0=lnbuf[:, T * o: T * (o + 1)],
                            scalar=gc2b[:, o:o + 1], in1=xk(par, o),
                            op0=ALU.add, op1=ALU.add)
                        V[("x1", t, o)] = ct.inc(ins, "V", 1)
                    block.vector(_)

                for k in range(KC):
                    s = k % 2
                    def _(a, t=t, k=k, s=s):
                        ct.w(a, "a", "V", V[("x1", t, k)])
                        if k >= 2:
                            ct.w(a, "a", "PE", P[("st2", t, k - 2)])
                        else:
                            ct.w(a, "a", "PE", P[("r2", t, 6 + k)])
                        ins = a.activation(out=stAB[:, T * s: T * (s + 1)],
                                           in_=xk(t % 2, k),
                                           func=AF.Square)
                        A[("st2", t, k)] = ct.inc(ins, "A", 1)
                    block.scalar(_)

                    def _(tn, t=t, k=k, s=s):
                        ct.w(tn, "t", "A", A[("st2", t, k)])
                        ct.w(tn, "t", "V", V[("x1", t, k)])
                        if k == 0:
                            ct.w(tn, "t", "V", V[("qz", t, 7)])
                        tn.matmul(srow2[:, 0:T], onc[:, 0:1], stAB[:, T * s: T * (s + 1)],
                                  start=(k == 0), stop=(k == KC - 1))
                        ins = tn.matmul(srow2[:, T:2 * T], oncf[:, 0:1],
                                        xk(t % 2, k),
                                        start=(k == 0), stop=(k == KC - 1))
                        P[("st2", t, k)] = ct.inc(ins, "PE", 1)
                    block.tensor(_)

                def _(v, t=t):
                    ct.w(v, "v", "PE", P[("st2", t, KC - 1)])
                    v.tensor_scalar_mul(mtile[0:1, :], srow2[0:1, T:2 * T], 1.0 / C)
                    v.tensor_tensor(out=rowt0[0:1, :], in0=mtile[0:1, :],
                                    in1=mtile[0:1, :], op=ALU.mult)
                    v.scalar_tensor_tensor(out=rowt0[0:1, :], in0=srow2[0:1, 0:T],
                                            scalar=1.0 / C, in1=rowt0[0:1, :],
                                            op0=ALU.mult, op1=ALU.subtract)
                    ins = v.tensor_scalar_add(rowt0[0:1, :], rowt0[0:1, :], 1e-5)
                    V[("rows2", t)] = ct.inc(ins, "V", 1)
                block.vector(_)

                def _(a, t=t):
                    ct.w(a, "a", "V", V[("rows2", t)])
                    a.activation(out=rowt1[0:1, :], in_=rowt0[0:1, :], func=AF.Ln)
                    ins = a.activation(out=rowt1[0:1, :], in_=rowt1[0:1, :],
                                       func=AF.Exp, scale=-0.5)
                    A[("i2", t)] = ct.inc(ins, "A", 1)
                block.scalar(_)

                def _(tn, t=t):
                    ct.w(tn, "t", "V", V[("rows2", t)])
                    ct.w(tn, "t", "A", A[("f2b", t)])
                    if t > 0:
                        ct.w(tn, "t", "V", V[("ln2", t - 1)])
                    ins = tn.matmul(bps2, onrb[0:1, 0:128], mtile[0:1, :],
                                    start=True, stop=True)
                    P[("m2b", t)] = ct.inc(ins, "PE", 1)
                block.tensor(_)

                def _(a, t=t):
                    ct.w(a, "a", "PE", P[("m2b", t)])
                    ins = a.copy(bc2[:, :], bps2)
                    A[("m2b", t)] = ct.inc(ins, "A", 1)
                block.scalar(_)

                def _(tn, t=t):
                    ct.w(tn, "t", "A", A[("i2", t)])
                    ct.w(tn, "t", "A", A[("m2b", t)])
                    ins = tn.matmul(bps2, onr[0:1, 0:128], rowt1[0:1, :],
                                    start=True, stop=True)
                    P[("i2b", t)] = ct.inc(ins, "PE", 1)
                block.tensor(_)

                def _(a, t=t):
                    ct.w(a, "a", "PE", P[("i2b", t)])
                    ins = a.copy(bc3[:, :], bps2)
                    A[("i2b", t)] = ct.inc(ins, "A", 1)
                block.scalar(_)

                block.flush()

            def _(v):
                _ln2amax(v, NT - 1, (NT - 1) % 2)
            block.vector(_)

            def _(sync):
                _spill2(sync, NT - 1, (NT - 1) % 2)
            block.sync(_)
            block.flush()

        if stage < 3:
            def _(sync):
                ct.w(sync, "s", "S", ct.cnt["S"])
                ct.w(sync, "s", "V", V[("am", NT - 1)])
            block.sync(_)
            block.flush()
            return nc

        def _(g):
            ct.w(g, "g", "V", V[("am", NT - 1)])
            ins = g.dma_start(out=ccin.ap().rearrange("o p -> p o"), in_=amax_acc[:, 0:1],
                              allow_slow_non_contiguous=True)
            mk = ct.inc(ins, "G", 16)
            ct.w(g, "g", "G", mk)
            ins = g.collective_compute(
                "AllReduce", ALU.max, replica_groups=[list(range(8))],
                ins=[ccin.ap().opt()], outs=[ccout.ap().opt()])
            ct.inc(ins, "CC", 1)
            ct.w(g, "g", "CC", ct.cnt["CC"])
            G["crow"] = ndma(g, "g", "GN", crow[:, :], ccout.ap()[0:1, :])
        block.gpsimd(_)

        def _(v):
            ct.w(v, "v", "GN", G["crow"])
            def step(f):
                ct.w(v, "v", "V", ct.inc(f(), "V", 1))
            step(lambda: v.reduce_max(scr[0:1, 5:6], crow[:, :], axis=AXX))
            step(lambda: v.tensor_scalar_max(scr[0:1, 5:6], scr[0:1, 5:6], 1e-12))
            step(lambda: v.reciprocal(scr[0:1, 6:7], scr[0:1, 5:6]))
            step(lambda: v.tensor_scalar(out=scr[0:1, 6:7], in0=scr[0:1, 6:7],
                                         scalar1=448.0, scalar2=448.0,
                                         op0=ALU.mult, op1=ALU.min))
            step(lambda: v.tensor_scalar_mul(scr[0:1, 7:8], scr[0:1, 6:7], 0.5))
            step(lambda: v.reciprocal(scr[0:1, 5:6], scr[0:1, 6:7]))
            step(lambda: v.tensor_tensor(out=scr[0:1, 5:6], in0=scr[0:1, 5:6],
                                         in1=scr[0:1, 4:5], op=ALU.mult))
            step(lambda: v.tensor_scalar_mul(scr[0:1, 5:6], scr[0:1, 5:6], 4.0))
            V["xsc"] = ct.cnt["V"]
        block.vector(_)
        block.flush()

        with contextlib.ExitStack() as p3:
            sb3 = lambda n_, s_, d_: p3.enter_context(nc.sbuf_tensor(n_, s_, d_))
            ps3 = lambda n_, s_: p3.enter_context(nc.psum_tensor(n_, s_, F32))

            l2buf = sb3("l2buf", [128, 2 * 4096], F32)
            xq8 = sb3("xq8", [128, 2 * 4096], FP8)
            ffst = sb3("ffst", [128, 2 * T], BF16)
            ffps = ps3("ffps", [128, 1024])
            screp_ps = ps3("screp_ps", [128, 2])
            ffl = lambda o: ffps[:, T * (o % 2): T * (o % 2 + 1)]

            def _(tn):
                ct.w(tn, "t", "V", V["xsc"])
                tn.matmul(screp_ps[:, 0:1], onr[0:1, 0:128], scr[0:1, 7:8],
                          start=True, stop=False)
                ins = tn.matmul(screp_ps[:, 1:2], onr[0:1, 0:128], scr[0:1, 5:6],
                                start=False, stop=True)
                P["screp"] = ct.inc(ins, "PE", 1)
            block.tensor(_)

            def _(a):
                ct.w(a, "a", "PE", P["screp"])
                ins = a.copy(screp[:, :], screp_ps[:, :])
                A["screp"] = ct.inc(ins, "A", 1)
            block.scalar(_)
            block.flush()

            if taps:
                def _(sync):
                    ct.w(sync, "s", "V", V["xsc"])
                    ndma(sync, "s", "R", tap_sc[:, :], scr[:, :])
                block.sync(_)

                def _(g):
                    ct.w(g, "g", "V", V["kvevac"])
                    ins = g.dma_start(out=tap_kv[:, :], in_=kv_bf[:, :])
                    ct.inc(ins, "G", 16)
                block.gpsimd(_)

            def _q8(tq, parq):
                def _(v, t=tq, par=parq):
                    ct.w(v, "v", "S", S[("l2", t)])
                    ct.w(v, "v", "A", A["screp"])
                    if t >= 2:
                        ct.w(v, "v", "PE", P[("ffn", t - 2, KC - 1)])
                    for k in range(KC):
                        ins = v.tensor_scalar(
                            out=xq8[:, par * 4096 + T * k: par * 4096 + T * (k + 1)],
                            in0=l2buf[:, par * 4096 + T * k: par * 4096 + T * (k + 1)],
                            scalar1=screp[:, 0:1], scalar2=None, op0=ALU.mult)
                    V[("q8", t)] = ct.inc(ins, "V", 1)
                block.vector(_)

            def _ystore(sync, t, par):
                ct.w(sync, "s", "V", V[("y", t, KC - 1)])
                ins = sync.dma_start(
                    out=dram_tile(yT, t),
                    in_=xbuf[:, par * 4096:(par + 1) * 4096].rearrange(
                        "p (k n) -> p k n", n=T))
                S[("y", t)] = ct.inc(ins, "SP", 16)

            for t in range(NT):
                par = t % 2

                def _(sync, t=t, par=par):
                    if t >= 2:
                        ct.w(sync, "s", "SP", S[("y", t - 2)])
                    else:
                        ct.w(sync, "s", "SP", S[("ln2sp", NT - 1)])
                        ct.w(sync, "s", "SP", S[("x1sp", NT - 1)])
                        ct.w(sync, "s", "V", V[("ln2", NT - 1)])
                        ct.w(sync, "s", "PE", P[("st2", NT - 1, KC - 1)])
                        ct.w(sync, "s", "V", V[("am", NT - 2 + t)])
                    ins = sync.dma_start(
                        out=l2buf[:, par * 4096:(par + 1) * 4096].rearrange(
                            "p (k n) -> p k n", n=T),
                        in_=dram_tile(ln2sp, t))
                    S[("l2", t)] = ct.inc(ins, "S", 16)
                    ins = sync.dma_start(
                        out=xbuf[:, par * 4096:(par + 1) * 4096].rearrange(
                            "p (k n) -> p k n", n=T),
                        in_=dram_tile(x1sp, t))
                    S[("x3", t)] = ct.inc(ins, "S", 16)
                    if t > 0:
                        _ystore(sync, t - 1, (t - 1) % 2)
                block.sync(_)

                if t == 0:
                    _q8(0, 0)

                for o in range(KC):
                    if o == 6 and t + 1 < NT:
                        _q8(t + 1, (t + 1) % 2)
                    def _(tn, t=t, o=o, par=par):
                        ct.w(tn, "t", "V", V[("q8", t)])
                        ct.w(tn, "t", "V", V[("wq8h", 1)])
                        if o >= 2:
                            ct.w(tn, "t", "A", A[("ffn", t, o - 2)])
                        elif t > 0:
                            ct.w(tn, "t", "A", A[("ffn", t - 1, 6 + o)])
                        for k in range(KC):
                            ins = tn.matmul(
                                ffl(o), wq8[:, C * k + 128 * o: C * k + 128 * (o + 1)],
                                xq8[:, par * 4096 + T * k: par * 4096 + T * (k + 1)],
                                start=(k == 0), stop=(k == KC - 1))
                        P[("ffn", t, o)] = ct.inc(ins, "PE", 1)
                    block.tensor(_)

                    def _(a, t=t, o=o):
                        ct.w(a, "a", "PE", P[("ffn", t, o)])
                        if o >= 2:
                            ct.w(a, "a", "V", V[("y", t, o - 2)])
                        elif t > 0:
                            ct.w(a, "a", "V", V[("y", t - 1, 6 + o)])
                        ins = a.activation(out=ffst[:, T * (o % 2): T * (o % 2 + 1)],
                                           in_=ffl(o), func=AF.Identity,
                                           bias=fbsb[:, o:o + 1], scale=screp[:, 1:2])
                        A[("ffn", t, o)] = ct.inc(ins, "A", 1)
                    block.scalar(_)

                    def _(v, t=t, o=o, par=par):
                        ct.w(v, "v", "A", A[("ffn", t, o)])
                        ct.w(v, "v", "S", S[("x3", t)])
                        ins = v.tensor_add(xk(par, o), ffst[:, T * (o % 2): T * (o % 2 + 1)],
                                           xk(par, o))
                        V[("y", t, o)] = ct.inc(ins, "V", 1)
                    block.vector(_)

                block.flush()

            def _(sync):
                _ystore(sync, NT - 1, (NT - 1) % 2)
                ct.w(sync, "s", "S", ct.cnt["S"])
                ct.w(sync, "s", "SP", ct.cnt["SP"])
            block.sync(_)
            block.flush()

    return nc


def kernel(L0_attn_x, L0_attn_scale1, L0_attn_scale2, L0_attn_gamma, L0_attn_qkv_w,
           L0_attn_proj_w, L0_attn_proj_b, L0_attn_conv1_w, L0_attn_conv1_b,
           L0_attn_conv2_w, L0_attn_conv2_b, L0_ffn_weight, L0_ffn_bias):
    x = np.asarray(L0_attn_x, dtype=np.float32)
    assert x.shape == (8, N, C)

    shared = {
        "qkvwT": np.ascontiguousarray(np.asarray(L0_attn_qkv_w, np.float32).T),
        "projwT": np.ascontiguousarray(np.asarray(L0_attn_proj_w, np.float32).T),
        "conv1wT": np.ascontiguousarray(np.asarray(L0_attn_conv1_w, np.float32).T),
        "conv2wT": np.ascontiguousarray(np.asarray(L0_attn_conv2_w, np.float32).T),
        "ffnwT": np.ascontiguousarray(np.asarray(L0_ffn_weight, np.float32).T),
        "scale1": np.ascontiguousarray(np.asarray(L0_attn_scale1, np.float32).reshape(KC, 128).T),
        "scale2": np.ascontiguousarray(np.asarray(L0_attn_scale2, np.float32).reshape(KC, 128).T),
        "gamma": np.ascontiguousarray(np.asarray(L0_attn_gamma, np.float32).reshape(KC, 128).T),
        "projb": np.ascontiguousarray(np.asarray(L0_attn_proj_b, np.float32)),
        "conv1b": np.ascontiguousarray(np.asarray(L0_attn_conv1_b, np.float32).reshape(HC, 128).T),
        "conv2b": np.ascontiguousarray(np.asarray(L0_attn_conv2_b, np.float32).reshape(KC, 128).T),
        "ffnb": np.ascontiguousarray(np.asarray(L0_ffn_bias, np.float32).reshape(KC, 128).T),
    }
    in_maps = []
    for i in range(8):
        m = dict(shared)
        m["xT"] = np.ascontiguousarray(x[i].T)
        in_maps.append(m)

    nc1 = build_nc(stage=3, taps=False)
    ph1 = {k: dict(v) for k, v in nc1._marks_out.items()}
    del nc1
    nc = build_nc(stage=3, taps=False, prev=ph1)
    for k, v in nc._marks_out.items():
        assert dict(v) == ph1[k], f"phase mismatch in {k}"
    res = run_bass_kernel_spmd(nc, in_maps, core_ids=list(range(8)))
    out = np.stack([np.asarray(r["yT"]).T for r in res.results], axis=0)
    return np.ascontiguousarray(out.astype(np.float32))


# version 15 (final)
# speedup vs baseline: 1.0514x; 1.0077x over previous
import contextlib

import ml_dtypes
import numpy as np

import concourse.bass as bass
import concourse.mybir as mybir
from concourse.bass_utils import run_bass_kernel_spmd

F32 = mybir.dt.float32
BF16 = mybir.dt.bfloat16
FP8 = mybir.dt.float8e4
AF = mybir.ActivationFunctionType
ALU = mybir.AluOpType
AXX = mybir.AxisListType.X

N = 4096
C = 1024
HID = 2048
T = 512
NT = N // T
KC = C // 128
HC = HID // 128
NB = T // 128
SQC = 32.000001


class Marks(dict):

    def __init__(self, prev=None):
        super().__init__()
        self.prev = prev

    def __getitem__(self, k):
        if dict.__contains__(self, k):
            return dict.__getitem__(self, k)
        if self.prev is not None:
            return self.prev.get(k)
        return None


class Coll:

    ORDER = ("sync", "gpsimd", "tensor", "scalar", "vector")

    def __init__(self, real):
        self.real = real
        self.q = {k: [] for k in self.ORDER}

    def _add(self, name, f):
        self.q[name].append(f)

    def sync(self, f):
        self._add("sync", f)

    def gpsimd(self, f):
        self._add("gpsimd", f)

    def tensor(self, f):
        self._add("tensor", f)

    def scalar(self, f):
        self._add("scalar", f)

    def vector(self, f):
        self._add("vector", f)

    def flush(self):
        for k in self.ORDER:
            fs, self.q[k] = self.q[k], []
            if fs:
                getattr(self.real, k)(lambda e, fs=fs: [f(e) for f in fs])


class Ctr:
    def __init__(self, sems):
        self.sems = sems
        self.cnt = {k: 0 for k in sems}
        self.water = {}

    def inc(self, ins, sem, k=1):
        self.cnt[sem] += k
        ins.then_inc(self.sems[sem], k)
        return self.cnt[sem]

    def w(self, eng, engname, sem, val):
        if val is None or val <= 0:
            return
        key = (engname, sem)
        if self.water.get(key, 0) >= val:
            return
        eng.wait_ge(self.sems[sem], val)
        self.water[key] = val


def build_nc(stage=3, taps=False, prev=None):
    nc = bass.Bass("TRN2", target_bir_lowering=False, detect_race_conditions=False)

    xT = nc.dram_tensor("xT", [C, N], F32, kind="ExternalInput").ap()
    qkvwT = nc.dram_tensor("qkvwT", [C, 3 * C], F32, kind="ExternalInput").ap()
    projwT = nc.dram_tensor("projwT", [C, C], F32, kind="ExternalInput").ap()
    conv1wT = nc.dram_tensor("conv1wT", [C, HID], F32, kind="ExternalInput").ap()
    conv2wT = nc.dram_tensor("conv2wT", [HID, C], F32, kind="ExternalInput").ap()
    ffnwT = nc.dram_tensor("ffnwT", [C, C], F32, kind="ExternalInput").ap()
    scale1_d = nc.dram_tensor("scale1", [128, KC], F32, kind="ExternalInput").ap()
    scale2_d = nc.dram_tensor("scale2", [128, KC], F32, kind="ExternalInput").ap()
    gamma_d = nc.dram_tensor("gamma", [128, KC], F32, kind="ExternalInput").ap()
    projb_d = nc.dram_tensor("projb", [C], F32, kind="ExternalInput").ap()
    conv1b_d = nc.dram_tensor("conv1b", [128, HC], F32, kind="ExternalInput").ap()
    conv2b_d = nc.dram_tensor("conv2b", [128, KC], F32, kind="ExternalInput").ap()
    ffnb_d = nc.dram_tensor("ffnb", [128, KC], F32, kind="ExternalInput").ap()

    yT = nc.dram_tensor("yT", [C, N], F32, kind="ExternalOutput").ap()

    spk = "ExternalOutput" if taps else "Internal"
    phisp = nc.dram_tensor("phisp", [C, N], BF16, kind=spk).ap()
    x1sp = nc.dram_tensor("x1sp", [C, N], F32, kind=spk).ap()
    ln2sp = nc.dram_tensor("ln2sp", [C, N], F32, kind=spk).ap()
    ksum_d = nc.dram_tensor("ksum_d", [C], F32, kind=spk).ap()
    rowsD = nc.dram_tensor("rowsD", [2, N], BF16, kind="Internal").ap()
    wam_d = nc.dram_tensor("wam_d", [128], F32, kind="Internal").ap()
    ccin = nc.dram_tensor("ccin", [1, 128], F32, kind="Internal")
    ccout = nc.dram_tensor("ccout", [1, 128], F32, kind="Internal")
    if taps:
        tap_sc = nc.dram_tensor("tap_sc", [1, 8], F32, kind="ExternalOutput").ap()
        tap_kv = nc.dram_tensor("tap_kv", [128, 512], F32, kind="ExternalOutput").ap()

    ones_col = nc.inline_tensor(np.ones((128, 1), dtype=ml_dtypes.bfloat16), "c_onescol").ap()
    onesrow_f = nc.inline_tensor(np.ones((1, 128), dtype=np.float32), "c_onesrow").ap()
    onesrow_b = nc.inline_tensor(np.ones((1, 128), dtype=ml_dtypes.bfloat16), "c_onesrowb").ap()
    ones_colf = nc.inline_tensor(np.ones((128, 1), dtype=np.float32), "c_onescolf").ap()
    ones512 = nc.inline_tensor(np.ones((1, T), dtype=ml_dtypes.bfloat16), "c_ones512").ap()
    ind = np.zeros((16, 8 * 128), dtype=np.float32)
    for p in range(8):
        ind[2 * p, 128 * p:128 * p + 64] = 1.0
        ind[2 * p + 1, 128 * p + 64:128 * p + 128] = 1.0
    ind16_d = nc.inline_tensor(ind.astype(ml_dtypes.bfloat16), "c_ind16").ap()

    with contextlib.ExitStack() as st:
        sb = lambda n_, s_, d_: st.enter_context(nc.sbuf_tensor(n_, s_, d_))
        sems = {k: st.enter_context(nc.semaphore(k)) for k in
                ["S", "G", "R", "GN", "PE", "V", "A", "CC", "W", "SP"]}
        ct = Ctr(sems)
        blk_real = st.enter_context(nc.Block())
        block = Coll(blk_real)

        def ndma(eng, engname, sem, out, in_, **kw):
            ct.w(eng, engname, sem, ct.cnt[sem])
            ins = eng.dma_start(out=out, in_=in_, **kw)
            return ct.inc(ins, sem, 16)

        xbuf = sb("xbuf", [128, 2 * 4096], F32)
        phq = sb("phq", [128, 4096], BF16)
        xn_bf = sb("xn_bf", [128, 4096], BF16)
        stAB = sb("stAB", [128, 2 * T], BF16)
        projw = sb("projw", [128, KC * C], BF16)
        wq8 = sb("wq8", [128, KC * C], FP8)
        kv_bf = sb("kv_bf", [128, 512], BF16)
        ksblk = sb("ksblk", [128, 128], BF16)
        mtile = sb("mtile", [1, T], BF16)
        gptile = sb("gptile", [1, T], BF16)
        rowb = sb("rowb", [1, T], BF16)
        rowt0 = sb("rowt0", [1, T], F32)
        rowt1 = sb("rowt1", [1, T], F32)
        bc0 = sb("bc0", [128, T], F32)
        bc1 = sb("bc1", [128, T], F32)
        tsub = sb("tsub", [128, T], F32)
        onc = sb("onc", [128, 1], BF16)
        oncf = sb("oncf", [128, 1], F32)
        onr = sb("onr", [1, 128], F32)
        onrb = sb("onrb", [1, 128], BF16)
        onr5 = sb("onr5", [1, T], BF16)
        ind16 = sb("ind16", [16, 8 * 128], BF16)
        s1sb = sb("s1sb", [128, KC], F32)
        s2sb = sb("s2sb", [128, KC], F32)
        gmsb = sb("gmsb", [128, KC], F32)
        c1bsb = sb("c1bsb", [128, HC], F32)
        fbsb = sb("fbsb", [128, KC], F32)
        prow = sb("prow", [1, C], BF16)
        c2bsb = sb("c2bsb", [128, KC], F32)
        gc2b = sb("gc2b", [128, KC], F32)
        scr = sb("scr", [1, 8], F32)
        screp = sb("screp", [128, 2], F32)
        swhrep = sb("swhrep", [128, 1], F32)
        wamax = sb("wamax", [128, 1], F32)
        wgtmp = sb("wgtmp", [128, 1], F32)
        amax_acc = sb("amax_acc", [128, 1], F32)
        ggtmp2 = sb("ggtmp2", [128, 1], F32)
        crow = sb("crow", [1, 128], F32)

        xk = lambda par, k: xbuf[:, par * 4096 + T * k: par * 4096 + T * (k + 1)]
        phqk = lambda j: phq[:, T * j: T * (j + 1)]
        xnk = lambda k: xn_bf[:, T * k: T * (k + 1)]
        stAs = lambda s: stAB[:, 0:T]
        stBs = lambda s: stAB[:, T:2 * T]

        def dram_tile(dram_ap, t):
            return dram_ap.rearrange("(k p) n -> p k n", p=128)[:, :, T * t: T * (t + 1)]

        pv = prev or {}
        V = Marks(pv.get("V"))
        A = Marks(pv.get("A"))
        P = Marks(pv.get("P"))
        G = Marks(pv.get("G"))
        GC = Marks(pv.get("GC"))
        S = Marks(pv.get("S"))
        marks = {"V": V, "A": A, "P": P, "G": G, "GC": GC, "S": S}
        nc._marks_out = marks

        def _(sync):
            ct.inc(sync.dma_start(out=onc[:, :], in_=ones_col[:, :]), "S", 16)
            ct.inc(sync.dma_start(out=oncf[:, :], in_=ones_colf[:, :]), "S", 16)
            ndma(sync, "s", "R", onr[:, :], onesrow_f[:, :])
            S["onr"] = ndma(sync, "s", "R", onrb[:, :], onesrow_b[:, :])
            ct.inc(sync.dma_start(out=s1sb[:, :], in_=scale1_d[:, :]), "S", 16)
            ct.inc(sync.dma_start(out=s2sb[:, :], in_=scale2_d[:, :]), "S", 16)
            ct.inc(sync.dma_start(out=gmsb[:, :], in_=gamma_d[:, :]), "S", 16)
            ct.inc(sync.dma_start(out=c1bsb[:, :], in_=conv1b_d[:, :]), "S", 16)
            ins = sync.dma_start(out=fbsb[:, :], in_=ffnb_d[:, :])
            S["consts"] = ct.inc(ins, "S", 16)
            ins = sync.dma_start(out=xbuf[:, 0:4096].rearrange("p (k n) -> p k n", n=T),
                                 in_=dram_tile(xT, 0))
            S[("x1p", 0)] = ct.inc(ins, "S", 16)
        block.sync(_)

        def _(g):
            ndma(g, "g", "GN", ind16[:, :], ind16_d[:, :])
            for k in range(KC):
                ct.inc(g.dma_start(out=projw[:, C * k: C * (k + 1)],
                                   in_=projwT[128 * k:128 * (k + 1), :]), "G", 16)
            G["pw"] = ct.cnt["G"]
            ndma(g, "g", "GN", prow[:, :], projb_d.rearrange("(o n) -> o n", o=1))
            G["rows_c"] = ndma(g, "g", "GN", onr5[:, :], ones512[:, :])
        block.gpsimd(_)

        def _(sync):
            ins = sync.dma_start(out=c2bsb[:, :], in_=conv2b_d[:, :])
            S["c2b"] = ct.inc(ins, "S", 16)
        block.sync(_)

        def _(v):
            ct.w(v, "v", "S", S["c2b"])
            ins = v.tensor_tensor(out=gc2b[:, :], in0=gmsb[:, :], in1=c2bsb[:, :],
                                  op=ALU.mult)
            V["gc2b"] = ct.inc(ins, "V", 1)
        block.vector(_)
        block.flush()

        with contextlib.ExitStack() as p1:
            sb1 = lambda n_, s_, d_: p1.enter_context(nc.sbuf_tensor(n_, s_, d_))
            ps1 = lambda n_, s_: p1.enter_context(nc.psum_tensor(n_, s_, F32))

            qkvw = sb1("qkvw", [128, KC * 3 * C], BF16)
            xn1b = sb1("xn1b", [128, 4096], BF16)
            wstage = sb1("wstage", [128, 4 * C], F32)
            vtile = sb1("vtile", [1, T], F32)
            ksrow = sb1("ksrow", [1, C], F32)
            rowt2 = sb1("rowt2", [1, T], F32)
            wrow = sb1("wrow", [1, 128], F32)
            st1 = sb1("st1", [128, 4 * T], BF16)
            st1A = lambda s: st1[:, 2 * s * T: (2 * s + 1) * T]
            st1B = lambda s: st1[:, (2 * s + 1) * T: (2 * s + 2) * T]
            ktmp = sb1("ktmp", [128, 1024], F32)
            phik = sb1("phik", [128, 1024], BF16)
            v_bf = sb1("v_bf", [128, 1024], BF16)
            phitmp = sb1("phitmp", [128, T], F32)

            kvacc = ps1("kvacc", [128, 512])
            ksum_ps = ps1("ksum_ps", [1, 1024])
            kvtok = ps1("kvtok", [128, 1024])
            qps = ps1("qps", [128, 1024])
            qpsl = lambda j: qps[:, T * (j % 2): T * (j % 2 + 1)]
            bps = kvtok[:, 0:T]
            srow = kvtok[0:1, :]
            xn1 = lambda par, k: (xn_bf if par == 0 else xn1b)[:, T * k: T * (k + 1)]

            def _(g):
                for k in range(KC):
                    ins = g.dma_start(out=qkvw[:, 3 * C * k: 3 * C * (k + 1)],
                                      in_=qkvwT[128 * k:128 * (k + 1), :])
                    ct.inc(ins, "G", 16)
                G["qkv"] = ct.cnt["G"]
            block.gpsimd(_)

            for half in range(2):
                def _(sync, half=half):
                    if half == 1:
                        ct.w(sync, "s", "V", V[("wmaxh", 0)])
                    ins = sync.dma_start(
                        out=wstage[:, :].rearrange("p (k n) -> p k n", n=C),
                        in_=ffnwT.rearrange("(k p) n -> p k n", p=128)[:, 4 * half:4 * half + 4, :])
                    S[("ffnw", half)] = ct.inc(ins, "W", 16)
                block.sync(_)

                def _(v, half=half):
                    ct.w(v, "v", "W", S[("ffnw", half)])
                    for k in range(4):
                        ct.w(v, "v", "V",
                             ct.inc(v.reduce_max(wgtmp[:, 0:1],
                                                 wstage[:, C * k: C * (k + 1)],
                                                 axis=AXX, apply_absolute_value=True),
                                    "V", 1))
                        if half == 0 and k == 0:
                            ins = v.tensor_copy(out=wamax[:, 0:1], in_=wgtmp[:, 0:1])
                        else:
                            ins = v.tensor_tensor(out=wamax[:, 0:1], in0=wamax[:, 0:1],
                                                  in1=wgtmp[:, 0:1], op=ALU.max)
                        ct.w(v, "v", "V", ct.inc(ins, "V", 1))
                    V[("wmaxh", half)] = ct.cnt["V"]
                block.vector(_)

            def _(sync):
                ct.w(sync, "s", "V", V[("wmaxh", 1)])
                ndma(sync, "s", "R", wam_d.rearrange("(p o) -> p o", o=1), wamax[:, 0:1])
                S["wrow"] = ndma(sync, "s", "R", wrow[:, :],
                                 wam_d.rearrange("(o p) -> o p", o=1))
            block.sync(_)

            def _(v):
                ct.w(v, "v", "R", S["wrow"])
                def step(f):
                    ct.w(v, "v", "V", ct.inc(f(), "V", 1))
                step(lambda: v.reduce_max(scr[0:1, 0:1], wrow[:, :], axis=AXX))
                step(lambda: v.tensor_scalar_max(scr[0:1, 0:1], scr[0:1, 0:1], 1e-12))
                step(lambda: v.reciprocal(scr[0:1, 1:2], scr[0:1, 0:1]))
                step(lambda: v.tensor_scalar(out=scr[0:1, 2:3], in0=scr[0:1, 1:2],
                                             scalar1=448.0, scalar2=448.0,
                                             op0=ALU.mult, op1=ALU.min))
                step(lambda: v.tensor_scalar_mul(scr[0:1, 3:4], scr[0:1, 2:3], 0.5))
                step(lambda: v.reciprocal(scr[0:1, 4:5], scr[0:1, 2:3]))
                V["wsc"] = ct.cnt["V"]
            block.vector(_)

            def _(tn):
                ct.w(tn, "t", "V", V["wsc"])
                ct.w(tn, "t", "S", S["consts"])
                ct.w(tn, "t", "R", S["onr"])
                ins = tn.matmul(kvacc[:, 0:1], onr[0:1, 0:128], scr[0:1, 3:4],
                                start=True, stop=True)
                P["swhrep"] = ct.inc(ins, "PE", 1)
            block.tensor(_)

            def _(a):
                ct.w(a, "a", "PE", P["swhrep"])
                ins = a.copy(swhrep[:, 0:1], kvacc[:, 0:1])
                A["swhrep"] = ct.inc(ins, "A", 1)
            block.scalar(_)

            for half in range(2):
                def _(sync, half=half):
                    ct.w(sync, "s", "V", V[("wmaxh", 1)] if half == 0 else V[("wq8h", 0)])
                    ins = sync.dma_start(
                        out=wstage[:, :].rearrange("p (k n) -> p k n", n=C),
                        in_=ffnwT.rearrange("(k p) n -> p k n", p=128)[:, 4 * half:4 * half + 4, :])
                    S[("ffnw2", half)] = ct.inc(ins, "W", 16)
                block.sync(_)

                def _(v, half=half):
                    ct.w(v, "v", "W", S[("ffnw2", half)])
                    ct.w(v, "v", "A", A["swhrep"])
                    for k in range(4):
                        ins = v.tensor_scalar(out=wq8[:, C * (4 * half + k): C * (4 * half + k + 1)],
                                              in0=wstage[:, C * k: C * (k + 1)],
                                              scalar1=swhrep[:, 0:1], scalar2=None,
                                              op0=ALU.mult)
                    V[("wq8h", half)] = ct.inc(ins, "V", 1)
                block.vector(_)
            block.flush()

            def _spill1(sync, t):
                ct.w(sync, "s", "A", A[("phi", t, KC - 1)])
                ins = sync.dma_start(
                    out=dram_tile(phisp, t),
                    in_=phq[:, :].rearrange("p (j n) -> p j n", n=T))
                S[("phisp", t)] = ct.inc(ins, "SP", 16)

            for t in range(NT):
                par = t % 2
                if t > 0:
                    def _(sync, t=t, par=par):
                        if t >= 2:
                            ct.w(sync, "s", "V", V[("xn", t - 2)])
                        ins = sync.dma_start(
                            out=xbuf[:, par * 4096:(par + 1) * 4096].rearrange(
                                "p (k n) -> p k n", n=T),
                            in_=dram_tile(xT, t))
                        S[("x1p", t)] = ct.inc(ins, "S", 16)
                        _spill1(sync, t - 1)
                    block.sync(_)

                for k in range(KC):
                    s = k % 2
                    def _(v, t=t, par=par, k=k, s=s):
                        ct.w(v, "v", "S", S[("x1p", t)])
                        if k >= 2:
                            ct.w(v, "v", "PE", P[("st", t, k - 2)])
                        elif t > 0:
                            ct.w(v, "v", "PE", P[("st", t - 1, 6 + k)])
                        v.tensor_copy(out=st1A(s), in_=xk(par, k))
                        ins = v.tensor_tensor(out=st1B(s), in0=st1A(s), in1=st1A(s),
                                              op=ALU.mult)
                        V[("st", t, k)] = ct.inc(ins, "V", 1)
                    block.vector(_)

                    def _(tn, t=t, k=k, s=s):
                        ct.w(tn, "t", "V", V[("st", t, k)])
                        if k == 0:
                            ct.w(tn, "t", "S", S["consts"])
                            if t > 0:
                                ct.w(tn, "t", "A", A[("vbf", t - 1, NB - 1)])
                                ct.w(tn, "t", "V", V[("rows_a", t - 1)])
                        tn.matmul(srow[:, 0:T], onc[:, 0:1], st1B(s),
                                  start=(k == 0), stop=(k == KC - 1))
                        ins = tn.matmul(srow[:, T:2 * T], onc[:, 0:1], st1A(s),
                                        start=(k == 0), stop=(k == KC - 1))
                        P[("st", t, k)] = ct.inc(ins, "PE", 1)
                    block.tensor(_)

                def _(v, t=t):
                    ct.w(v, "v", "PE", P[("st", t, KC - 1)])
                    if t > 0:
                        ct.w(v, "v", "R", S[("rowsd", t - 1)])
                        ct.w(v, "v", "A", A[("gp1", t - 1)])
                    v.tensor_scalar_mul(mtile[0:1, :], srow[0:1, T:2 * T], 1.0 / C)
                    v.tensor_tensor(out=rowt0[0:1, :], in0=mtile[0:1, :],
                                    in1=mtile[0:1, :], op=ALU.mult)
                    v.scalar_tensor_tensor(
                        out=vtile[0:1, :], in0=srow[0:1, 0:T],
                        scalar=1.0 / C, in1=rowt0[0:1, :], op0=ALU.mult, op1=ALU.subtract)
                    ins = v.tensor_scalar_add(rowt0[0:1, :], vtile[0:1, :], 1e-5)
                    V[("rows_a", t)] = ct.inc(ins, "V", 1)
                block.vector(_)

                def _(a, t=t):
                    ct.w(a, "a", "V", V[("rows_a", t)])
                    if t > 0:
                        ct.w(a, "a", "R", S[("rowsd", t - 1)])
                    a.activation(out=rowt1[0:1, :], in_=vtile[0:1, :],
                                 func=AF.Ln, scale=float(C))
                    ins = a.activation(out=rowt2[0:1, :], in_=rowt1[0:1, :],
                                       func=AF.Exp, scale=-0.5)
                    A[("gx0", t)] = ct.inc(ins, "A", 1)
                    a.activation(out=rowt1[0:1, :], in_=rowt0[0:1, :], func=AF.Ln)
                    ins = a.activation(out=gptile[0:1, :], in_=rowt1[0:1, :],
                                       func=AF.Exp, scale=-0.5)
                    A[("gp1", t)] = ct.inc(ins, "A", 1)
                block.scalar(_)

                def _(sync, t=t):
                    ct.w(sync, "s", "A", A[("gp1", t)])
                    ndma(sync, "s", "R", rowsD[0:1, T * t: T * (t + 1)], mtile[0:1, :])
                    S[("rowsd", t)] = ndma(sync, "s", "R",
                                           rowsD[1:2, T * t: T * (t + 1)], gptile[0:1, :])
                block.sync(_)

                def _(v, t=t):
                    ct.w(v, "v", "A", A[("gx0", t)])
                    if t > 0:
                        ct.w(v, "v", "PE", P[("gx", t - 1)])
                    ins = v.tensor_scalar_mul(rowb[0:1, :], rowt2[0:1, :], SQC)
                    V[("rows_b", t)] = ct.inc(ins, "V", 1)
                block.vector(_)

                def _(tn, t=t):
                    ct.w(tn, "t", "V", V[("rows_a", t)])
                    ct.w(tn, "t", "R", S["onr"])
                    ins = tn.matmul(bps, onrb[0:1, 0:128], mtile[0:1, :],
                                    start=True, stop=True)
                    P[("mb", t)] = ct.inc(ins, "PE", 1)
                block.tensor(_)

                def _(a, t=t):
                    ct.w(a, "a", "PE", P[("mb", t)])
                    ins = a.copy(bc0[:, :], bps)
                    A[("mb", t)] = ct.inc(ins, "A", 1)
                block.scalar(_)

                def _(tn, t=t):
                    ct.w(tn, "t", "V", V[("rows_b", t)])
                    ct.w(tn, "t", "A", A[("mb", t)])
                    ins = tn.matmul(bps, onrb[0:1, 0:128], rowb[0:1, :],
                                    start=True, stop=True)
                    P[("gx", t)] = ct.inc(ins, "PE", 1)
                block.tensor(_)

                def _(a, t=t):
                    ct.w(a, "a", "PE", P[("gx", t)])
                    ins = a.copy(bc1[:, :], bps)
                    A[("gx", t)] = ct.inc(ins, "A", 1)
                block.scalar(_)

                def _(v, t=t, par=par):
                    ct.w(v, "v", "A", A[("gx", t)])
                    if t >= 2:
                        ct.w(v, "v", "PE", P[("q", t - 2, KC - 1)])
                    for k in range(KC):
                        v.tensor_sub(tsub[:, :], xk(par, k), bc0[:, :])
                        ins = v.scalar_tensor_tensor(out=xn1(par, k), in0=tsub[:, :],
                                                     scalar=s1sb[:, k:k + 1], in1=bc1[:, :],
                                                     op0=ALU.mult, op1=ALU.mult)
                    V[("xn", t)] = ct.inc(ins, "V", 1)
                block.vector(_)

                for b in range(NB):
                    def _(tn, t=t, b=b, par=par):
                        ct.w(tn, "t", "V", V[("xn", t)])
                        if b == 0:
                            ct.w(tn, "t", "A", A[("gx", t)])
                        else:
                            ct.w(tn, "t", "A", A[("vbf", t, b - 1)])
                        for k in range(KC):
                            lhs = xn1(par, 0)[0:0, 0:0]
                            lhs = (xn_bf if par == 0 else xn1b)[:, T * k + 128 * b: T * k + 128 * (b + 1)]
                            tn.matmul(kvtok[:, 0:512], lhs,
                                      qkvw[:, 3 * C * k + C: 3 * C * k + C + 512],
                                      start=(k == 0), stop=(k == KC - 1))
                            ins = tn.matmul(kvtok[:, 512:1024], lhs,
                                            qkvw[:, 3 * C * k + C + 512: 3 * C * k + 2 * C],
                                            start=(k == 0), stop=(k == KC - 1))
                        P[("ktok", t, b)] = ct.inc(ins, "PE", 1)
                    block.tensor(_)

                    def _(a, t=t, b=b):
                        ct.w(a, "a", "PE", P[("ktok", t, b)])
                        ins = a.activation(out=ktmp[:, :], in_=kvtok[:, :], func=AF.Square)
                        A[("ksq", t, b)] = ct.inc(ins, "A", 1)
                        if (t, b) != (0, 0):
                            pt, pb = (t, b - 1) if b > 0 else (t - 1, NB - 1)
                            ct.w(a, "a", "PE", P[("kv", pt, pb)])
                        ins = a.activation(out=phik[:, :], in_=ktmp[:, :],
                                           func=AF.Exp, scale=-0.5)
                        A[("phik", t, b)] = ct.inc(ins, "A", 1)
                    block.scalar(_)

                    def _(tn, t=t, b=b, par=par):
                        ct.w(tn, "t", "A", A[("ksq", t, b)])
                        for k in range(KC):
                            lhs = (xn_bf if par == 0 else xn1b)[:, T * k + 128 * b: T * k + 128 * (b + 1)]
                            tn.matmul(kvtok[:, 0:512], lhs,
                                      qkvw[:, 3 * C * k + 2 * C: 3 * C * k + 2 * C + 512],
                                      start=(k == 0), stop=(k == KC - 1))
                            ins = tn.matmul(kvtok[:, 512:1024], lhs,
                                            qkvw[:, 3 * C * k + 2 * C + 512: 3 * C * (k + 1)],
                                            start=(k == 0), stop=(k == KC - 1))
                        P[("vtok", t, b)] = ct.inc(ins, "PE", 1)
                    block.tensor(_)

                    def _(a, t=t, b=b):
                        ct.w(a, "a", "PE", P[("vtok", t, b)])
                        ins = a.copy(v_bf[:, :], kvtok[:, :])
                        A[("vbf", t, b)] = ct.inc(ins, "A", 1)
                    block.scalar(_)

                    def _(tn, t=t, b=b):
                        ct.w(tn, "t", "A", A[("phik", t, b)])
                        ct.w(tn, "t", "A", A[("vbf", t, b)])
                        first = (t == 0 and b == 0)
                        last = (t == NT - 1 and b == NB - 1)
                        for p in range(8):
                            h0, h1 = 2 * p, 2 * p + 1
                            tn.matmul(kvacc[0:64, 64 * p:64 * (p + 1)],
                                      phik[:, 64 * h0:64 * (h0 + 1)],
                                      v_bf[:, 64 * h0:64 * (h0 + 1)],
                                      start=first, stop=last, tile_position=(0, 0))
                            tn.matmul(kvacc[64:128, 64 * p:64 * (p + 1)],
                                      phik[:, 64 * h1:64 * (h1 + 1)],
                                      v_bf[:, 64 * h1:64 * (h1 + 1)],
                                      start=first, stop=last, tile_position=(0, 64))
                        tn.matmul(ksum_ps[0:1, 0:512], onc[:, 0:1], phik[:, 0:512],
                                  start=first, stop=last)
                        ins = tn.matmul(ksum_ps[0:1, 512:1024], onc[:, 0:1],
                                        phik[:, 512:1024], start=first, stop=last)
                        P[("kv", t, b)] = ct.inc(ins, "PE", 1)
                    block.tensor(_)

                for j in range(KC):
                    def _(tn, t=t, j=j, par=par):
                        ct.w(tn, "t", "V", V[("xn", t)])
                        ct.w(tn, "t", "G", G["qkv"])
                        if j < 2:
                            if t > 0:
                                ct.w(tn, "t", "A", A[("sqr", t - 1, 6 + j)])
                        else:
                            ct.w(tn, "t", "A", A[("sqr", t, j - 2)])
                        for k in range(KC):
                            ins = tn.matmul(
                                qpsl(j),
                                qkvw[:, 3 * C * k + 128 * j: 3 * C * k + 128 * (j + 1)],
                                xn1(par, k), start=(k == 0), stop=(k == KC - 1))
                        P[("q", t, j)] = ct.inc(ins, "PE", 1)
                    block.tensor(_)

                    def _(a, t=t, j=j, par=par):
                        ct.w(a, "a", "PE", P[("q", t, j)])
                        ins = a.activation(out=phitmp[:, :], in_=qpsl(j), func=AF.Square)
                        A[("sqr", t, j)] = ct.inc(ins, "A", 1)
                        if t >= 1 and j == 0:
                            ct.w(a, "a", "SP", S[("phisp", t - 1)])
                        ins = a.activation(out=phqk(j), in_=phitmp[:, :],
                                           func=AF.Exp, scale=-0.5)
                        A[("phi", t, j)] = ct.inc(ins, "A", 1)
                    block.scalar(_)

                block.flush()

            def _(v):
                ct.w(v, "v", "PE", P[("kv", NT - 1, NB - 1)])
                v.tensor_copy(out=kv_bf[:, :], in_=kvacc[:, :])
                v.tensor_copy(out=ksrow[:, :], in_=ksum_ps[0:1, :])
                ins = v.memset(ksblk[:, :], 0.0)
                V["kvevac"] = ct.inc(ins, "V", 1)
            block.vector(_)

            def _(sync):
                _spill1(sync, NT - 1)
                ct.w(sync, "s", "V", V["kvevac"])
                S["ksum_d"] = ndma(sync, "s", "R", ksum_d.rearrange("(o n) -> o n", o=1),
                                   ksrow[:, :])
            block.sync(_)

            def _(g):
                ct.w(g, "g", "R", S["ksum_d"])
                ct.w(g, "g", "V", V["kvevac"])
                ndma(g, "g", "GN", ksblk[0:64, 0:127:18],
                     ksum_d.rearrange("(p d) -> d p", p=8)[0:64, :],
                     allow_slow_non_contiguous=True)
                G["ksblk"] = ndma(g, "g", "GN", ksblk[64:128, 1:128:18],
                                  ksum_d.rearrange("(p d) -> d p", p=8)[64:128, :],
                                  allow_slow_non_contiguous=True)
            block.gpsimd(_)
            block.flush()

        if stage < 2:
            def _(sync):
                ct.w(sync, "s", "GN", G["ksblk"])
                ct.w(sync, "s", "S", ct.cnt["S"])
            block.sync(_)
            block.flush()
            return nc

        with contextlib.ExitStack() as p2:
            sb2 = lambda n_, s_, d_: p2.enter_context(nc.sbuf_tensor(n_, s_, d_))
            ps2 = lambda n_, s_: p2.enter_context(nc.psum_tensor(n_, s_, F32))

            conv1w = sb2("conv1w", [128, KC * HID], BF16)
            conv2w = sb2("conv2w", [128, HC * C], BF16)
            lnbuf = sb2("lnbuf", [128, 4096], F32)
            hdd = sb2("hdd", [128, HC * T], BF16)
            zrow_b = sb2("zrow_b", [16, T], BF16)
            qz = sb2("qz", [128, 2 * T], BF16)
            out_bf = sb2("out_bf", [128, 4096], BF16)

            bc2 = sb2("bc2", [128, T], F32)
            bc3 = sb2("bc3", [128, T], F32)
            den16 = ps2("den16", [16, 512])
            zqps = ps2("zqps", [128, 1024])
            outps = ps2("outps", [128, 1024])
            projps = ps2("projps", [128, 1024])
            zql = lambda p: zqps[:, 512 * (p % 2): 512 * (p % 2 + 1)]
            outl = lambda i: outps[:, 512 * (i % 2): 512 * (i % 2 + 1)]
            projl = lambda i: projps[:, 512 * (i % 2): 512 * (i % 2 + 1)]
            bps2 = projps[:, 0:T]
            r2row = den16[0:1, 0:T]
            srow2 = zqps[0:1, :]

            def _(g):
                ct.w(g, "g", "PE", P[("kv", NT - 1, NB - 1)])
                ct.w(g, "g", "V", V[("wq8h", 1)])
                for k in range(KC):
                    ct.inc(g.dma_start(out=conv1w[:, HID * k: HID * (k + 1)],
                                       in_=conv1wT[128 * k:128 * (k + 1), :]), "G", 16)
                for h in range(HC):
                    ins = g.dma_start(out=conv2w[:, C * h: C * (h + 1)],
                                      in_=conv2wT[128 * h:128 * (h + 1), :])
                    ct.inc(ins, "G", 16)
                G["conv"] = ct.cnt["G"]
            block.gpsimd(_)
            block.flush()

            def _ln2amax(v, t, par):
                ct.w(v, "v", "A", A[("i2b", t)])
                ct.w(v, "v", "SP", S[("x1sp", t)])
                for k in range(KC):
                    v.tensor_sub(tsub[:, :], xk(par, k), bc2[:, :])
                    ins = v.tensor_mul(xk(par, k), tsub[:, :], bc3[:, :])
                V[("ln2", t)] = ct.inc(ins, "V", 1)
                if t == 0:
                    ins = v.reduce_max(amax_acc[:, 0:1],
                                       xbuf[:, par * 4096:(par + 1) * 4096],
                                       axis=AXX, apply_absolute_value=True)
                else:
                    ct.w(v, "v", "V",
                         ct.inc(v.reduce_max(ggtmp2[:, 0:1],
                                             xbuf[:, par * 4096:(par + 1) * 4096],
                                             axis=AXX, apply_absolute_value=True),
                                "V", 1))
                    ins = v.tensor_tensor(out=amax_acc[:, 0:1], in0=amax_acc[:, 0:1],
                                          in1=ggtmp2[:, 0:1], op=ALU.max)
                V[("am", t)] = ct.inc(ins, "V", 1)

            def _spill2(sync, t, par):
                ct.w(sync, "s", "V", V[("x1", t, KC - 1)])
                ins = sync.dma_start(
                    out=dram_tile(x1sp, t),
                    in_=xbuf[:, par * 4096:(par + 1) * 4096].rearrange(
                        "p (k n) -> p k n", n=T))
                S[("x1sp", t)] = ct.inc(ins, "SP", 16)
                ct.w(sync, "s", "V", V[("ln2", t)])
                ins = sync.dma_start(
                    out=dram_tile(ln2sp, t),
                    in_=xbuf[:, par * 4096:(par + 1) * 4096].rearrange(
                        "p (k n) -> p k n", n=T))
                S[("ln2sp", t)] = ct.inc(ins, "SP", 16)

            for t in range(NT):
                par = t % 2

                def _(sync, t=t, par=par):
                    if t >= 2:
                        ct.w(sync, "s", "SP", S[("ln2sp", t - 2)])
                        ct.w(sync, "s", "V", V[("am", t - 2)])
                    else:
                        ct.w(sync, "s", "V", V[("xn", NT - 2 + t)])
                    ins = sync.dma_start(
                        out=xbuf[:, par * 4096:(par + 1) * 4096].rearrange(
                            "p (k n) -> p k n", n=T),
                        in_=dram_tile(xT, t))
                    S[("x2p", t)] = ct.inc(ins, "S", 16)
                    if t >= 1:
                        ct.w(sync, "s", "V", V[("qz", t - 1, 7)])
                    else:
                        ct.w(sync, "s", "SP", S[("phisp", NT - 1)])
                    ins = sync.dma_start(
                        out=phq[:, :].rearrange("p (j n) -> p j n", n=T),
                        in_=dram_tile(phisp, t))
                    S[("phi2", t)] = ct.inc(ins, "S", 16)
                    if t > 0:
                        ct.w(sync, "s", "PE", P[("i2b", t - 1)])
                    else:
                        ct.w(sync, "s", "PE", P[("gx", NT - 1)])
                    if t == 0:
                        ct.w(sync, "s", "R", S[("rowsd", NT - 1)])
                    ndma(sync, "s", "R", mtile[0:1, :], rowsD[0:1, T * t: T * (t + 1)])
                    S[("rows2", t)] = ndma(sync, "s", "R", gptile[0:1, :],
                                           rowsD[1:2, T * t: T * (t + 1)])
                    if t > 0:
                        _spill2(sync, t - 1, (t - 1) % 2)
                block.sync(_)

                def _(tn, t=t, par=par):
                    ct.w(tn, "t", "S", S[("phi2", t)])
                    ct.w(tn, "t", "GN", G["ksblk"])
                    if t > 0:
                        ct.w(tn, "t", "V", V[("z", t - 1)])
                        ct.w(tn, "t", "V", V[("f2", t - 1)])
                    for p in range(8):
                        ins = tn.matmul(den16[:, :], ksblk[:, 16 * p:16 * (p + 1)],
                                        phqk(p), start=(p == 0), stop=(p == 7))
                    P[("den", t)] = ct.inc(ins, "PE", 1)
                block.tensor(_)

                def _(v, t=t):
                    ct.w(v, "v", "PE", P[("den", t)])
                    v.tensor_scalar_add(tsub[0:16, :], den16[:, :], 1e-6)
                    v.reciprocal(tsub[0:16, :], tsub[0:16, :])
                    ins = v.tensor_copy(out=zrow_b[:, :], in_=tsub[0:16, :])
                    V[("z", t)] = ct.inc(ins, "V", 1)
                block.vector(_)

                for p in range(8):
                    def _(tn, t=t, p=p):
                        ct.w(tn, "t", "V", V[("z", t)])
                        if p < 2:
                            if t > 0:
                                ct.w(tn, "t", "V", V[("rows2", t - 1)])
                            ct.w(tn, "t", "GN", G["rows_c"])
                        else:
                            ct.w(tn, "t", "V", V[("qz", t, p - 2)])
                        ins = tn.matmul(zql(p), ind16[:, 128 * p:128 * (p + 1)],
                                        zrow_b[:, :], start=True, stop=True)
                        P[("zq", t, p)] = ct.inc(ins, "PE", 1)
                    block.tensor(_)

                    def _(v, t=t, p=p, par=par):
                        ct.w(v, "v", "PE", P[("zq", t, p)])
                        if p >= 2:
                            ct.w(v, "v", "PE", P[("out", t, p - 2)])
                        ins = v.tensor_mul(qz[:, T * (p % 2): T * (p % 2 + 1)],
                                           phqk(p), zql(p))
                        V[("qz", t, p)] = ct.inc(ins, "V", 1)
                    block.vector(_)

                    def _(tn, t=t, p=p):
                        ct.w(tn, "t", "V", V[("qz", t, p)])
                        ct.w(tn, "t", "V", V["kvevac"])
                        if p < 2 and t > 0:
                            ct.w(tn, "t", "V", V[("x1", t - 1, 6 + p)])
                        if p >= 2:
                            ct.w(tn, "t", "A", A[("outbf", t, p - 2)])
                        sl = qz[:, T * (p % 2): T * (p % 2 + 1)]
                        tn.matmul(outl(p)[0:64, :], kv_bf[0:64, 64 * p:64 * (p + 1)],
                                  sl[0:64, :], start=True, stop=True, tile_position=(0, 0))
                        ins = tn.matmul(outl(p)[64:128, :], kv_bf[64:128, 64 * p:64 * (p + 1)],
                                        sl[64:128, :], start=True, stop=True,
                                        tile_position=(64, 64))
                        P[("out", t, p)] = ct.inc(ins, "PE", 1)
                    block.tensor(_)

                    def _(a, t=t, p=p):
                        ct.w(a, "a", "PE", P[("out", t, p)])
                        ins = a.copy(out_bf[:, T * p: T * (p + 1)], outl(p))
                        A[("outbf", t, p)] = ct.inc(ins, "A", 1)
                    block.scalar(_)

                def _(tn, t=t):
                    ct.w(tn, "t", "R", S[("rows2", t)])
                    if t > 0:
                        ct.w(tn, "t", "A", A[("i2b", t - 1)])
                        ct.w(tn, "t", "V", V[("xn2", t - 1)])
                        ct.w(tn, "t", "V", V[("ln", t - 1)])
                    ins = tn.matmul(bps2, onrb[0:1, 0:128], mtile[0:1, :],
                                    start=True, stop=True)
                    P[("mb2", t)] = ct.inc(ins, "PE", 1)
                block.tensor(_)

                def _(a, t=t):
                    ct.w(a, "a", "PE", P[("mb2", t)])
                    ins = a.copy(bc0[:, :], bps2)
                    A[("mb2", t)] = ct.inc(ins, "A", 1)
                block.scalar(_)

                def _(tn, t=t):
                    ct.w(tn, "t", "A", A[("mb2", t)])
                    ins = tn.matmul(bps2, onrb[0:1, 0:128], gptile[0:1, :],
                                    start=True, stop=True)
                    P[("gpb", t)] = ct.inc(ins, "PE", 1)
                block.tensor(_)

                def _(a, t=t):
                    ct.w(a, "a", "PE", P[("gpb", t)])
                    ins = a.copy(bc1[:, :], bps2)
                    A[("gpb", t)] = ct.inc(ins, "A", 1)
                block.scalar(_)

                def _(v, t=t, par=par):
                    ct.w(v, "v", "A", A[("gpb", t)])
                    ct.w(v, "v", "S", S[("x2p", t)])
                    for k in range(KC):
                        v.tensor_sub(tsub[:, :], xk(par, k), bc0[:, :])
                        ins = v.tensor_mul(lnbuf[:, T * k: T * (k + 1)], tsub[:, :],
                                           bc1[:, :])
                    V[("ln", t)] = ct.inc(ins, "V", 1)
                block.vector(_)

                for j in range(KC):
                    def _(tn, t=t, j=j):
                        ct.w(tn, "t", "A", A[("outbf", t, 7)])
                        ct.w(tn, "t", "G", G["pw"])
                        ct.w(tn, "t", "GN", G["rows_c"])
                        if j == 0:
                            ct.w(tn, "t", "A", A[("gpb", t)])
                        if j >= 2:
                            ct.w(tn, "t", "V", V[("u", t, j - 2)])
                        for k in range(KC):
                            tn.matmul(projl(j),
                                      projw[:, C * k + 128 * j: C * k + 128 * (j + 1)],
                                      out_bf[:, T * k: T * (k + 1)],
                                      start=(k == 0), stop=False)
                        ins = tn.matmul(projl(j), prow[0:1, 128 * j:128 * (j + 1)],
                                        onr5[0:1, :], start=False, stop=True)
                        P[("proj", t, j)] = ct.inc(ins, "PE", 1)
                    block.tensor(_)

                    def _(v, t=t, j=j):
                        ct.w(v, "v", "PE", P[("proj", t, j)])
                        ins = v.scalar_tensor_tensor(
                            out=lnbuf[:, T * j: T * (j + 1)], in0=projl(j),
                            scalar=gmsb[:, j:j + 1], in1=lnbuf[:, T * j: T * (j + 1)],
                            op0=ALU.mult, op1=ALU.add)
                        V[("u", t, j)] = ct.inc(ins, "V", 1)
                    block.vector(_)

                for k in range(KC):
                    s = k % 2
                    def _(a, t=t, k=k, s=s):
                        ct.w(a, "a", "V", V[("u", t, k)])
                        if k >= 2:
                            ct.w(a, "a", "PE", P[("r2", t, k - 2)])
                        elif t > 0:
                            ct.w(a, "a", "PE", P[("st2", t - 1, 6 + k)])
                        ins = a.activation(out=stAB[:, T * s: T * (s + 1)],
                                           in_=lnbuf[:, T * k: T * (k + 1)],
                                           func=AF.Square)
                        A[("squ", t, k)] = ct.inc(ins, "A", 1)
                    block.scalar(_)

                    def _(tn, t=t, k=k, s=s):
                        ct.w(tn, "t", "A", A[("squ", t, k)])
                        if k == 0:
                            ct.w(tn, "t", "V", V[("z", t)])
                        ins = tn.matmul(r2row, onc[:, 0:1], stAB[:, T * s: T * (s + 1)],
                                        start=(k == 0), stop=(k == KC - 1))
                        P[("r2", t, k)] = ct.inc(ins, "PE", 1)
                    block.tensor(_)

                def _(a, t=t):
                    ct.w(a, "a", "PE", P[("r2", t, KC - 1)])
                    if t > 0:
                        ct.w(a, "a", "PE", P[("f2b", t - 1)])
                    a.activation(out=rowt1[0:1, :], in_=r2row, func=AF.Ln)
                    ins = a.activation(out=rowb[0:1, :], in_=rowt1[0:1, :],
                                       func=AF.Exp, scale=-0.5)
                    A[("f20", t)] = ct.inc(ins, "A", 1)
                block.scalar(_)

                def _(v, t=t):
                    ct.w(v, "v", "A", A[("f20", t)])
                    if t > 0:
                        ct.w(v, "v", "PE", P[("f2b", t - 1)])
                    ins = v.tensor_scalar_mul(rowb[0:1, :], rowb[0:1, :], SQC)
                    V[("f2", t)] = ct.inc(ins, "V", 1)
                block.vector(_)

                def _(tn, t=t):
                    ct.w(tn, "t", "V", V[("f2", t)])
                    ct.w(tn, "t", "V", V[("u", t, KC - 1)])
                    ins = tn.matmul(bps2, onrb[0:1, 0:128], rowb[0:1, :],
                                    start=True, stop=True)
                    P[("f2b", t)] = ct.inc(ins, "PE", 1)
                block.tensor(_)

                def _(a, t=t):
                    ct.w(a, "a", "PE", P[("f2b", t)])
                    ins = a.copy(bc0[:, :], bps2)
                    A[("f2b", t)] = ct.inc(ins, "A", 1)
                block.scalar(_)

                def _(v, t=t):
                    ct.w(v, "v", "A", A[("f2b", t)])
                    for k in range(KC):
                        ins = v.scalar_tensor_tensor(
                            out=xnk(k), in0=lnbuf[:, T * k: T * (k + 1)],
                            scalar=s2sb[:, k:k + 1], in1=bc0[:, :],
                            op0=ALU.mult, op1=ALU.mult)
                    V[("xn2", t)] = ct.inc(ins, "V", 1)
                block.vector(_)

                for h in range(HC):
                    def _(tn, t=t, h=h):
                        ct.w(tn, "t", "V", V[("xn2", t)])
                        ct.w(tn, "t", "G", G["conv"])
                        if h >= 2:
                            ct.w(tn, "t", "A", A[("hdd", t, h - 2)])
                        else:
                            ct.w(tn, "t", "A", A[("outbf", t, 6 + h)])
                        for k in range(KC):
                            ins = tn.matmul(
                                outl(h),
                                conv1w[:, HID * k + 128 * h: HID * k + 128 * (h + 1)],
                                xnk(k), start=(k == 0), stop=(k == KC - 1))
                        P[("c1", t, h)] = ct.inc(ins, "PE", 1)
                    block.tensor(_)

                    def _(a, t=t, h=h):
                        ct.w(a, "a", "PE", P[("c1", t, h)])
                        ins = a.activation(out=hdd[:, T * h: T * (h + 1)], in_=outl(h),
                                           func=AF.Gelu, bias=c1bsb[:, h:h + 1])
                        A[("hdd", t, h)] = ct.inc(ins, "A", 1)
                    block.scalar(_)

                if t >= 1:
                    def _(v, t=t):
                        _ln2amax(v, t - 1, (t - 1) % 2)
                    block.vector(_)

                for o in range(KC):
                    def _(tn, t=t, o=o):
                        ct.w(tn, "t", "A", A[("hdd", t, HC - 1)])
                        if o >= 2:
                            ct.w(tn, "t", "V", V[("x1", t, o - 2)])
                        for h in range(HC):
                            ins = tn.matmul(outl(o),
                                            conv2w[:, C * h + 128 * o: C * h + 128 * (o + 1)],
                                            hdd[:, T * h: T * (h + 1)],
                                            start=(h == 0), stop=(h == HC - 1))
                        P[("c2", t, o)] = ct.inc(ins, "PE", 1)
                    block.tensor(_)

                    def _(v, t=t, o=o, par=par):
                        ct.w(v, "v", "PE", P[("c2", t, o)])
                        ct.w(v, "v", "V", V["gc2b"])
                        v.scalar_tensor_tensor(
                            out=lnbuf[:, T * o: T * (o + 1)], in0=outl(o),
                            scalar=gmsb[:, o:o + 1], in1=lnbuf[:, T * o: T * (o + 1)],
                            op0=ALU.mult, op1=ALU.add)
                        ins = v.scalar_tensor_tensor(
                            out=xk(par, o),
                            in0=lnbuf[:, T * o: T * (o + 1)],
                            scalar=gc2b[:, o:o + 1], in1=xk(par, o),
                            op0=ALU.add, op1=ALU.add)
                        V[("x1", t, o)] = ct.inc(ins, "V", 1)
                    block.vector(_)

                for k in range(KC):
                    s = k % 2
                    def _(a, t=t, k=k, s=s):
                        ct.w(a, "a", "V", V[("x1", t, k)])
                        if k >= 2:
                            ct.w(a, "a", "PE", P[("st2", t, k - 2)])
                        else:
                            ct.w(a, "a", "PE", P[("r2", t, 6 + k)])
                        ins = a.activation(out=stAB[:, T * s: T * (s + 1)],
                                           in_=xk(t % 2, k),
                                           func=AF.Square)
                        A[("st2", t, k)] = ct.inc(ins, "A", 1)
                    block.scalar(_)

                    def _(tn, t=t, k=k, s=s):
                        ct.w(tn, "t", "A", A[("st2", t, k)])
                        ct.w(tn, "t", "V", V[("x1", t, k)])
                        if k == 0:
                            ct.w(tn, "t", "V", V[("qz", t, 7)])
                        tn.matmul(srow2[:, 0:T], onc[:, 0:1], stAB[:, T * s: T * (s + 1)],
                                  start=(k == 0), stop=(k == KC - 1))
                        ins = tn.matmul(srow2[:, T:2 * T], oncf[:, 0:1],
                                        xk(t % 2, k),
                                        start=(k == 0), stop=(k == KC - 1))
                        P[("st2", t, k)] = ct.inc(ins, "PE", 1)
                    block.tensor(_)

                def _(v, t=t):
                    ct.w(v, "v", "PE", P[("st2", t, KC - 1)])
                    v.tensor_scalar_mul(mtile[0:1, :], srow2[0:1, T:2 * T], 1.0 / C)
                    v.tensor_tensor(out=rowt0[0:1, :], in0=mtile[0:1, :],
                                    in1=mtile[0:1, :], op=ALU.mult)
                    v.scalar_tensor_tensor(out=rowt0[0:1, :], in0=srow2[0:1, 0:T],
                                            scalar=1.0 / C, in1=rowt0[0:1, :],
                                            op0=ALU.mult, op1=ALU.subtract)
                    ins = v.tensor_scalar_add(rowt0[0:1, :], rowt0[0:1, :], 1e-5)
                    V[("rows2", t)] = ct.inc(ins, "V", 1)
                block.vector(_)

                def _(a, t=t):
                    ct.w(a, "a", "V", V[("rows2", t)])
                    a.activation(out=rowt1[0:1, :], in_=rowt0[0:1, :], func=AF.Ln)
                    ins = a.activation(out=rowt1[0:1, :], in_=rowt1[0:1, :],
                                       func=AF.Exp, scale=-0.5)
                    A[("i2", t)] = ct.inc(ins, "A", 1)
                block.scalar(_)

                def _(tn, t=t):
                    ct.w(tn, "t", "V", V[("rows2", t)])
                    ct.w(tn, "t", "A", A[("f2b", t)])
                    if t > 0:
                        ct.w(tn, "t", "V", V[("ln2", t - 1)])
                    ins = tn.matmul(bps2, onrb[0:1, 0:128], mtile[0:1, :],
                                    start=True, stop=True)
                    P[("m2b", t)] = ct.inc(ins, "PE", 1)
                block.tensor(_)

                def _(a, t=t):
                    ct.w(a, "a", "PE", P[("m2b", t)])
                    ins = a.copy(bc2[:, :], bps2)
                    A[("m2b", t)] = ct.inc(ins, "A", 1)
                block.scalar(_)

                def _(tn, t=t):
                    ct.w(tn, "t", "A", A[("i2", t)])
                    ct.w(tn, "t", "A", A[("m2b", t)])
                    ins = tn.matmul(bps2, onr[0:1, 0:128], rowt1[0:1, :],
                                    start=True, stop=True)
                    P[("i2b", t)] = ct.inc(ins, "PE", 1)
                block.tensor(_)

                def _(a, t=t):
                    ct.w(a, "a", "PE", P[("i2b", t)])
                    ins = a.copy(bc3[:, :], bps2)
                    A[("i2b", t)] = ct.inc(ins, "A", 1)
                block.scalar(_)

                block.flush()

            def _(v):
                _ln2amax(v, NT - 1, (NT - 1) % 2)
            block.vector(_)

            def _(sync):
                _spill2(sync, NT - 1, (NT - 1) % 2)
            block.sync(_)
            block.flush()

        if stage < 3:
            def _(sync):
                ct.w(sync, "s", "S", ct.cnt["S"])
                ct.w(sync, "s", "V", V[("am", NT - 1)])
            block.sync(_)
            block.flush()
            return nc

        def _(g):
            ct.w(g, "g", "V", V[("am", NT - 1)])
            ins = g.dma_start(out=ccin.ap().rearrange("o p -> p o"), in_=amax_acc[:, 0:1],
                              allow_slow_non_contiguous=True)
            mk = ct.inc(ins, "G", 16)
            ct.w(g, "g", "G", mk)
            ins = g.collective_compute(
                "AllReduce", ALU.max, replica_groups=[list(range(8))],
                ins=[ccin.ap().opt()], outs=[ccout.ap().opt()])
            ct.inc(ins, "CC", 1)
            ct.w(g, "g", "CC", ct.cnt["CC"])
            G["crow"] = ndma(g, "g", "GN", crow[:, :], ccout.ap()[0:1, :])
        block.gpsimd(_)

        def _(v):
            ct.w(v, "v", "GN", G["crow"])
            def step(f):
                ct.w(v, "v", "V", ct.inc(f(), "V", 1))
            step(lambda: v.reduce_max(scr[0:1, 5:6], crow[:, :], axis=AXX))
            step(lambda: v.tensor_scalar_max(scr[0:1, 5:6], scr[0:1, 5:6], 1e-12))
            step(lambda: v.reciprocal(scr[0:1, 6:7], scr[0:1, 5:6]))
            step(lambda: v.tensor_scalar(out=scr[0:1, 6:7], in0=scr[0:1, 6:7],
                                         scalar1=448.0, scalar2=448.0,
                                         op0=ALU.mult, op1=ALU.min))
            step(lambda: v.tensor_scalar_mul(scr[0:1, 7:8], scr[0:1, 6:7], 0.5))
            step(lambda: v.reciprocal(scr[0:1, 5:6], scr[0:1, 6:7]))
            step(lambda: v.tensor_tensor(out=scr[0:1, 5:6], in0=scr[0:1, 5:6],
                                         in1=scr[0:1, 4:5], op=ALU.mult))
            step(lambda: v.tensor_scalar_mul(scr[0:1, 5:6], scr[0:1, 5:6], 4.0))
            V["xsc"] = ct.cnt["V"]
        block.vector(_)
        block.flush()

        with contextlib.ExitStack() as p3:
            sb3 = lambda n_, s_, d_: p3.enter_context(nc.sbuf_tensor(n_, s_, d_))
            ps3 = lambda n_, s_: p3.enter_context(nc.psum_tensor(n_, s_, F32))

            l2buf = sb3("l2buf", [128, 2 * 4096], F32)
            xq8 = sb3("xq8", [128, 2 * 4096], FP8)
            ffst = sb3("ffst", [128, 4 * T], BF16)
            ffps = ps3("ffps", [128, 2048])
            screp_ps = ps3("screp_ps", [128, 2])
            ffl = lambda o: ffps[:, T * (o % 4): T * (o % 4 + 1)]

            def _(tn):
                ct.w(tn, "t", "V", V["xsc"])
                tn.matmul(screp_ps[:, 0:1], onr[0:1, 0:128], scr[0:1, 7:8],
                          start=True, stop=False)
                ins = tn.matmul(screp_ps[:, 1:2], onr[0:1, 0:128], scr[0:1, 5:6],
                                start=False, stop=True)
                P["screp"] = ct.inc(ins, "PE", 1)
            block.tensor(_)

            def _(a):
                ct.w(a, "a", "PE", P["screp"])
                ins = a.copy(screp[:, :], screp_ps[:, :])
                A["screp"] = ct.inc(ins, "A", 1)
            block.scalar(_)
            block.flush()

            if taps:
                def _(sync):
                    ct.w(sync, "s", "V", V["xsc"])
                    ndma(sync, "s", "R", tap_sc[:, :], scr[:, :])
                block.sync(_)

                def _(g):
                    ct.w(g, "g", "V", V["kvevac"])
                    ins = g.dma_start(out=tap_kv[:, :], in_=kv_bf[:, :])
                    ct.inc(ins, "G", 16)
                block.gpsimd(_)

            def _q8(tq, parq):
                def _(v, t=tq, par=parq):
                    ct.w(v, "v", "S", S[("l2", t)])
                    ct.w(v, "v", "A", A["screp"])
                    if t >= 2:
                        ct.w(v, "v", "PE", P[("ffn", t - 2, KC - 1)])
                    for k in range(KC):
                        ins = v.tensor_scalar(
                            out=xq8[:, par * 4096 + T * k: par * 4096 + T * (k + 1)],
                            in0=l2buf[:, par * 4096 + T * k: par * 4096 + T * (k + 1)],
                            scalar1=screp[:, 0:1], scalar2=None, op0=ALU.mult)
                    V[("q8", t)] = ct.inc(ins, "V", 1)
                block.vector(_)

            def _ystore(sync, t, par):
                ct.w(sync, "s", "V", V[("y", t, KC - 1)])
                ins = sync.dma_start(
                    out=dram_tile(yT, t),
                    in_=xbuf[:, par * 4096:(par + 1) * 4096].rearrange(
                        "p (k n) -> p k n", n=T))
                S[("y", t)] = ct.inc(ins, "SP", 16)

            for t in range(NT):
                par = t % 2

                def _(sync, t=t, par=par):
                    if t >= 2:
                        ct.w(sync, "s", "SP", S[("y", t - 2)])
                    else:
                        ct.w(sync, "s", "SP", S[("ln2sp", NT - 1)])
                        ct.w(sync, "s", "SP", S[("x1sp", NT - 1)])
                        ct.w(sync, "s", "V", V[("ln2", NT - 1)])
                        ct.w(sync, "s", "PE", P[("st2", NT - 1, KC - 1)])
                        ct.w(sync, "s", "V", V[("am", NT - 2 + t)])
                    ins = sync.dma_start(
                        out=l2buf[:, par * 4096:(par + 1) * 4096].rearrange(
                            "p (k n) -> p k n", n=T),
                        in_=dram_tile(ln2sp, t))
                    S[("l2", t)] = ct.inc(ins, "S", 16)
                    ins = sync.dma_start(
                        out=xbuf[:, par * 4096:(par + 1) * 4096].rearrange(
                            "p (k n) -> p k n", n=T),
                        in_=dram_tile(x1sp, t))
                    S[("x3", t)] = ct.inc(ins, "S", 16)
                    if t > 0:
                        _ystore(sync, t - 1, (t - 1) % 2)
                block.sync(_)

                if t == 0:
                    _q8(0, 0)

                for o in range(KC):
                    if o == 6 and t + 1 < NT:
                        _q8(t + 1, (t + 1) % 2)
                    def _(tn, t=t, o=o, par=par):
                        ct.w(tn, "t", "V", V[("q8", t)])
                        ct.w(tn, "t", "V", V[("wq8h", 1)])
                        if o >= 4:
                            ct.w(tn, "t", "A", A[("ffn", t, o - 4)])
                        elif t > 0:
                            ct.w(tn, "t", "A", A[("ffn", t - 1, 4 + o)])
                        for k in range(KC):
                            ins = tn.matmul(
                                ffl(o), wq8[:, C * k + 128 * o: C * k + 128 * (o + 1)],
                                xq8[:, par * 4096 + T * k: par * 4096 + T * (k + 1)],
                                start=(k == 0), stop=(k == KC - 1))
                        P[("ffn", t, o)] = ct.inc(ins, "PE", 1)
                    block.tensor(_)

                    def _(a, t=t, o=o):
                        ct.w(a, "a", "PE", P[("ffn", t, o)])
                        if o >= 4:
                            ct.w(a, "a", "V", V[("y", t, o - 4)])
                        elif t > 0:
                            ct.w(a, "a", "V", V[("y", t - 1, 4 + o)])
                        ins = a.activation(out=ffst[:, T * (o % 4): T * (o % 4 + 1)],
                                           in_=ffl(o), func=AF.Identity,
                                           bias=fbsb[:, o:o + 1], scale=screp[:, 1:2])
                        A[("ffn", t, o)] = ct.inc(ins, "A", 1)
                    block.scalar(_)

                    def _(v, t=t, o=o, par=par):
                        ct.w(v, "v", "A", A[("ffn", t, o)])
                        ct.w(v, "v", "S", S[("x3", t)])
                        ins = v.tensor_add(xk(par, o), ffst[:, T * (o % 4): T * (o % 4 + 1)],
                                           xk(par, o))
                        V[("y", t, o)] = ct.inc(ins, "V", 1)
                    block.vector(_)

                block.flush()

            def _(sync):
                _ystore(sync, NT - 1, (NT - 1) % 2)
                ct.w(sync, "s", "S", ct.cnt["S"])
                ct.w(sync, "s", "SP", ct.cnt["SP"])
            block.sync(_)
            block.flush()

    return nc


def kernel(L0_attn_x, L0_attn_scale1, L0_attn_scale2, L0_attn_gamma, L0_attn_qkv_w,
           L0_attn_proj_w, L0_attn_proj_b, L0_attn_conv1_w, L0_attn_conv1_b,
           L0_attn_conv2_w, L0_attn_conv2_b, L0_ffn_weight, L0_ffn_bias):
    x = np.asarray(L0_attn_x, dtype=np.float32)
    assert x.shape == (8, N, C)

    shared = {
        "qkvwT": np.ascontiguousarray(np.asarray(L0_attn_qkv_w, np.float32).T),
        "projwT": np.ascontiguousarray(np.asarray(L0_attn_proj_w, np.float32).T),
        "conv1wT": np.ascontiguousarray(np.asarray(L0_attn_conv1_w, np.float32).T),
        "conv2wT": np.ascontiguousarray(np.asarray(L0_attn_conv2_w, np.float32).T),
        "ffnwT": np.ascontiguousarray(np.asarray(L0_ffn_weight, np.float32).T),
        "scale1": np.ascontiguousarray(np.asarray(L0_attn_scale1, np.float32).reshape(KC, 128).T),
        "scale2": np.ascontiguousarray(np.asarray(L0_attn_scale2, np.float32).reshape(KC, 128).T),
        "gamma": np.ascontiguousarray(np.asarray(L0_attn_gamma, np.float32).reshape(KC, 128).T),
        "projb": np.ascontiguousarray(np.asarray(L0_attn_proj_b, np.float32)),
        "conv1b": np.ascontiguousarray(np.asarray(L0_attn_conv1_b, np.float32).reshape(HC, 128).T),
        "conv2b": np.ascontiguousarray(np.asarray(L0_attn_conv2_b, np.float32).reshape(KC, 128).T),
        "ffnb": np.ascontiguousarray(np.asarray(L0_ffn_bias, np.float32).reshape(KC, 128).T),
    }
    in_maps = []
    for i in range(8):
        m = dict(shared)
        m["xT"] = np.ascontiguousarray(x[i].T)
        in_maps.append(m)

    nc1 = build_nc(stage=3, taps=False)
    ph1 = {k: dict(v) for k, v in nc1._marks_out.items()}
    del nc1
    nc = build_nc(stage=3, taps=False, prev=ph1)
    for k, v in nc._marks_out.items():
        assert dict(v) == ph1[k], f"phase mismatch in {k}"
    res = run_bass_kernel_spmd(nc, in_maps, core_ids=list(range(8)))
    out = np.stack([np.asarray(r["yT"]).T for r in res.results], axis=0)
    return np.ascontiguousarray(out.astype(np.float32))
